# version 08
# baseline (speedup 1.0000x reference)
from contextlib import ExitStack

import numpy as np
import ml_dtypes

import concourse.bass as bass
import concourse.mybir as mybir
from concourse.bass_utils import run_bass_kernel_spmd

F32 = mybir.dt.float32
BF16 = mybir.dt.bfloat16
I32 = mybir.dt.int32
U8 = mybir.dt.uint8
AF = mybir.ActivationFunctionType
OP = mybir.AluOpType
AXX = mybir.AxisListType.X

bf16 = ml_dtypes.bfloat16

H = 1024
TOK = 4096
NBLK = 8
NPAIR = 8
EPS = 1e-5
R1024 = float(np.float32(1.0 / 1024.0))

CHI = float(np.float32(2 * np.pi))
CLO = float(np.float32(2 * np.pi - np.float64(np.float32(2 * np.pi))))
I2HI = float(np.float32(1.0 / (2 * np.pi)))
I2LO = float(np.float32(1.0 / (2 * np.pi) - np.float64(np.float32(1.0 / (2 * np.pi)))))
PHHI = float(np.float32(np.pi / 2))
PHLO = float(np.float32(np.pi / 2 - np.float64(np.float32(np.pi / 2))))
PI_SAFE = float(np.nextafter(np.float32(np.pi), np.float32(0)))

DMA_SEMS = ('dC', 'dKa', 'dKb', 'dOa', 'dOb', 'dU0', 'dU1', 'dSp', 'dX',
            'dW0a', 'dW0b', 'dW1a', 'dW1b', 'dW2a', 'dW2b', 'dW2c', 'dW2d',
            'dOut')
ENG_SEMS = ('sT', 'sV', 'sA')


def _split_const(M64):
    m32 = M64.astype(np.float32)
    corr = (M64 - m32.astype(np.float64)).astype(bf16)
    return m32, corr


def _host_consts():
    t = np.arange(512, dtype=np.float64)
    f = np.arange(512, dtype=np.float64)
    ang = 2 * np.pi * np.outer(t, f) / 1024.0
    C32, Ccorr = _split_const(np.cos(ang))
    S32, Scorr = _split_const(-np.sin(ang))
    t2 = np.arange(512, dtype=np.float64)
    wf = np.where(f == 0, 1.0, 2.0)
    angi = 2 * np.pi * np.outer(f, 512 + t2) / 1024.0
    Ci32, Cicorr = _split_const(wf[:, None] * np.cos(angi) / 1024.0)
    Si32, Sicorr = _split_const(-wf[:, None] * np.sin(angi) / 1024.0)
    altcol = ((-1.0) ** np.arange(128, dtype=np.float64)).astype(np.float32)[:, None]
    S32[:, 0] = ((-1.0) ** t).astype(np.float32)
    Scorr[:, 0] = 0
    Si32[0, :] = (((-1.0) ** t2) / 1024.0).astype(np.float32)
    Sicorr[0, :] = 0
    iota32 = np.broadcast_to(np.arange(32, dtype=np.float32), (128, 32)).copy()
    iota16 = np.broadcast_to(np.arange(16, dtype=np.float32) * 32.0, (128, 16)).copy()
    ident = np.eye(128, dtype=np.float32)
    par = (np.arange(128) & 1)
    Mplus = np.zeros((128, 254), np.float32)
    Mplus[:, 126] = (par == 0)
    Mplus[:, 127] = (par == 1)
    Mminus = -Mplus
    cpool = np.broadcast_to(np.array([-1.0, 1.0, 2.0, EPS], np.float32),
                            (128, 4)).copy()
    return dict(C32=C32, Ccorr=Ccorr, S32=S32, Scorr=Scorr,
                Ci32=Ci32, Cicorr=Cicorr, Si32=Si32, Sicorr=Sicorr,
                altcol=altcol, iota32=iota32, iota16=iota16,
                ident=ident, Mplus=Mplus, Mminus=Mminus, cpool=cpool)


class Sched:
    ENGS = ('t', 'v', 's', 'y', 'g')

    def __init__(self):
        self.ops = {e: [] for e in self.ENGS}
        self.cnt = {s: 0 for s in DMA_SEMS + ENG_SEMS}
        self.lastwait = {}

    def wait(self, eng, sem, val):
        if val is None or val <= 0:
            return
        key = (eng, sem)
        if self.lastwait.get(key, -1) >= val:
            return
        self.lastwait[key] = val
        self.ops[eng].append(('w', sem, val))

    def waitc(self, eng, ck):
        if ck:
            self.wait(eng, ck[0], ck[1])

    def emit(self, eng, fn, inc=None):
        if inc is not None:
            sname, k = inc
            self.cnt[sname] += k
            post = self.cnt[sname]
            self.ops[eng].append(('i', fn, sname, k))
            return post
        self.ops[eng].append(('e', fn))
        return None

    def run(self, eng, proxy, sems):
        for op in self.ops[eng]:
            if op[0] == 'w':
                proxy.wait_ge(sems[op[1]], op[2])
            elif op[0] == 'e':
                op[1](proxy)
            else:
                op[1](proxy).then_inc(sems[op[2]], op[3])


class Arena:
    def __init__(self, handle, nf32):
        self.h = handle
        self.n = nf32
        self.off = 0

    @staticmethod
    def _shape(ap, shape):
        if len(shape) == 3:
            return ap.rearrange("p (a b) -> p a b", b=shape[2])
        if len(shape) == 4:
            return ap.rearrange("p (a b c) -> p a b c", b=shape[2], c=shape[3])
        return ap

    def f32(self, shape):
        n = int(np.prod(shape[1:]))
        ap = self.h[0:shape[0], self.off:self.off + n]
        self.off += n
        assert self.off <= self.n, f"arena overflow {self.off} > {self.n}"
        return self._shape(ap, shape)

    def bf(self, shape):
        n = int(np.prod(shape[1:]))
        slots = (n + 1) // 2
        ap = self.h[0:shape[0], self.off:self.off + slots].bitcast(BF16)
        self.off += slots
        assert self.off <= self.n, f"arena overflow {self.off} > {self.n}"
        return self._shape(ap, shape)


ARENA_N = 44500


def build_nc():
    nc = bass.Bass("TRN2", target_bir_lowering=False,
                   detect_race_conditions=False)

    d_u = nc.dram_tensor("u_shard", [TOK + 512, H], F32, kind="ExternalInput").ap()
    d_logdt = nc.dram_tensor("logdt_arr", [128, 512], F32, kind="ExternalInput").ap()
    d_lamre = nc.dram_tensor("lamre_col", [128, 1], F32, kind="ExternalInput").ap()
    d_lamim = nc.dram_tensor("lamim_col", [128, 1], F32, kind="ExternalInput").ap()
    d_wre = nc.dram_tensor("wre_arr", [128, 512], F32, kind="ExternalInput").ap()
    d_wim = nc.dram_tensor("wim_arr", [128, 512], F32, kind="ExternalInput").ap()
    d_k0t = nc.dram_tensor("k0t", [1024, 4096], U8, kind="ExternalInput").ap()
    d_k0b = nc.dram_tensor("k0b", [1024, 4096], U8, kind="ExternalInput").ap()
    d_k1 = nc.dram_tensor("k1", [4096, 4096], U8, kind="ExternalInput").ap()
    d_k2 = nc.dram_tensor("k2", [4096, 1024], U8, kind="ExternalInput").ap()
    d_svec = nc.dram_tensor("svec", [128, 3], F32, kind="ExternalInput").ap()
    d_Drep = nc.dram_tensor("D_rep", [128, H], F32, kind="ExternalInput").ap()
    CONSTS = _host_consts()
    dc = {}
    for name, arr in CONSTS.items():
        dt = BF16 if arr.dtype == bf16 else F32
        dc[name] = nc.dram_tensor("c_" + name, list(arr.shape), dt,
                                  kind="ExternalInput").ap()
    d_out = nc.dram_tensor("out", [TOK, H], F32, kind="ExternalOutput").ap()
    d_dbg = None
    d_x1 = nc.dram_tensor("x1scr", [TOK, H], F32, kind="Internal").ap()
    d_w0 = nc.dram_tensor("w0scr", [32, 128, 8, 128], BF16, kind="Internal").ap()
    d_w1 = nc.dram_tensor("w1scr", [32, 128, 32, 128], BF16, kind="Internal").ap()
    d_w2 = nc.dram_tensor("w2scr", [32, 128, 1024], BF16, kind="Internal").ap()

    import os
    KCUT = os.environ.get('KCUT', '')
    KDBG = os.environ.get('KDBG') == '1'

    sch = Sched()
    import os as _os
    if _os.environ.get('KDBG') == '1':
        d_dbg = nc.dram_tensor("dbg", [16, 128, 1024], F32,
                               kind="ExternalOutput").ap()

    with ExitStack() as _st:
        def sb(name, shape, dtype):
            return _st.enter_context(nc.sbuf_tensor(name, shape, dtype))

        sb_cpool = sb("cn_cpool", [128, 4], F32)
        nc.const_aps.register(F32, -1.0, sb_cpool[:, 0:1])
        nc.const_aps.register(F32, 1.0, sb_cpool[:, 1:2])
        nc.const_aps.register(F32, 2.0, sb_cpool[:, 2:3])
        nc.const_aps.register(F32, EPS, sb_cpool[:, 3:4])
        sb_altcol = sb("cn_altcol", [128, 1], F32)
        sb_iota32 = sb("cn_iota32", [128, 32], F32)
        sb_iota16 = sb("cn_iota16", [128, 16], F32)
        sb_ident = sb("cn_ident", [128, 128], F32)
        sb_mplus = sb("cn_mplus", [128, 254], F32)
        sb_mminus = sb("cn_mminus", [128, 254], F32)
        sb_logdt = sb("in_logdt", [128, 512], F32)
        sb_lamre = sb("in_lamre", [128, 1], F32)
        sb_lamim = sb("in_lamim", [128, 1], F32)
        sb_wre = sb("in_wre", [128, 512], F32)
        sb_wim = sb("in_wim", [128, 512], F32)
        sb_svec = sb("in_svec", [128, 3], F32)
        sb_drep = sb("in_drep", [128, H], F32)
        sb_ab = sb("t_ab", [128, 1024], F32)
        sb_alpha = sb("t_alpha", [128, 1], F32)
        sb_dp1 = sb("t_dp1", [128, H], F32)
        sb_dump = sb("t_dump", [128, 1024], F32)
        sb_stats = sb("t_stats", [128, 16], F32)
        sb_mi = sb("t_mi", [128, 1024], I32)
        sb_arena = sb("arena", [128, ARENA_N], F32)
        ppA = _st.enter_context(nc.psum_tensor("ppA", [128, 512], F32))
        ppB = _st.enter_context(nc.psum_tensor("ppB", [128, 512], F32))
        ppC = _st.enter_context(nc.psum_tensor("ppC", [128, 512], F32))
        ppD = _st.enter_context(nc.psum_tensor("ppD", [128, 512], F32))
        pM = [_st.enter_context(nc.psum_tensor(f"pM{i}", [128, 512], F32))
              for i in range(4)]
        pM0 = pM[0]
        sems = {}
        for s in DMA_SEMS + ENG_SEMS:
            sems[s] = _st.enter_context(nc.semaphore(s))
        blk = _st.enter_context(nc.Block())

        ar = Arena(sb_arena, ARENA_N)

        aC32 = ar.f32([128, 4, 512])
        aS32 = ar.f32([128, 4, 512])
        aCc = ar.bf([128, 4, 512])
        aSc = ar.bf([128, 4, 512])
        aCi32 = ar.f32([128, 4, 512])
        aSi32 = ar.f32([128, 4, 512])
        aCic = ar.bf([128, 4, 512])
        aSic = ar.bf([128, 4, 512])
        aKfRe = ar.f32([128, 4, 1024])
        aKfIm = ar.f32([128, 4, 1024])
        aKfX = ar.f32([128, 2, 1024])
        p1_base = ar.off

        aU = [ar.f32([128, 4, 1024]) for _ in range(2)]
        aUbf = ar.bf([128, 8, 512])
        aUf = ar.f32([128, 8, 512])
        aYbf = ar.bf([128, 8, 512])
        aZX = ar.f32([128, 4, 512])
        aZasm = ar.f32([128, 1024])
        aScr = ar.f32([128, 4, 512])
        p1_end = ar.off

        ar.off = p1_base
        cvA = ar.f32([128, 1024]).bitcast(U8)
        cvB = ar.f32([128, 1024]).bitcast(U8)
        cvAb = ar.f32([128, 2048]).bitcast(BF16)
        cvBb = ar.f32([128, 2048]).bitcast(BF16)

        ar.off = p1_base
        g = {}
        for nm, w in (('ApRe', 1024), ('ApIm', 1024), ('EA', 1024), ('snA', 1024),
                      ('csA', 1024), ('tA', 1024), ('tB', 1024), ('argI', 1024),
                      ('BRe', 512), ('BIm', 512), ('EB', 512), ('snB', 512),
                      ('csB', 512), ('SS1', 512), ('SS2', 512), ('khl', 512)):
            g[nm] = ar.f32([128, w])
        aKlh = ar.f32([128, 4, 1024])
        aKbf = ar.bf([128, 4, 1024])
        assert ar.off <= p1_end, (ar.off, p1_end)
        ar.off = p1_end

        ar.off = 0
        aH1hi = ar.bf([128, 32, 512])
        aH1lo = ar.bf([128, 32, 512])
        aH2Bhi = ar.bf([128, 32, 256])
        aH2Blo = ar.bf([128, 32, 256])
        aH2f = ar.f32([128, 512])
        aH2Ahi = ar.bf([128, 2, 256])
        aH2Alo = ar.bf([128, 2, 256])
        aX1tm = ar.f32([128, 4, 1024])
        aX1hi = ar.bf([128, 8, 512])
        aX1lo = ar.bf([128, 8, 512])
        aW0 = ar.bf([128, 2, 8, 128])
        w1_off = ar.off
        aW1 = ar.bf([128, 2, 32, 128])
        aW1_f32 = sb_arena[0:128, w1_off:w1_off + 4096]
        aW2 = ar.bf([128, 4, 1024])
        aG1f = ar.f32([128, 512])
        aZ2 = ar.f32([128, 1024])
        assert ar.off <= ARENA_N, ar.off

        E = sch.emit
        W = sch.wait
        WC = sch.waitc

        def dma(q, sem, dst, src):
            post = E(q, lambda e, d=dst, r=src: e.dma_start(out=d, in_=r),
                     inc=(sem, 16))
            return (sem, post)

        dma('y', 'dC', aC32, dc['C32'].rearrange("(a p) f -> p a f", p=128))
        dma('y', 'dC', aS32, dc['S32'].rearrange("(a p) f -> p a f", p=128))
        dma('y', 'dC', aCc, dc['Ccorr'].rearrange("(a p) f -> p a f", p=128))
        dma('y', 'dC', aSc, dc['Scorr'].rearrange("(a p) f -> p a f", p=128))
        dma('y', 'dC', aCi32, dc['Ci32'].rearrange("(a p) f -> p a f", p=128))
        dma('y', 'dC', aSi32, dc['Si32'].rearrange("(a p) f -> p a f", p=128))
        dma('y', 'dC', aCic, dc['Cicorr'].rearrange("(a p) f -> p a f", p=128))
        dma('y', 'dC', aSic, dc['Sicorr'].rearrange("(a p) f -> p a f", p=128))
        for nm, dst in (('cpool', sb_cpool), ('altcol', sb_altcol),
                        ('iota32', sb_iota32), ('iota16', sb_iota16),
                        ('ident', sb_ident), ('Mplus', sb_mplus),
                        ('Mminus', sb_mminus)):
            dma('y', 'dC', dst[:, :], dc[nm])
        for src, dst in ((d_logdt, sb_logdt), (d_lamre, sb_lamre), (d_lamim, sb_lamim),
                         (d_wre, sb_wre), (d_wim, sb_wim), (d_svec, sb_svec),
                         (d_Drep, sb_drep)):
            dma('y', 'dC', dst[:, :], src)
        ck_consts = ('dC', sch.cnt['dC'])

        a_col = sb_ab[:, 0:512]
        b_col = sb_ab[:, 512:1024]
        WC('s', ck_consts)
        E('s', lambda e: e.activation(sb_logdt[:, :], sb_logdt[:, :], AF.Exp))
        E('s', lambda e: e.activation(sb_lamre[:, :], sb_lamre[:, :], AF.Exp))
        ck = E('s', lambda e: e.activation(sb_dp1[:, :], sb_drep[:, :], AF.Identity,
                                           bias=1.0), inc=('sA', 1))
        W('v', 'sA', ck)
        E('v', lambda e: e.tensor_mul(sb_stats[:, 0:1], sb_svec[:, 0:1],
                                      sb_svec[:, 1:2]))
        E('v', lambda e: e.tensor_scalar(a_col, sb_logdt[:, :], sb_lamre[:, 0:1],
                                         -1.0, OP.mult, OP.mult))
        E('v', lambda e: e.tensor_scalar(sb_stats[:, 1:2], sb_svec[:, 2:3], 8.0,
                                         None, OP.mult))
        E('v', lambda e: e.tensor_scalar(b_col, sb_logdt[:, :], sb_lamim[:, 0:1],
                                         None, OP.mult))
        E('v', lambda e: e.tensor_mul(sb_alpha[:, :], sb_stats[:, 0:1],
                                      sb_stats[:, 1:2]), inc=('sV', 1))

        if KCUT == 'p0a':
            W('g', 'sV', sch.cnt['sV'])
            ck_dummy = dma('g', 'dOut', d_out[0:128, :], sb_dump[:, :])
            WC('g', ck_dummy)
            return _finish(sch, blk, sems, nc)

        ck_w1out = {}
        ck_w1cvt = {}
        for ic in range(32):
            srcu, ksem = (cvA, 'dKa') if ic % 2 == 0 else (cvB, 'dKb')
            dstb, osem = (cvAb, 'dOa') if ic % 2 == 0 else (cvBb, 'dOb')
            W('y', 'sA', ck_w1cvt.get(ic - 2, 0))
            ckl = dma('y', ksem, srcu[:, 0:4096], d_k1[ic * 128:(ic + 1) * 128, :])
            WC('s', ckl)
            WC('s', ck_w1out.get(ic - 2))
            cka = E('s', lambda e, d=dstb, r=srcu: e.activation(
                d[:, 0:4096], r[:, 0:4096], AF.Identity, bias=-1.0, scale=2.0),
                inc=('sA', 1))
            ck_w1cvt[ic] = cka
            W('g', 'sA', cka)
            for ocg in range(4):
                ck_w1out[ic] = dma(
                    'g', osem,
                    d_w1[ocg * 8:(ocg + 1) * 8, :, ic, :].rearrange(
                        "o q c -> q o c"),
                    dstb[:, ocg * 1024:(ocg + 1) * 1024].rearrange(
                        "p (o c) -> p o c", c=128))
        ck_w0out = {}
        ck_w0cvt = 0
        for ic in range(8):
            W('y', 'sV', ck_w0cvt)
            if ic == 0:
                W('y', 'sA', ck_w1cvt[31])
            ckl0 = dma('y', 'dKa', cvA[:, 0:4096], d_k0t[ic * 128:(ic + 1) * 128, :])
            ckl1 = dma('y', 'dKb', cvB[:, 0:4096], d_k0b[ic * 128:(ic + 1) * 128, :])
            WC('v', ckl0)
            WC('v', ckl1)
            if ic == 0:
                WC('v', ck_w1out[30])
            WC('v', ck_w0out.get(ic - 1))
            ckv = E('v', lambda e: e.tensor_sub(cvAb[:, 0:4096], cvA[:, 0:4096],
                                                cvB[:, 0:4096]), inc=('sV', 1))
            ck_w0cvt = ckv
            W('g', 'sV', ckv)
            for ocg in range(4):
                ck_w0out[ic] = dma(
                    'g', 'dOa',
                    d_w0[ocg * 8:(ocg + 1) * 8, :, ic, :].rearrange(
                        "o q c -> q o c"),
                    cvAb[:, ocg * 1024:(ocg + 1) * 1024].rearrange(
                        "p (o c) -> p o c", c=128))
        ck_w2out = {}
        ck_w2cvt = 0
        for ic in range(32):
            W('y', 'sA', ck_w2cvt)
            if ic == 0:
                W('y', 'sV', ck_w0cvt)
            ckl = dma('y', 'dKa', cvA[:, 0:1024], d_k2[ic * 128:(ic + 1) * 128, :])
            WC('s', ckl)
            if ic == 0:
                WC('s', ck_w1out[31])
            WC('s', ck_w2out.get(ic - 1))
            cka = E('s', lambda e: e.activation(cvBb[:, 0:1024], cvA[:, 0:1024],
                                                AF.Identity, bias=-1.0, scale=2.0),
                    inc=('sA', 1))
            ck_w2cvt = cka
            W('g', 'sA', cka)
            ck_w2out[ic] = dma('g', 'dOb', d_w2[ic, :, :], cvBb[:, 0:1024])
        ck_oa_total = ('dOa', sch.cnt['dOa'])
        ck_ob_total = ('dOb', sch.cnt['dOb'])

        if KCUT == 'p0b':
            WC('g', ck_oa_total)
            WC('g', ck_ob_total)
            ck_dummy = dma('g', 'dOut', d_out[0:128, :], sb_dump[:, :])
            WC('g', ck_dummy)
            return _finish(sch, blk, sems, nc)

        def r3(ap, b):
            return ap.rearrange("p (a b) -> p a b", b=b)

        def sincos(argI, sn_out, cs_out, width):
            tA = g['tA'][:, 0:width]
            tB = g['tB'][:, 0:width]
            mi = sb_mi[:, 0:width]
            sg = sb_dump[:, 0:width]
            cks = None
            for mode in (0, 1):
                if mode == 0:
                    E('v', lambda e: e.tensor_scalar(tA, argI, I2LO, None, OP.mult))
                else:
                    W('v', 'sA', cks)
                    E('v', lambda e: e.tensor_scalar(tA, argI, I2LO, 0.25,
                                                     OP.mult, OP.add))
                E('v', lambda e: e.scalar_tensor_tensor(
                    tA, argI, I2HI, tA, OP.mult, OP.add))
                E('v', lambda e: e.tensor_copy(mi, tA))
                E('v', lambda e: e.tensor_copy(tB, mi))
                E('v', lambda e: e.tensor_sub(tA, tA, tB))
                E('v', lambda e: e.tensor_single_scalar(sg, tA, 0.5, OP.is_gt))
                E('v', lambda e: e.tensor_single_scalar(tA, tA, -0.5, OP.is_lt))
                E('v', lambda e: e.tensor_add(tB, tB, sg))
                E('v', lambda e: e.tensor_sub(tB, tB, tA))
                E('v', lambda e: e.scalar_tensor_tensor(tA, tB, -CHI, argI,
                                                        OP.mult, OP.add))
                E('v', lambda e: e.scalar_tensor_tensor(tA, tB, -CLO, tA,
                                                        OP.mult, OP.add))
                if mode == 1:
                    E('v', lambda e: e.tensor_scalar(tA, tA, PHHI, PHLO,
                                                     OP.add, OP.add))
                ckv = E('v', lambda e: e.tensor_scalar(tA, tA, PI_SAFE, -PI_SAFE,
                                                       OP.min, OP.max), inc=('sV', 1))
                W('s', 'sV', ckv)
                dst = sn_out if mode == 0 else cs_out
                cks = E('s', lambda e, d=dst: e.activation(d, tA, AF.Sin),
                        inc=('sA', 1))
            return cks

        WC('v', ck_oa_total)
        WC('v', ck_ob_total)
        WC('s', ck_oa_total)
        WC('s', ck_ob_total)

        KHG = int(os.environ.get('KHG', '16'))
        ck_ss_mm = 0
        ck_klh_v = 0
        for hg in range(KHG):
            j0 = hg * 32
            a_sl = a_col[:, j0:j0 + 32].unsqueeze(2).broadcast_to([128, 32, 32])
            b_sl = b_col[:, j0:j0 + 32].unsqueeze(2).broadcast_to([128, 32, 32])
            wre_sl = sb_wre[:, j0:j0 + 32].unsqueeze(2).broadcast_to([128, 32, 32])
            wim_sl = sb_wim[:, j0:j0 + 32].unsqueeze(2).broadcast_to([128, 32, 32])
            i32b = sb_iota32[:, :].unsqueeze(1).broadcast_to([128, 32, 32])
            W('v', 'sA', sch.cnt['sA'])
            W('v', 'sT', ck_ss_mm)
            ckv = E('v', lambda e, a=a_sl, i=i32b: e.tensor_tensor(
                r3(g['ApRe'], 32), a, i, OP.mult), inc=('sV', 1))
            W('s', 'sV', ckv)
            E('s', lambda e: e.activation(g['EA'], g['ApRe'], AF.Exp))
            E('v', lambda e, b=b_sl, i=i32b: e.tensor_tensor(
                r3(g['argI'], 32), b, i, OP.mult))
            cks = sincos(g['argI'], g['snA'], g['csA'], 1024)
            W('v', 'sA', cks)
            E('v', lambda e, w=wre_sl: e.tensor_tensor(r3(g['tA'], 32),
                                                       r3(g['csA'], 32), w, OP.mult))
            E('v', lambda e, w=wim_sl: e.tensor_tensor(r3(g['tB'], 32),
                                                       r3(g['snA'], 32), w, OP.mult))
            E('v', lambda e: e.tensor_sub(g['tA'], g['tA'], g['tB']))
            E('v', lambda e: e.tensor_mul(g['ApRe'], g['tA'], g['EA']))
            E('v', lambda e, w=wre_sl: e.tensor_tensor(r3(g['tA'], 32),
                                                       r3(g['snA'], 32), w, OP.mult))
            E('v', lambda e, w=wim_sl: e.tensor_tensor(r3(g['tB'], 32),
                                                       r3(g['csA'], 32), w, OP.mult))
            E('v', lambda e: e.tensor_add(g['tA'], g['tA'], g['tB']))
            E('v', lambda e: e.tensor_mul(g['ApIm'], g['tA'], g['EA']))
            a_sb = a_col[:, j0:j0 + 32].unsqueeze(2).broadcast_to([128, 32, 16])
            b_sb = b_col[:, j0:j0 + 32].unsqueeze(2).broadcast_to([128, 32, 16])
            i16b = sb_iota16[:, :].unsqueeze(1).broadcast_to([128, 32, 16])
            ckv = E('v', lambda e, a=a_sb, i=i16b: e.tensor_tensor(
                r3(g['BRe'], 16), a, i, OP.mult), inc=('sV', 1))
            W('s', 'sV', ckv)
            E('s', lambda e: e.activation(g['EB'], g['BRe'], AF.Exp))
            E('v', lambda e, b=b_sb, i=i16b: e.tensor_tensor(
                r3(g['argI'][:, 0:512], 16), b, i, OP.mult))
            cks = sincos(g['argI'][:, 0:512], g['snB'], g['csB'], 512)
            W('v', 'sA', cks)
            E('v', lambda e: e.tensor_mul(g['BRe'], g['EB'], g['csB']))
            ckB = E('v', lambda e: e.tensor_mul(g['BIm'], g['EB'], g['snB']),
                    inc=('sV', 1))
            kp = ppA if (hg // 2) % 2 == 0 else ppB
            ApRe_j = r3(g['ApRe'], 32)
            ApIm_j = r3(g['ApIm'], 32)
            BRe_j = r3(g['BRe'], 16)
            BIm_j = r3(g['BIm'], 16)
            for jl in range(32):
                ss1 = r3(g['SS1'], 32)
                ss2 = r3(g['SS2'], 32)
                apre = ApRe_j[:, jl, :].unsqueeze(1).broadcast_to([128, 16, 32])
                apim = ApIm_j[:, jl, :].unsqueeze(1).broadcast_to([128, 16, 32])
                bre = BRe_j[:, jl, :].unsqueeze(2).broadcast_to([128, 16, 32])
                bim = BIm_j[:, jl, :].unsqueeze(2).broadcast_to([128, 16, 32])
                W('v', 'sV', ckB)
                W('v', 'sT', ck_ss_mm)
                E('v', lambda e, a=apre, b=bre, o=ss1: e.tensor_tensor(o, a, b, OP.mult))
                ckss = E('v', lambda e, a=apim, b=bim, o=ss2: e.tensor_tensor(
                    o, a, b, OP.mult), inc=('sV', 1))
                W('t', 'sV', ckss)
                jfull = (hg % 2) * 32 + jl
                c0 = 126 - 2 * jfull
                E('t', lambda e, p=kp, c=c0: e.matmul(
                    p[:, :], sb_mplus[:, c:c + 128], g['SS1'],
                    start=(c == 126), stop=False))
                ck_ss_mm = E('t', lambda e, p=kp, c=c0, lst=(jfull == 63): e.matmul(
                    p[:, :], sb_mminus[:, c:c + 128], g['SS2'],
                    start=False, stop=lst), inc=('sT', 1))
            if hg % 2 == 1:
                grp = hg // 2
                W('v', 'sT', ck_ss_mm)
                ckv = E('v', lambda e, p=kp: e.tensor_copy(g['khl'], p[:, :]),
                        inc=('sV', 1))
                khl2 = r3(g['khl'], 128)
                ck_tev = {}
                for lc in range(4):
                    pt = ppC if lc % 2 == 0 else ppD
                    W('t', 'sV', ckv)
                    W('t', 'sV', ck_tev.get(lc - 2, 0))
                    ckt = E('t', lambda e, p=pt, l=lc: e.transpose(
                        p[:, 0:128], khl2[:, l, :], sb_ident[:, :]), inc=('sT', 1))
                    W('v', 'sT', ckt)
                    E('v', lambda e, p=pt, l=lc, gg=grp: e.tensor_copy(
                        aKlh[:, l, gg * 128:(gg + 1) * 128], p[:, 0:128]))
                    ck_klh_v = E('v', lambda e, p=pt, l=lc, gg=grp: e.tensor_copy(
                        aKbf[:, l, gg * 128:(gg + 1) * 128], p[:, 0:128]),
                        inc=('sV', 1))
                    ck_tev[lc] = ck_klh_v
        ck_k_done = ck_klh_v

        if KCUT == 'p0c':
            W('g', 'sV', sch.cnt['sV'])
            W('g', 'sT', sch.cnt['sT'])
            ck_dummy = dma('g', 'dOut', d_out[0:128, :], sb_dump[:, :])
            WC('g', ck_dummy)
            return _finish(sch, blk, sems, nc)

        ck_kf_ev = {}
        for comp in range(2):
            dst = aKfRe if comp == 0 else aKfIm
            c32v, ccv = (aC32, aCc) if comp == 0 else (aS32, aSc)
            for fc in range(4):
                for hh in range(2):
                    idx = comp * 8 + fc * 2 + hh
                    pp = ppA if idx % 2 == 0 else ppB
                    W('t', 'sV', ck_k_done)
                    W('t', 'sV', ck_kf_ev.get(idx - 2, 0))
                    for lc in range(4):
                        E('t', lambda e, p=pp, l=lc, f=fc, h=hh, cv=c32v: e.matmul(
                            p[:, :], cv[:, l, f * 128:(f + 1) * 128],
                            aKlh[:, l, h * 512:(h + 1) * 512],
                            start=(l == 0), stop=False))
                    ckt = None
                    for lc in range(4):
                        ckt = E('t', lambda e, p=pp, l=lc, f=fc, h=hh, cv=ccv:
                                e.matmul(
                                    p[:, :], cv[:, l, f * 128:(f + 1) * 128],
                                    aKbf[:, l, h * 512:(h + 1) * 512],
                                    start=False, stop=(l == 3)),
                                inc=('sT', 1) if lc == 3 else None)
                    W('v', 'sT', ckt)
                    ck_kf_ev[idx] = E(
                        'v', lambda e, p=pp, f=fc, h=hh, d=dst: e.tensor_copy(
                            d[:, f, h * 512:(h + 1) * 512], p[:, :]), inc=('sV', 1))
        ck_last_kf = max(ck_kf_ev.values())
        W('v', 'sV', ck_last_kf)
        E('v', lambda e: e.tensor_copy(aKfX[:, 0, :], aKfIm[:, 0, :]))
        E('v', lambda e: e.memset(aKfX[0:1, 0, :], 0.0))
        E('v', lambda e: e.tensor_copy(aKfX[:, 1, :], aKfRe[:, 0, :]))
        E('v', lambda e: e.tensor_copy(aKfX[0:1, 1, :], aKfIm[0:1, 0, :]),
          inc=('sV', 1))
        ck_p0_v = sch.cnt['sV']
        ck_p0_t = sch.cnt['sT']
        if KDBG:
            W('g', 'sV', ck_p0_v)
            W('g', 'sT', ck_p0_t)
            WC('g', ck_oa_total)
            WC('g', ck_ob_total)
            ckw1t = dma('g', 'dKb', cvAb.rearrange("p (a b) -> p a b", b=128),
                        d_w1[1, :, :, :])
            WC('g', ckw1t)
            dma('g', 'dOut', d_dbg[3, :, :],
                cvAb.bitcast(F32)[:, 0:1024])
            dma('g', 'dOut', d_dbg[4, :, :],
                cvAb.bitcast(F32)[:, 1024:2048])
            dma('g', 'dOut', d_dbg[0, :, :], aKfRe[:, 0, :])
            dma('g', 'dOut', d_dbg[1, :, :], aKfIm[:, 0, :])
            dma('g', 'dOut', d_dbg[2, :, :], aKfX[:, 0, :])
            dma('g', 'dOut', d_dbg[5, :, 0:1024], sb_ab[:, :])
            ckdx = dma('g', 'dOut', d_dbg[6, :, 0:512], sb_wre[:, :])
            WC('t', ckdx)

        if KCUT == 'p0':
            W('g', 'sV', ck_p0_v)
            W('g', 'sT', ck_p0_t)
            ck_dummy = dma('g', 'dOut', d_out[0:128, :], sb_dump[:, :])
            WC('g', ck_dummy)
            return _finish(sch, blk, sems, nc)

        ck_uh = {}

        def load_uhalf(h):
            if h > NBLK:
                return
            if h < 2:
                W('g', 'sV', ck_p0_v)
                W('g', 'sT', ck_p0_t)
            else:
                W('g', 'sT', sch.cnt['sT'])
                W('g', 'sA', sch.cnt['sA'])
            ck_uh[h] = dma('g', 'dU0' if h % 2 == 0 else 'dU1', aU[h % 2],
                           d_u[h * 512:(h + 1) * 512, :].rearrange(
                               "(a p) c -> p a c", p=128))

        load_uhalf(0)
        load_uhalf(1)

        def ln_emit(tile, outtile):
            st = sb_stats
            ckv = E('v', lambda e, ti=tile: e.tensor_reduce(st[:, 0:1], ti, AXX,
                                                            OP.add), inc=('sV', 1))
            W('s', 'sV', ckv)
            cks = E('s', lambda e, ti=tile: e.activation(sb_dump[:, :], ti, AF.Square,
                                                         accum_out=st[:, 1:2]),
                    inc=('sA', 1))
            W('v', 'sA', cks)
            E('v', lambda e: e.tensor_scalar(st[:, 2:3], st[:, 0:1], R1024,
                                             None, OP.mult))
            E('v', lambda e: e.tensor_scalar(st[:, 3:4], st[:, 1:2], R1024,
                                             None, OP.mult))
            ckv = E('v', lambda e: e.tensor_scalar(st[:, 8:9], st[:, 0:1], -R1024,
                                                   None, OP.mult), inc=('sV', 1))
            W('s', 'sV', ckv)
            cks = E('s', lambda e: e.activation(st[:, 4:5], st[:, 2:3], AF.Square),
                    inc=('sA', 1))
            W('v', 'sA', cks)
            ckv = E('v', lambda e: e.tensor_sub(st[:, 5:6], st[:, 3:4], st[:, 4:5]),
                    inc=('sV', 1))
            W('s', 'sV', ckv)
            cks = E('s', lambda e: e.activation(st[:, 6:7], st[:, 5:6], AF.Sqrt,
                                                bias=EPS), inc=('sA', 1))
            W('v', 'sA', cks)
            ckv = E('v', lambda e: e.reciprocal(st[:, 7:8], st[:, 6:7]),
                    inc=('sV', 1))
            W('s', 'sV', ckv)
            cks = E('s', lambda e: e.activation(st[:, 9:10], st[:, 8:9], AF.Copy,
                                                scale=st[:, 7:8]), inc=('sA', 1))
            W('v', 'sA', cks)
            return E('v', lambda e, ti=tile, o=outtile: e.scalar_tensor_tensor(
                o, ti, st[:, 7:8], st[:, 9:10].broadcast_to([128, 1024]),
                OP.mult, OP.add), inc=('sV', 1))

        ck_fwd_last = 0
        ck_inv_ev = 0
        last_spill = None
        for blkid in range(NBLK):
            hA, hB = blkid, blkid + 1
            uA, uB = aU[hA % 2], aU[hB % 2]

            for chh in range(2):
                def usl(tc, uA=uA, uB=uB, c=chh):
                    return (uA if tc < 4 else uB)[:, tc % 4, c * 512:(c + 1) * 512]

                WC('s', ck_uh[hA])
                WC('s', ck_uh[hB])
                W('s', 'sT', ck_fwd_last)
                ck_ubf = None
                for tc in range(8):
                    ck_ubf = E('s', lambda e, t=tc, us=usl: e.activation(
                        aUbf[:, t, :], us(t), AF.Copy),
                        inc=('sA', 1) if tc == 7 else None)

                WC('t', ck_uh[hA])
                WC('t', ck_uh[hB])
                W('t', 'sA', ck_ubf)
                W('t', 'sV', ck_inv_ev)
                ck_uf_ev = {}
                for comp in range(2):
                    c32v, ccv = (aC32, aCc) if comp == 0 else (aS32, aSc)
                    for fc in range(4):
                        gi = comp * 4 + fc
                        g1 = ppA if gi % 2 == 0 else ppC
                        g2 = ppB if gi % 2 == 0 else ppD
                        W('t', 'sV', ck_uf_ev.get(gi - 2, 0))
                        ckt = None
                        for tc in range(4):
                            for half in range(2):
                                pp = g1 if half == 0 else g2
                                E('t', lambda e, p=pp, t=tc, hf=half, f=fc, us=usl,
                                  cv=c32v: e.matmul(
                                    p[:, :], cv[:, t, f * 128:(f + 1) * 128],
                                    us(t + 4 * hf), start=(t == 0), stop=False))
                        for tc in range(4):
                            for half in range(2):
                                pp = g1 if half == 0 else g2
                                ckt = E('t', lambda e, p=pp, t=tc, hf=half, f=fc,
                                        cv=ccv: e.matmul(
                                            p[:, :], cv[:, t, f * 128:(f + 1) * 128],
                                            aUbf[:, t + 4 * hf, :],
                                            start=False, stop=(t == 3)),
                                        inc=('sT', 1) if (half == 1 and tc == 3)
                                        else None)
                        W('v', 'sT', ckt)
                        E('v', lambda e, p=g2: e.tensor_copy(aScr[:, 0, :], p[:, :]))
                        ck_uf_ev[gi] = E(
                            'v', lambda e, p=g1, sl=gi: e.scalar_tensor_tensor(
                                aUf[:, sl, :], aScr[:, 0, :], sb_altcol[:, 0:1],
                                p[:, :], OP.mult, OP.add), inc=('sV', 1))
                ck_fwd_last = sch.cnt['sT']

                ck_yf = None
                for fc in range(4):
                    W('v', 'sV', ck_uf_ev[fc])
                    W('v', 'sV', ck_uf_ev[4 + fc])
                    sl = slice(chh * 512, (chh + 1) * 512)
                    kre = aKfRe[:, fc, sl]
                    kim = aKfX[:, 0, sl] if fc == 0 else aKfIm[:, fc, sl]
                    kreI = aKfX[:, 1, sl] if fc == 0 else aKfRe[:, fc, sl]
                    ure = aUf[:, fc, :]
                    uim = aUf[:, 4 + fc, :]
                    E('v', lambda e, k=kreI, u=uim: e.tensor_tensor(
                        aScr[:, 0, :], k, u, OP.mult))
                    E('v', lambda e, k=kim, u=ure: e.tensor_tensor(
                        aScr[:, 1, :], k, u, OP.mult))
                    E('v', lambda e, k=kre, u=ure: e.tensor_tensor(
                        aScr[:, 2, :], k, u, OP.mult))
                    E('v', lambda e, k=kim, u=uim: e.tensor_tensor(
                        aScr[:, 3, :], k, u, OP.mult))
                    E('v', lambda e, o=ure: e.tensor_sub(o, aScr[:, 2, :],
                                                         aScr[:, 3, :]))
                    E('v', lambda e, o=uim: e.tensor_add(o, aScr[:, 0, :],
                                                         aScr[:, 1, :]))
                    E('v', lambda e, o=ure, f=fc: e.tensor_copy(aYbf[:, f, :], o))
                    ck_yf = E('v', lambda e, o=uim, f=fc: e.tensor_copy(
                        aYbf[:, 4 + f, :], o), inc=('sV', 1))

                for tb in range(4):
                    pp = ppA if tb % 2 == 0 else ppB
                    W('t', 'sV', ck_yf)
                    W('t', 'sV', ck_inv_ev)
                    for comp in range(2):
                        civ = aCi32 if comp == 0 else aSi32
                        for fc in range(4):
                            E('t', lambda e, p=pp, f=fc, t=tb, cv=civ, cp=comp:
                              e.matmul(
                                p[:, :], cv[:, f, t * 128:(t + 1) * 128],
                                aUf[:, cp * 4 + f, :], start=(cp == 0 and f == 0),
                                stop=False))
                    ckt = None
                    for comp in range(2):
                        cicv = aCic if comp == 0 else aSic
                        for fc in range(4):
                            islast = (comp == 1 and fc == 3)
                            ckt = E('t', lambda e, p=pp, f=fc, t=tb, cv=cicv, cp=comp,
                                    lst=islast: e.matmul(
                                        p[:, :], cv[:, f, t * 128:(t + 1) * 128],
                                        aYbf[:, cp * 4 + f, :], start=False, stop=lst),
                                    inc=('sT', 1) if islast else None)
                    W('v', 'sT', ckt)
                    E('v', lambda e, t=tb, c=chh, u=uB: e.tensor_tensor(
                        aScr[:, 0, :], u[:, t, c * 512:(c + 1) * 512],
                        sb_dp1[:, c * 512:(c + 1) * 512], OP.mult))
                    if chh == 0:
                        ck_inv_ev = E('v', lambda e, p=pp, t=tb: e.tensor_add(
                            aZX[:, t, :], aScr[:, 0, :], p[:, :]), inc=('sV', 1))
                    else:
                        WC('v', last_spill)
                        E('v', lambda e, t=tb: e.tensor_copy(
                            aZasm[:, 0:512], aZX[:, t, :]))
                        ck_inv_ev = E('v', lambda e, p=pp: e.tensor_add(
                            aZasm[:, 512:1024], aScr[:, 0, :], p[:, :]),
                            inc=('sV', 1))
                        ckln = ln_emit(aZasm[:, :], aZasm[:, :])
                        ck_inv_ev = ckln
                        W('g', 'sV', ckln)
                        last_spill = dma(
                            'g', 'dSp', d_x1[blkid * 512 + tb * 128:
                                             blkid * 512 + (tb + 1) * 128, :],
                            aZasm[:, :])
            load_uhalf(blkid + 2)

        if KCUT == 'p1':
            W('g', 'dSp', sch.cnt['dSp'])
            ck_dummy = dma('g', 'dOut', d_out[0:128, :], sb_dump[:, :])
            WC('g', ck_dummy)
            return _finish(sch, blk, sems, nc)

        ck_bar = {s: sch.cnt[s] for s in DMA_SEMS + ENG_SEMS}
        for eng in ('t', 'v', 's', 'y', 'g'):
            for sname, val in ck_bar.items():
                W(eng, sname, val)

        W0SEM = ('dW0a', 'dW0b')
        W1SEM = ('dW1a', 'dW1b')
        W2SEM = ('dW2a', 'dW2b', 'dW2c', 'dW2d')
        ck_w0_use = {}
        ck_w1_use = {}
        ck_w2_use = {}
        ck_w2b_use = {}
        ck_z2b = 0
        ck_out = {}
        def load_w0(p, oc, ck_w0_load, gate):
            W('y', 'sT', gate)
            ck_w0_load[oc] = dma('y', W0SEM[oc % 2], aW0[:, oc % 2, :, :],
                                 d_w0[oc, :, :, :])

        import os as _os2
        W1Q = _os2.environ.get('KW1Q', 'y')
        assert W1Q in ('y', 'g', 's')
        W1R1 = _os2.environ.get('KW1R1') == '1'

        def load_w1(p, oc, ck_w1_load, gate):
            W(W1Q, 'sT', gate)
            slot = 0 if W1R1 else oc % 2
            ck_w1_load[oc] = dma(
                W1Q, W1SEM[oc % 2],
                aW1_f32[:, slot * 2048:(slot + 1) * 2048],
                d_w1[oc, :, :, :].rearrange("q a b -> q (a b)").bitcast(F32))

        def load_w2(p, oc, ck_w2_load, gate):
            W('y', 'sT', gate)
            ck_w2_load[oc] = dma('y', W2SEM[oc % 4], aW2[:, oc % 4, :],
                                 d_w2[oc, :, :])

        for p in range(NPAIR):
            if p > 0:
                W('y', 'sV', ck_z2b)
                WC('y', ck_out[(p - 1, 1)])
            ck_x1_load = dma('y', 'dX', aX1tm,
                             d_x1[p * 512:(p + 1) * 512, :].rearrange(
                                 "(a q) c -> q a c", q=128))
            ck_w0_load = {}
            ck_w1_load = {}
            ck_w2_load = {}
            ck_w2b_load = {}
            for oc in range(2):
                load_w0(p, oc, ck_w0_load, ck_w0_use.get((p - 1, 30 + oc), 0))

            WC('t', ck_x1_load)
            ck_x1t = {}
            ck_tr = {}
            ck_tr_last = 0
            for tc in range(4):
                for fc in range(8):
                    i = tc * 8 + fc
                    pt = ppC if i % 2 == 0 else ppD
                    W('t', 'sV', ck_tr.get(i - 2, 0))
                    ckt = E('t', lambda e, q=pt, t=tc, f=fc: e.transpose(
                        q[:, 0:128], aX1tm[:, t, f * 128:(f + 1) * 128],
                        sb_ident[:, :]), inc=('sT', 1))
                    W('s', 'sT', ckt)
                    ck_x1t[i] = E('s', lambda e, q=pt, t=tc, f=fc: e.activation(
                        aX1hi[:, f, t * 128:(t + 1) * 128], q[:, 0:128], AF.Copy),
                        inc=('sA', 1))
                    W('v', 'sA', ck_x1t[i])
                    ck_tr[i] = E('v', lambda e, q=pt, t=tc, f=fc: e.tensor_sub(
                        aX1lo[:, f, t * 128:(t + 1) * 128], q[:, 0:128],
                        aX1hi[:, f, t * 128:(t + 1) * 128]), inc=('sV', 1))
                    ck_tr_last = ck_tr[i]

            ck_h1_ev = {}
            for oc in range(32):
                pp = ppA if oc % 2 == 0 else ppB
                W('t', 'sV', ck_h1_ev.get(oc - 2, 0))
                WC('t', ck_w0_load[oc])
                W('t', 'sV', ck_tr_last)
                ckt = None
                for ic in range(8):
                    E('t', lambda e, q=pp, o=oc, i=ic: e.matmul(
                        q[:, :], aW0[:, o % 2, i, :], aX1hi[:, i, :],
                        start=(i == 0), stop=False))
                    ckt = E('t', lambda e, q=pp, o=oc, i=ic: e.matmul(
                        q[:, :], aW0[:, o % 2, i, :], aX1lo[:, i, :],
                        start=False, stop=(i == 7)),
                        inc=('sT', 1) if ic == 7 else None)
                ck_w0_use[(p, oc)] = ckt
                if oc + 2 <= 31:
                    load_w0(p, oc + 2, ck_w0_load, ckt)
                elif oc == 30:
                    if W1R1:
                        load_w1(p, 0, ck_w1_load, ck_w1_use.get((p - 1, 31), 0))
                    else:
                        for o2 in range(2):
                            load_w1(p, o2, ck_w1_load,
                                    ck_w1_use.get((p - 1, 30 + o2), 0))
                    for o2 in range(4):
                        load_w2(p, o2, ck_w2_load,
                                ck_w2b_use.get((p - 1, 28 + o2), 0))
                W('s', 'sT', ckt)
                W('s', 'sV', ck_h1_ev.get(oc - 1, 0))
                cka = E('s', lambda e, q=pp: e.activation(aG1f[:, :], q[:, :],
                                                          AF.Relu), inc=('sA', 1))
                W('v', 'sA', cka)
                E('v', lambda e, o=oc: e.tensor_copy(aH1hi[:, o, :], aG1f[:, :]))
                ck_h1_ev[oc] = E('v', lambda e, o=oc: e.tensor_sub(
                    aH1lo[:, o, :], aG1f[:, :], aH1hi[:, o, :]), inc=('sV', 1))
            ck_h1_all = ck_h1_ev[31]
            if KDBG and p == 0:
                W('g', 'sV', ck_h1_all)
                h1flat = aH1hi.rearrange("q a b -> q (a b)").bitcast(F32)
                for di in range(4):
                    dma('g', 'dOut', d_dbg[11 + di, :, :],
                        h1flat[:, di * 1024:(di + 1) * 1024])
                x1flat = aX1hi.rearrange("q a b -> q (a b)").bitcast(F32)
                ckdx = dma('g', 'dOut', d_dbg[15, :, 0:1024], x1flat[:, 0:1024])
                WC('t', ckdx)

            ck_h2a_ev = {}
            ck_h2b_ev = {}
            ck_mm2a = {}
            for oc in range(32):
                pp = ppA if oc % 2 == 0 else ppB
                sl = oc % 2
                W('t', 'sV', ck_h2a_ev.get(oc - 2, 0))
                W('t', 'sV', ck_h1_all)
                WC('t', ck_w1_load[oc])
                ckt = None
                w1slot = 0 if W1R1 else oc % 2
                for ic in range(32):
                    E('t', lambda e, q=pp, s2=w1slot, i=ic: e.matmul(
                        q[:, :], aW1[:, s2, i, :], aH1hi[:, i, :],
                        start=(i == 0), stop=False))
                    ckt = E('t', lambda e, q=pp, s2=w1slot, i=ic: e.matmul(
                        q[:, :], aW1[:, s2, i, :], aH1lo[:, i, :],
                        start=False, stop=(i == 31)),
                        inc=('sT', 1) if ic == 31 else None)
                ck_w1_use[(p, oc)] = ckt
                if W1R1 and oc + 1 <= 31:
                    load_w1(p, oc + 1, ck_w1_load, ckt)
                elif (not W1R1) and oc + 2 <= 31:
                    load_w1(p, oc + 2, ck_w1_load, ckt)
                W('s', 'sT', ckt)
                W('s', 'sV', ck_h2b_ev.get(oc - 1, 0))
                cka = E('s', lambda e, q=pp: e.activation(aH2f[:, :], q[:, :],
                                                          AF.Relu), inc=('sA', 1))
                W('v', 'sA', cka)
                W('v', 'sT', ck_mm2a.get(oc - 2, 0))
                E('v', lambda e, s2=sl: e.tensor_copy(aH2Ahi[:, s2, :],
                                                      aH2f[:, 0:256]))
                ck_h2a_ev[oc] = E('v', lambda e, s2=sl: e.tensor_sub(
                    aH2Alo[:, s2, :], aH2f[:, 0:256], aH2Ahi[:, s2, :]),
                    inc=('sV', 1))
                E('v', lambda e, o=oc: e.tensor_copy(aH2Bhi[:, o, :],
                                                     aH2f[:, 256:512]))
                ck_h2b_ev[oc] = E('v', lambda e, o=oc: e.tensor_sub(
                    aH2Blo[:, o, :], aH2f[:, 256:512], aH2Bhi[:, o, :]),
                    inc=('sV', 1))
                W('t', 'sV', ck_h2a_ev[oc])
                WC('t', ck_w2_load[oc])
                if oc == 0:
                    W('t', 'sV', ck_z2b)
                ckt = None
                for hilo in range(2):
                    src = aH2Ahi if hilo == 0 else aH2Alo
                    for tb in range(2):
                        for chh in range(2):
                            ckt = E('t', lambda e, s2=sl, hl=hilo, t=tb, c=chh, o=oc,
                                    sr=src: e.matmul(
                                        pM[t * 2 + c][:, :],
                                        sr[:, s2, t * 128:(t + 1) * 128],
                                        aW2[:, o % 4, c * 512:(c + 1) * 512],
                                        start=(o == 0 and hl == 0),
                                        stop=(o == 31 and hl == 1)),
                                    inc=('sT', 1) if (hilo == 1 and tb == 1 and
                                                      chh == 1) else None)
                ck_mm2a[oc] = ckt
                ck_w2_use[(p, oc)] = ckt
                if oc + 4 <= 31:
                    load_w2(p, oc + 4, ck_w2_load, ckt)
                elif oc == 31:
                    for o2 in range(4):
                        W('y', 'sT', ck_w2_use[(p, 28 + o2)])
                        ck_w2b_load[o2] = dma('y', W2SEM[o2 % 4], aW2[:, o2 % 4, :],
                                              d_w2[o2, :, :])

            ck_z2a = 0
            for tb in range(2):
                W('v', 'sT', ck_mm2a[31])
                if p > 0:
                    WC('v', ck_out.get((p - 1, 1)))
                ck_stt = None
                for chh in range(2):
                    ck_stt = E('v', lambda e, t=tb, c=chh: e.scalar_tensor_tensor(
                        aZ2[:, c * 512:(c + 1) * 512], pM[t * 2 + c][:, :],
                        sb_alpha[:, 0:1], aX1tm[:, t, c * 512:(c + 1) * 512],
                        OP.mult, OP.add),
                        inc=('sV', 1) if (KDBG and p == 0 and tb == 0
                                          and chh == 1) else None)
                if KDBG and p == 0 and tb == 0:
                    W('s', 'sV', ck_stt)
                    E('s', lambda e: e.activation(sb_dump[:, 0:512],
                                                  pM[0][:, :], AF.Copy))
                    ckps = E('s', lambda e: e.activation(sb_dump[:, 512:1024],
                                                         pM[1][:, :], AF.Copy),
                             inc=('sA', 1))
                    W('g', 'sA', ckps)
                    dma('g', 'dOut', d_dbg[7, :, :], aZ2[:, :])
                    dma('g', 'dOut', d_dbg[8, :, 0:512], sb_dump[:, 0:512])
                    E('g', lambda e: e.dma_start(
                        out=d_dbg[8, :, 512:513], in_=sb_alpha[:, 0:1],
                        allow_slow_non_contiguous=True), inc=('dOut', 16))
                    dma('g', 'dOut', d_dbg[9, :, 0:512], sb_dump[:, 512:1024])
                    ckdz = dma('g', 'dOut', d_dbg[10, :, :], aX1tm[:, 0, :])
                    WC('v', ckdz)
                    WC('t', ckdz)
                    WC('s', ckdz)
                ck_z2a = ln_emit(aZ2[:, :], aX1tm[:, tb, :])
            W('g', 'sV', ck_z2a)
            ck_out[(p, 0)] = dma('g', 'dOut',
                                 d_out[p * 512:p * 512 + 256, :].rearrange(
                                     "(a q) c -> q a c", q=128), aX1tm[:, 0:2, :])

            ck_mm2b = 0
            for oc in range(32):
                W('t', 'sV', ck_z2a)
                WC('t', ck_w2b_load[oc])
                W('t', 'sV', ck_h2b_ev[31])
                ckt = None
                for hilo in range(2):
                    src = aH2Bhi if hilo == 0 else aH2Blo
                    for tb in range(2):
                        for chh in range(2):
                            ckt = E('t', lambda e, hl=hilo, t=tb, c=chh, o=oc,
                                    sr=src: e.matmul(
                                        pM[t * 2 + c][:, :],
                                        sr[:, o, t * 128:(t + 1) * 128],
                                        aW2[:, o % 4, c * 512:(c + 1) * 512],
                                        start=(o == 0 and hl == 0),
                                        stop=(o == 31 and hl == 1)),
                                    inc=('sT', 1) if (hilo == 1 and tb == 1 and
                                                      chh == 1) else None)
                ck_mm2b = ckt
                ck_w2b_use[(p, oc)] = ckt
                if oc + 4 <= 31:
                    W('y', 'sT', ckt)
                    ck_w2b_load[oc + 4] = dma('y', W2SEM[(oc + 4) % 4],
                                              aW2[:, (oc + 4) % 4, :],
                                              d_w2[oc + 4, :, :])
            for tb in range(2):
                W('v', 'sT', ck_mm2b)
                WC('v', ck_out[(p, 0)])
                for chh in range(2):
                    E('v', lambda e, t=tb, c=chh: e.scalar_tensor_tensor(
                        aZ2[:, c * 512:(c + 1) * 512], pM[t * 2 + c][:, :],
                        sb_alpha[:, 0:1], aX1tm[:, 2 + t, c * 512:(c + 1) * 512],
                        OP.mult, OP.add))
                ck_z2b = ln_emit(aZ2[:, :], aX1tm[:, 2 + tb, :])
            W('g', 'sV', ck_z2b)
            ck_out[(p, 1)] = dma('g', 'dOut',
                                 d_out[p * 512 + 256:(p + 1) * 512, :].rearrange(
                                     "(a q) c -> q a c", q=128), aX1tm[:, 2:4, :])

        W('g', 'dOut', sch.cnt['dOut'])
        return _finish(sch, blk, sems, nc)


def _finish(sch, blk, sems, nc):
    @blk.tensor
    def _(eng):
        sch.run('t', eng, sems)

    @blk.vector
    def _(eng):
        sch.run('v', eng, sems)

    @blk.scalar
    def _(eng):
        sch.run('s', eng, sems)

    @blk.sync
    def _(eng):
        sch.run('y', eng, sems)

    @blk.gpsimd
    def _(eng):
        sch.run('g', eng, sems)

    return nc


def _prep_inputs(inputs):
    u = np.asarray(inputs['L0_attn_u'])
    D = np.ascontiguousarray(np.asarray(inputs['L0_attn_D'], dtype=np.float32))
    lam_re = np.asarray(inputs['L0_attn_Lambda_re'], dtype=np.float32)
    lam_im = np.asarray(inputs['L0_attn_Lambda_im'], dtype=np.float32)
    log_dt = np.asarray(inputs['L0_attn_log_dt'], dtype=np.float32)
    W_re = np.asarray(inputs['L0_attn_W_re'], dtype=np.float32)
    W_im = np.asarray(inputs['L0_attn_W_im'], dtype=np.float32)
    k0 = np.asarray(inputs['L0_ffn_k0'])
    k1 = np.asarray(inputs['L0_ffn_k1'])
    k2 = np.asarray(inputs['L0_ffn_k2'])
    s0 = np.float32(np.asarray(inputs['L0_ffn_s0']))
    s1 = np.float32(np.asarray(inputs['L0_ffn_s1']))
    s2 = np.float32(np.asarray(inputs['L0_ffn_s2']))

    logdt_arr = np.empty((128, 512), np.float32)
    ld = log_dt.reshape(512, 2)
    logdt_arr[0::2, :] = ld[:, 0][None, :]
    logdt_arr[1::2, :] = ld[:, 1][None, :]
    lamre_col = np.ascontiguousarray(np.repeat(lam_re, 2)[:, None])
    lamim_col = np.ascontiguousarray(np.repeat(lam_im, 2)[:, None])
    wre_arr = np.empty((128, 512), np.float32)
    wim_arr = np.empty((128, 512), np.float32)
    wr = W_re.reshape(512, 2, 64)
    wi = W_im.reshape(512, 2, 64)
    for par in range(2):
        wre_arr[par::2, :] = wr[:, par, :].T
        wim_arr[par::2, :] = wi[:, par, :].T
    k0t = np.ascontiguousarray(k0[:1024]).view(np.uint8)
    k0b = np.ascontiguousarray(k0[1024:]).view(np.uint8)
    k1u = np.ascontiguousarray(k1).view(np.uint8)
    k2u = np.ascontiguousarray(k2).view(np.uint8)
    svec = np.broadcast_to(np.array([s0, s1, s2], np.float32), (128, 3)).copy()
    D_rep = np.broadcast_to(D, (128, H)).copy()

    consts = _host_consts()
    shared = dict(logdt_arr=logdt_arr, lamre_col=lamre_col, lamim_col=lamim_col,
                  wre_arr=wre_arr, wim_arr=wim_arr, k0t=k0t, k0b=k0b, k1=k1u,
                  k2=k2u, svec=svec, D_rep=D_rep)
    for nm, arr in consts.items():
        shared['c_' + nm] = np.ascontiguousarray(arr)

    in_maps = []
    for c in range(8):
        b, half = c // 2, c % 2
        tok0 = half * 4096
        if half == 0:
            halo = np.zeros((512, H), np.float32)
        else:
            halo = u[b, tok0 - 512:tok0]
        shard = np.concatenate([halo, u[b, tok0:tok0 + 4096]], axis=0)
        m = dict(shared)
        m['u_shard'] = np.ascontiguousarray(shard, dtype=np.float32)
        in_maps.append(m)
    return in_maps


_NC_CACHE = None


def kernel(**inputs):
    global _NC_CACHE
    if _NC_CACHE is None:
        _NC_CACHE = build_nc()
    nc = _NC_CACHE
    in_maps = _prep_inputs(inputs)
    res = run_bass_kernel_spmd(nc, in_maps, core_ids=list(range(8)))
    outs = [np.asarray(res.results[c]['out']) for c in range(8)]
    full = np.stack([np.concatenate([outs[2 * b], outs[2 * b + 1]], axis=0)
                     for b in range(4)], axis=0)
    return full.astype(np.float32)


# version 10
# speedup vs baseline: 1.1505x; 1.1505x over previous
from contextlib import ExitStack

import numpy as np
import ml_dtypes

import concourse.bass as bass
import concourse.mybir as mybir
from concourse.bass_utils import run_bass_kernel_spmd

F32 = mybir.dt.float32
BF16 = mybir.dt.bfloat16
I32 = mybir.dt.int32
U8 = mybir.dt.uint8
AF = mybir.ActivationFunctionType
OP = mybir.AluOpType
AXX = mybir.AxisListType.X

bf16 = ml_dtypes.bfloat16

H = 1024
TOK = 4096
NBLK = 8
NPAIR = 8
EPS = 1e-5
R1024 = float(np.float32(1.0 / 1024.0))

CHI = float(np.float32(2 * np.pi))
CLO = float(np.float32(2 * np.pi - np.float64(np.float32(2 * np.pi))))
I2HI = float(np.float32(1.0 / (2 * np.pi)))
I2LO = float(np.float32(1.0 / (2 * np.pi) - np.float64(np.float32(1.0 / (2 * np.pi)))))
PHHI = float(np.float32(np.pi / 2))
PHLO = float(np.float32(np.pi / 2 - np.float64(np.float32(np.pi / 2))))
PI_SAFE = float(np.nextafter(np.float32(np.pi), np.float32(0)))

DMA_SEMS = ('dC', 'dKa', 'dKb', 'dOa', 'dOb', 'dU0', 'dU1', 'dSp', 'dX',
            'dW0a', 'dW0b', 'dW1a', 'dW1b', 'dW2a', 'dW2b', 'dW2c', 'dW2d',
            'dOut')
ENG_SEMS = ('sT', 'sV', 'sA')


def _split_const(M64):
    m32 = M64.astype(np.float32)
    corr = (M64 - m32.astype(np.float64)).astype(bf16)
    return m32, corr


def _host_consts():
    t = np.arange(512, dtype=np.float64)
    f = np.arange(512, dtype=np.float64)
    ang = 2 * np.pi * np.outer(t, f) / 1024.0
    C32, Ccorr = _split_const(np.cos(ang))
    S32, Scorr = _split_const(-np.sin(ang))
    t2 = np.arange(512, dtype=np.float64)
    wf = np.where(f == 0, 1.0, 2.0)
    angi = 2 * np.pi * np.outer(f, 512 + t2) / 1024.0
    Ci32, Cicorr = _split_const(wf[:, None] * np.cos(angi) / 1024.0)
    Si32, Sicorr = _split_const(-wf[:, None] * np.sin(angi) / 1024.0)
    altcol = ((-1.0) ** np.arange(128, dtype=np.float64)).astype(np.float32)[:, None]
    S32[:, 0] = ((-1.0) ** t).astype(np.float32)
    Scorr[:, 0] = 0
    Si32[0, :] = (((-1.0) ** t2) / 1024.0).astype(np.float32)
    Sicorr[0, :] = 0
    iota32 = np.broadcast_to(np.arange(32, dtype=np.float32), (128, 32)).copy()
    iota16 = np.broadcast_to(np.arange(16, dtype=np.float32) * 32.0, (128, 16)).copy()
    ident = np.eye(128, dtype=np.float32)
    par = (np.arange(128) & 1)
    Mplus = np.zeros((128, 254), np.float32)
    Mplus[:, 126] = (par == 0)
    Mplus[:, 127] = (par == 1)
    Mminus = -Mplus
    cpool = np.broadcast_to(np.array([-1.0, 1.0, 2.0, EPS], np.float32),
                            (128, 4)).copy()
    return dict(C32=C32, Ccorr=Ccorr, S32=S32, Scorr=Scorr,
                Ci32=Ci32, Cicorr=Cicorr, Si32=Si32, Sicorr=Sicorr,
                altcol=altcol, iota32=iota32, iota16=iota16,
                ident=ident, Mplus=Mplus, Mminus=Mminus, cpool=cpool)


class Sched:
    ENGS = ('t', 'v', 's', 'y', 'g')

    def __init__(self):
        self.ops = {e: [] for e in self.ENGS}
        self.cnt = {s: 0 for s in DMA_SEMS + ENG_SEMS}
        self.lastwait = {}

    def wait(self, eng, sem, val):
        if val is None or val <= 0:
            return
        key = (eng, sem)
        if self.lastwait.get(key, -1) >= val:
            return
        self.lastwait[key] = val
        self.ops[eng].append(('w', sem, val))

    def waitc(self, eng, ck):
        if ck:
            self.wait(eng, ck[0], ck[1])

    def emit(self, eng, fn, inc=None):
        if inc is not None:
            sname, k = inc
            self.cnt[sname] += k
            post = self.cnt[sname]
            self.ops[eng].append(('i', fn, sname, k))
            return post
        self.ops[eng].append(('e', fn))
        return None

    def run(self, eng, proxy, sems):
        for op in self.ops[eng]:
            if op[0] == 'w':
                proxy.wait_ge(sems[op[1]], op[2])
            elif op[0] == 'e':
                op[1](proxy)
            else:
                op[1](proxy).then_inc(sems[op[2]], op[3])


class Arena:
    def __init__(self, handle, nf32):
        self.h = handle
        self.n = nf32
        self.off = 0

    @staticmethod
    def _shape(ap, shape):
        if len(shape) == 3:
            return ap.rearrange("p (a b) -> p a b", b=shape[2])
        if len(shape) == 4:
            return ap.rearrange("p (a b c) -> p a b c", b=shape[2], c=shape[3])
        return ap

    def f32(self, shape):
        n = int(np.prod(shape[1:]))
        ap = self.h[0:shape[0], self.off:self.off + n]
        self.off += n
        assert self.off <= self.n, f"arena overflow {self.off} > {self.n}"
        return self._shape(ap, shape)

    def bf(self, shape):
        n = int(np.prod(shape[1:]))
        slots = (n + 1) // 2
        ap = self.h[0:shape[0], self.off:self.off + slots].bitcast(BF16)
        self.off += slots
        assert self.off <= self.n, f"arena overflow {self.off} > {self.n}"
        return self._shape(ap, shape)


ARENA_N = 44500


def build_nc():
    nc = bass.Bass("TRN2", target_bir_lowering=False,
                   detect_race_conditions=False)

    d_u = nc.dram_tensor("u_shard", [TOK + 512, H], F32, kind="ExternalInput").ap()
    d_logdt = nc.dram_tensor("logdt_arr", [128, 512], F32, kind="ExternalInput").ap()
    d_lamre = nc.dram_tensor("lamre_col", [128, 1], F32, kind="ExternalInput").ap()
    d_lamim = nc.dram_tensor("lamim_col", [128, 1], F32, kind="ExternalInput").ap()
    d_wre = nc.dram_tensor("wre_arr", [128, 512], F32, kind="ExternalInput").ap()
    d_wim = nc.dram_tensor("wim_arr", [128, 512], F32, kind="ExternalInput").ap()
    d_k0t = nc.dram_tensor("k0t", [1024, 4096], U8, kind="ExternalInput").ap()
    d_k0b = nc.dram_tensor("k0b", [1024, 4096], U8, kind="ExternalInput").ap()
    d_k1 = nc.dram_tensor("k1", [4096, 4096], U8, kind="ExternalInput").ap()
    d_k2 = nc.dram_tensor("k2", [4096, 1024], U8, kind="ExternalInput").ap()
    d_svec = nc.dram_tensor("svec", [128, 3], F32, kind="ExternalInput").ap()
    d_Drep = nc.dram_tensor("D_rep", [128, H], F32, kind="ExternalInput").ap()
    CONSTS = _host_consts()
    dc = {}
    for name, arr in CONSTS.items():
        dt = BF16 if arr.dtype == bf16 else F32
        dc[name] = nc.dram_tensor("c_" + name, list(arr.shape), dt,
                                  kind="ExternalInput").ap()
    d_out = nc.dram_tensor("out", [TOK, H], F32, kind="ExternalOutput").ap()
    d_dbg = None
    d_x1 = nc.dram_tensor("x1scr", [TOK, H], F32, kind="Internal").ap()
    d_w0 = nc.dram_tensor("w0scr", [32, 128, 8, 128], BF16, kind="Internal").ap()
    d_w1 = nc.dram_tensor("w1scr", [32, 128, 32, 128], BF16, kind="Internal").ap()
    d_w2 = nc.dram_tensor("w2scr", [32, 128, 1024], BF16, kind="Internal").ap()

    import os
    KCUT = os.environ.get('KCUT', '')
    KDBG = os.environ.get('KDBG') == '1'

    sch = Sched()
    import os as _os
    if _os.environ.get('KDBG') == '1':
        d_dbg = nc.dram_tensor("dbg", [16, 128, 1024], F32,
                               kind="ExternalOutput").ap()

    with ExitStack() as _st:
        def sb(name, shape, dtype):
            return _st.enter_context(nc.sbuf_tensor(name, shape, dtype))

        sb_cpool = sb("cn_cpool", [128, 4], F32)
        nc.const_aps.register(F32, -1.0, sb_cpool[:, 0:1])
        nc.const_aps.register(F32, 1.0, sb_cpool[:, 1:2])
        nc.const_aps.register(F32, 2.0, sb_cpool[:, 2:3])
        nc.const_aps.register(F32, EPS, sb_cpool[:, 3:4])
        sb_altcol = sb("cn_altcol", [128, 1], F32)
        sb_iota32 = sb("cn_iota32", [128, 32], F32)
        sb_iota16 = sb("cn_iota16", [128, 16], F32)
        sb_ident = sb("cn_ident", [128, 128], F32)
        sb_mplus = sb("cn_mplus", [128, 254], F32)
        sb_mminus = sb("cn_mminus", [128, 254], F32)
        sb_logdt = sb("in_logdt", [128, 512], F32)
        sb_lamre = sb("in_lamre", [128, 1], F32)
        sb_lamim = sb("in_lamim", [128, 1], F32)
        sb_wre = sb("in_wre", [128, 512], F32)
        sb_wim = sb("in_wim", [128, 512], F32)
        sb_svec = sb("in_svec", [128, 3], F32)
        sb_drep = sb("in_drep", [128, H], F32)
        sb_ab = sb("t_ab", [128, 1024], F32)
        sb_alpha = sb("t_alpha", [128, 1], F32)
        sb_dp1 = sb("t_dp1", [128, H], F32)
        sb_dump = sb("t_dump", [128, 1024], F32)
        sb_stats = sb("t_stats", [128, 16], F32)
        sb_mi = sb("t_mi", [128, 1024], I32)
        sb_arena = sb("arena", [128, ARENA_N], F32)
        ppA = _st.enter_context(nc.psum_tensor("ppA", [128, 512], F32))
        ppB = _st.enter_context(nc.psum_tensor("ppB", [128, 512], F32))
        ppC = _st.enter_context(nc.psum_tensor("ppC", [128, 512], F32))
        ppD = _st.enter_context(nc.psum_tensor("ppD", [128, 512], F32))
        pM = [_st.enter_context(nc.psum_tensor(f"pM{i}", [128, 512], F32))
              for i in range(4)]
        pM0 = pM[0]
        sems = {}
        for s in DMA_SEMS + ENG_SEMS:
            sems[s] = _st.enter_context(nc.semaphore(s))
        blk = _st.enter_context(nc.Block())

        ar = Arena(sb_arena, ARENA_N)

        aC32 = ar.f32([128, 4, 512])
        aS32 = ar.f32([128, 4, 512])
        aCc = ar.bf([128, 4, 512])
        aSc = ar.bf([128, 4, 512])
        aCi32 = ar.f32([128, 4, 512])
        aSi32 = ar.f32([128, 4, 512])
        aCic = ar.bf([128, 4, 512])
        aSic = ar.bf([128, 4, 512])
        aKfRe = ar.f32([128, 4, 1024])
        aKfIm = ar.f32([128, 4, 1024])
        aKfX = ar.f32([128, 2, 1024])
        p1_base = ar.off

        aU = [ar.f32([128, 4, 1024]) for _ in range(2)]
        aUbf = ar.bf([128, 8, 512])
        aUf = ar.f32([128, 8, 512])
        aYbf = ar.bf([128, 8, 512])
        aZX = ar.f32([128, 4, 512])
        aZasm = ar.f32([128, 1024])
        aScr = ar.f32([128, 4, 512])
        p1_end = ar.off

        ar.off = p1_base
        cvA = ar.f32([128, 1024]).bitcast(U8)
        cvB = ar.f32([128, 1024]).bitcast(U8)
        cvAb = ar.f32([128, 2048]).bitcast(BF16)
        cvBb = ar.f32([128, 2048]).bitcast(BF16)

        ar.off = p1_base
        g = {}
        for nm, w in (('ApRe', 1024), ('ApIm', 1024), ('EA', 1024), ('snA', 1024),
                      ('csA', 1024), ('tA', 1024), ('tB', 1024), ('argI', 1024),
                      ('BRe', 512), ('BIm', 512), ('EB', 512), ('snB', 512),
                      ('csB', 512), ('SS1', 512), ('SS2', 512), ('khl', 512),
                      ('SS1b', 512), ('SS2b', 512)):
            g[nm] = ar.f32([128, w])
        aKlh = ar.f32([128, 4, 1024])
        aKbf = ar.bf([128, 4, 1024])
        assert ar.off <= p1_end, (ar.off, p1_end)
        ar.off = p1_end

        ar.off = 0
        aH1hi = ar.bf([128, 32, 512])
        aH1lo = ar.bf([128, 32, 512])
        aH2Bhi = ar.bf([128, 32, 256])
        aH2Blo = ar.bf([128, 32, 256])
        aH2f = ar.f32([128, 512])
        aH2Ahi = ar.bf([128, 2, 256])
        aH2Alo = ar.bf([128, 2, 256])
        aX1tm = ar.f32([128, 4, 1024])
        aX1hi = ar.bf([128, 8, 512])
        aX1lo = ar.bf([128, 8, 512])
        aW0 = ar.bf([128, 2, 8, 128])
        w1_off = ar.off
        aW1 = ar.bf([128, 2, 32, 128])
        aW1_f32 = sb_arena[0:128, w1_off:w1_off + 4096]
        aW2 = ar.bf([128, 4, 1024])
        aG1f = ar.f32([128, 512])
        aZ2 = ar.f32([128, 1024])
        assert ar.off <= ARENA_N, ar.off

        E = sch.emit
        W = sch.wait
        WC = sch.waitc

        def dma(q, sem, dst, src):
            post = E(q, lambda e, d=dst, r=src: e.dma_start(out=d, in_=r),
                     inc=(sem, 16))
            return (sem, post)

        dma('y', 'dC', aC32, dc['C32'].rearrange("(a p) f -> p a f", p=128))
        dma('y', 'dC', aS32, dc['S32'].rearrange("(a p) f -> p a f", p=128))
        dma('y', 'dC', aCc, dc['Ccorr'].rearrange("(a p) f -> p a f", p=128))
        dma('y', 'dC', aSc, dc['Scorr'].rearrange("(a p) f -> p a f", p=128))
        dma('y', 'dC', aCi32, dc['Ci32'].rearrange("(a p) f -> p a f", p=128))
        dma('y', 'dC', aSi32, dc['Si32'].rearrange("(a p) f -> p a f", p=128))
        dma('y', 'dC', aCic, dc['Cicorr'].rearrange("(a p) f -> p a f", p=128))
        dma('y', 'dC', aSic, dc['Sicorr'].rearrange("(a p) f -> p a f", p=128))
        for nm, dst in (('cpool', sb_cpool), ('altcol', sb_altcol),
                        ('iota32', sb_iota32), ('iota16', sb_iota16),
                        ('ident', sb_ident), ('Mplus', sb_mplus),
                        ('Mminus', sb_mminus)):
            dma('y', 'dC', dst[:, :], dc[nm])
        for src, dst in ((d_logdt, sb_logdt), (d_lamre, sb_lamre), (d_lamim, sb_lamim),
                         (d_wre, sb_wre), (d_wim, sb_wim), (d_svec, sb_svec),
                         (d_Drep, sb_drep)):
            dma('y', 'dC', dst[:, :], src)
        ck_consts = ('dC', sch.cnt['dC'])

        a_col = sb_ab[:, 0:512]
        b_col = sb_ab[:, 512:1024]
        WC('s', ck_consts)
        E('s', lambda e: e.activation(sb_logdt[:, :], sb_logdt[:, :], AF.Exp))
        E('s', lambda e: e.activation(sb_lamre[:, :], sb_lamre[:, :], AF.Exp))
        ck = E('s', lambda e: e.activation(sb_dp1[:, :], sb_drep[:, :], AF.Identity,
                                           bias=1.0), inc=('sA', 1))
        W('v', 'sA', ck)
        E('v', lambda e: e.tensor_mul(sb_stats[:, 0:1], sb_svec[:, 0:1],
                                      sb_svec[:, 1:2]))
        E('v', lambda e: e.tensor_scalar(a_col, sb_logdt[:, :], sb_lamre[:, 0:1],
                                         -1.0, OP.mult, OP.mult))
        E('v', lambda e: e.tensor_scalar(sb_stats[:, 1:2], sb_svec[:, 2:3], 8.0,
                                         None, OP.mult))
        E('v', lambda e: e.tensor_scalar(b_col, sb_logdt[:, :], sb_lamim[:, 0:1],
                                         None, OP.mult))
        E('v', lambda e: e.tensor_mul(sb_alpha[:, :], sb_stats[:, 0:1],
                                      sb_stats[:, 1:2]), inc=('sV', 1))

        if KCUT == 'p0a':
            W('g', 'sV', sch.cnt['sV'])
            ck_dummy = dma('g', 'dOut', d_out[0:128, :], sb_dump[:, :])
            WC('g', ck_dummy)
            return _finish(sch, blk, sems, nc)

        ck_w1out = {}
        ck_w1cvt = {}
        for ic in range(32):
            srcu, ksem = (cvA, 'dKa') if ic % 2 == 0 else (cvB, 'dKb')
            dstb, osem = (cvAb, 'dOa') if ic % 2 == 0 else (cvBb, 'dOb')
            W('y', 'sA', ck_w1cvt.get(ic - 2, 0))
            ckl = dma('y', ksem, srcu[:, 0:4096], d_k1[ic * 128:(ic + 1) * 128, :])
            WC('s', ckl)
            WC('s', ck_w1out.get(ic - 2))
            cka = E('s', lambda e, d=dstb, r=srcu: e.activation(
                d[:, 0:4096], r[:, 0:4096], AF.Identity, bias=-1.0, scale=2.0),
                inc=('sA', 1))
            ck_w1cvt[ic] = cka
            W('g', 'sA', cka)
            for ocg in range(4):
                ck_w1out[ic] = dma(
                    'g', osem,
                    d_w1[ocg * 8:(ocg + 1) * 8, :, ic, :].rearrange(
                        "o q c -> q o c"),
                    dstb[:, ocg * 1024:(ocg + 1) * 1024].rearrange(
                        "p (o c) -> p o c", c=128))
        ck_w0out = {}
        ck_w0cvt = 0
        for ic in range(8):
            W('y', 'sV', ck_w0cvt)
            if ic == 0:
                W('y', 'sA', ck_w1cvt[31])
            ckl0 = dma('y', 'dKa', cvA[:, 0:4096], d_k0t[ic * 128:(ic + 1) * 128, :])
            ckl1 = dma('y', 'dKb', cvB[:, 0:4096], d_k0b[ic * 128:(ic + 1) * 128, :])
            WC('v', ckl0)
            WC('v', ckl1)
            if ic == 0:
                WC('v', ck_w1out[30])
            WC('v', ck_w0out.get(ic - 1))
            ckv = E('v', lambda e: e.tensor_sub(cvAb[:, 0:4096], cvA[:, 0:4096],
                                                cvB[:, 0:4096]), inc=('sV', 1))
            ck_w0cvt = ckv
            W('g', 'sV', ckv)
            for ocg in range(4):
                ck_w0out[ic] = dma(
                    'g', 'dOa',
                    d_w0[ocg * 8:(ocg + 1) * 8, :, ic, :].rearrange(
                        "o q c -> q o c"),
                    cvAb[:, ocg * 1024:(ocg + 1) * 1024].rearrange(
                        "p (o c) -> p o c", c=128))
        ck_w2out = {}
        ck_w2cvt = 0
        for ic in range(32):
            W('y', 'sA', ck_w2cvt)
            if ic == 0:
                W('y', 'sV', ck_w0cvt)
            ckl = dma('y', 'dKa', cvA[:, 0:1024], d_k2[ic * 128:(ic + 1) * 128, :])
            WC('s', ckl)
            if ic == 0:
                WC('s', ck_w1out[31])
            WC('s', ck_w2out.get(ic - 1))
            cka = E('s', lambda e: e.activation(cvBb[:, 0:1024], cvA[:, 0:1024],
                                                AF.Identity, bias=-1.0, scale=2.0),
                    inc=('sA', 1))
            ck_w2cvt = cka
            W('g', 'sA', cka)
            ck_w2out[ic] = dma('g', 'dOb', d_w2[ic, :, :], cvBb[:, 0:1024])
        ck_oa_total = ('dOa', sch.cnt['dOa'])
        ck_ob_total = ('dOb', sch.cnt['dOb'])

        if KCUT == 'p0b':
            WC('g', ck_oa_total)
            WC('g', ck_ob_total)
            ck_dummy = dma('g', 'dOut', d_out[0:128, :], sb_dump[:, :])
            WC('g', ck_dummy)
            return _finish(sch, blk, sems, nc)

        def r3(ap, b):
            return ap.rearrange("p (a b) -> p a b", b=b)

        def sincos(argI, sn_out, cs_out, width):
            tA = g['tA'][:, 0:width]
            tB = g['tB'][:, 0:width]
            mi = sb_mi[:, 0:width]
            sg = sb_dump[:, 0:width]
            cks = None
            for mode in (0, 1):
                if mode == 0:
                    E('v', lambda e: e.tensor_scalar(tA, argI, I2LO, None, OP.mult))
                else:
                    W('v', 'sA', cks)
                    E('v', lambda e: e.tensor_scalar(tA, argI, I2LO, 0.25,
                                                     OP.mult, OP.add))
                E('v', lambda e: e.scalar_tensor_tensor(
                    tA, argI, I2HI, tA, OP.mult, OP.add))
                E('v', lambda e: e.tensor_copy(mi, tA))
                E('v', lambda e: e.tensor_copy(tB, mi))
                E('v', lambda e: e.tensor_sub(tA, tA, tB))
                E('v', lambda e: e.tensor_single_scalar(sg, tA, 0.5, OP.is_gt))
                E('v', lambda e: e.tensor_single_scalar(tA, tA, -0.5, OP.is_lt))
                E('v', lambda e: e.tensor_add(tB, tB, sg))
                E('v', lambda e: e.tensor_sub(tB, tB, tA))
                E('v', lambda e: e.scalar_tensor_tensor(tA, tB, -CHI, argI,
                                                        OP.mult, OP.add))
                E('v', lambda e: e.scalar_tensor_tensor(tA, tB, -CLO, tA,
                                                        OP.mult, OP.add))
                if mode == 1:
                    E('v', lambda e: e.tensor_scalar(tA, tA, PHHI, PHLO,
                                                     OP.add, OP.add))
                ckv = E('v', lambda e: e.tensor_scalar(tA, tA, PI_SAFE, -PI_SAFE,
                                                       OP.min, OP.max), inc=('sV', 1))
                W('s', 'sV', ckv)
                dst = sn_out if mode == 0 else cs_out
                cks = E('s', lambda e, d=dst: e.activation(d, tA, AF.Sin),
                        inc=('sA', 1))
            return cks

        WC('v', ck_oa_total)
        WC('v', ck_ob_total)
        WC('s', ck_oa_total)
        WC('s', ck_ob_total)

        KHG = int(os.environ.get('KHG', '16'))
        ck_ss_mm = 0
        ck_ssmm_hist = {}
        ck_klh_v = 0
        for hg in range(KHG):
            j0 = hg * 32
            a_sl = a_col[:, j0:j0 + 32].unsqueeze(2).broadcast_to([128, 32, 32])
            b_sl = b_col[:, j0:j0 + 32].unsqueeze(2).broadcast_to([128, 32, 32])
            wre_sl = sb_wre[:, j0:j0 + 32].unsqueeze(2).broadcast_to([128, 32, 32])
            wim_sl = sb_wim[:, j0:j0 + 32].unsqueeze(2).broadcast_to([128, 32, 32])
            i32b = sb_iota32[:, :].unsqueeze(1).broadcast_to([128, 32, 32])
            W('v', 'sA', sch.cnt['sA'])
            ckv = E('v', lambda e, a=a_sl, i=i32b: e.tensor_tensor(
                r3(g['ApRe'], 32), a, i, OP.mult), inc=('sV', 1))
            W('s', 'sV', ckv)
            E('s', lambda e: e.activation(g['EA'], g['ApRe'], AF.Exp))
            E('v', lambda e, b=b_sl, i=i32b: e.tensor_tensor(
                r3(g['argI'], 32), b, i, OP.mult))
            cks = sincos(g['argI'], g['snA'], g['csA'], 1024)
            W('v', 'sA', cks)
            E('v', lambda e, w=wre_sl: e.tensor_tensor(r3(g['tA'], 32),
                                                       r3(g['csA'], 32), w, OP.mult))
            E('v', lambda e, w=wim_sl: e.tensor_tensor(r3(g['tB'], 32),
                                                       r3(g['snA'], 32), w, OP.mult))
            E('v', lambda e: e.tensor_sub(g['tA'], g['tA'], g['tB']))
            E('v', lambda e: e.tensor_mul(g['ApRe'], g['tA'], g['EA']))
            E('v', lambda e, w=wre_sl: e.tensor_tensor(r3(g['tA'], 32),
                                                       r3(g['snA'], 32), w, OP.mult))
            E('v', lambda e, w=wim_sl: e.tensor_tensor(r3(g['tB'], 32),
                                                       r3(g['csA'], 32), w, OP.mult))
            E('v', lambda e: e.tensor_add(g['tA'], g['tA'], g['tB']))
            E('v', lambda e: e.tensor_mul(g['ApIm'], g['tA'], g['EA']))
            a_sb = a_col[:, j0:j0 + 32].unsqueeze(2).broadcast_to([128, 32, 16])
            b_sb = b_col[:, j0:j0 + 32].unsqueeze(2).broadcast_to([128, 32, 16])
            i16b = sb_iota16[:, :].unsqueeze(1).broadcast_to([128, 32, 16])
            ckv = E('v', lambda e, a=a_sb, i=i16b: e.tensor_tensor(
                r3(g['BRe'], 16), a, i, OP.mult), inc=('sV', 1))
            W('s', 'sV', ckv)
            E('s', lambda e: e.activation(g['EB'], g['BRe'], AF.Exp))
            E('v', lambda e, b=b_sb, i=i16b: e.tensor_tensor(
                r3(g['argI'][:, 0:512], 16), b, i, OP.mult))
            cks = sincos(g['argI'][:, 0:512], g['snB'], g['csB'], 512)
            W('v', 'sA', cks)
            E('v', lambda e: e.tensor_mul(g['BRe'], g['EB'], g['csB']))
            ckB = E('v', lambda e: e.tensor_mul(g['BIm'], g['EB'], g['snB']),
                    inc=('sV', 1))
            kp = ppA if (hg // 2) % 2 == 0 else ppB
            ApRe_j = r3(g['ApRe'], 32)
            ApIm_j = r3(g['ApIm'], 32)
            BRe_j = r3(g['BRe'], 16)
            BIm_j = r3(g['BIm'], 16)
            for jl in range(32):
                jfull = (hg % 2) * 32 + jl
                jg = hg * 32 + jl
                s1n = 'SS1' if jg % 2 == 0 else 'SS1b'
                s2n = 'SS2' if jg % 2 == 0 else 'SS2b'
                ss1 = r3(g[s1n], 32)
                ss2 = r3(g[s2n], 32)
                apre = ApRe_j[:, jl, :].unsqueeze(1).broadcast_to([128, 16, 32])
                apim = ApIm_j[:, jl, :].unsqueeze(1).broadcast_to([128, 16, 32])
                bre = BRe_j[:, jl, :].unsqueeze(2).broadcast_to([128, 16, 32])
                bim = BIm_j[:, jl, :].unsqueeze(2).broadcast_to([128, 16, 32])
                W('v', 'sV', ckB)
                W('v', 'sT', ck_ssmm_hist.get(jg - 2, 0))
                E('v', lambda e, a=apre, b=bre, o=ss1: e.tensor_tensor(o, a, b, OP.mult))
                ckss = E('v', lambda e, a=apim, b=bim, o=ss2: e.tensor_tensor(
                    o, a, b, OP.mult), inc=('sV', 1))
                W('t', 'sV', ckss)
                c0 = 126 - 2 * jfull
                E('t', lambda e, p=kp, c=c0, s1=s1n: e.matmul(
                    p[:, :], sb_mplus[:, c:c + 128], g[s1],
                    start=(c == 126), stop=False))
                ck_ss_mm = E('t', lambda e, p=kp, c=c0, s2=s2n,
                             lst=(jfull == 63): e.matmul(
                    p[:, :], sb_mminus[:, c:c + 128], g[s2],
                    start=False, stop=lst), inc=('sT', 1))
                ck_ssmm_hist[jg] = ck_ss_mm
            if hg % 2 == 1:
                grp = hg // 2
                W('v', 'sT', ck_ss_mm)
                ckv = E('v', lambda e, p=kp: e.tensor_copy(g['khl'], p[:, :]),
                        inc=('sV', 1))
                khl2 = r3(g['khl'], 128)
                ck_tev = {}
                for lc in range(4):
                    pt = ppC if lc % 2 == 0 else ppD
                    W('t', 'sV', ckv)
                    W('t', 'sV', ck_tev.get(lc - 2, 0))
                    ckt = E('t', lambda e, p=pt, l=lc: e.transpose(
                        p[:, 0:128], khl2[:, l, :], sb_ident[:, :]), inc=('sT', 1))
                    W('v', 'sT', ckt)
                    E('v', lambda e, p=pt, l=lc, gg=grp: e.tensor_copy(
                        aKlh[:, l, gg * 128:(gg + 1) * 128], p[:, 0:128]))
                    ck_klh_v = E('v', lambda e, p=pt, l=lc, gg=grp: e.tensor_copy(
                        aKbf[:, l, gg * 128:(gg + 1) * 128], p[:, 0:128]),
                        inc=('sV', 1))
                    ck_tev[lc] = ck_klh_v
        ck_k_done = ck_klh_v

        if KCUT == 'p0c':
            W('g', 'sV', sch.cnt['sV'])
            W('g', 'sT', sch.cnt['sT'])
            ck_dummy = dma('g', 'dOut', d_out[0:128, :], sb_dump[:, :])
            WC('g', ck_dummy)
            return _finish(sch, blk, sems, nc)

        ck_kf_ev = {}
        for comp in range(2):
            dst = aKfRe if comp == 0 else aKfIm
            c32v, ccv = (aC32, aCc) if comp == 0 else (aS32, aSc)
            for fc in range(4):
                for hh in range(2):
                    idx = comp * 8 + fc * 2 + hh
                    pp = ppA if idx % 2 == 0 else ppB
                    W('t', 'sV', ck_k_done)
                    W('t', 'sV', ck_kf_ev.get(idx - 2, 0))
                    for lc in range(4):
                        E('t', lambda e, p=pp, l=lc, f=fc, h=hh, cv=c32v: e.matmul(
                            p[:, :], cv[:, l, f * 128:(f + 1) * 128],
                            aKlh[:, l, h * 512:(h + 1) * 512],
                            start=(l == 0), stop=False))
                    ckt = None
                    for lc in range(4):
                        ckt = E('t', lambda e, p=pp, l=lc, f=fc, h=hh, cv=ccv:
                                e.matmul(
                                    p[:, :], cv[:, l, f * 128:(f + 1) * 128],
                                    aKbf[:, l, h * 512:(h + 1) * 512],
                                    start=False, stop=(l == 3)),
                                inc=('sT', 1) if lc == 3 else None)
                    W('v', 'sT', ckt)
                    ck_kf_ev[idx] = E(
                        'v', lambda e, p=pp, f=fc, h=hh, d=dst: e.tensor_copy(
                            d[:, f, h * 512:(h + 1) * 512], p[:, :]), inc=('sV', 1))
        ck_last_kf = max(ck_kf_ev.values())
        W('v', 'sV', ck_last_kf)
        E('v', lambda e: e.tensor_copy(aKfX[:, 0, :], aKfIm[:, 0, :]))
        E('v', lambda e: e.memset(aKfX[0:1, 0, :], 0.0))
        E('v', lambda e: e.tensor_copy(aKfX[:, 1, :], aKfRe[:, 0, :]))
        E('v', lambda e: e.tensor_copy(aKfX[0:1, 1, :], aKfIm[0:1, 0, :]),
          inc=('sV', 1))
        ck_p0_v = sch.cnt['sV']
        ck_p0_t = sch.cnt['sT']
        if KDBG:
            W('g', 'sV', ck_p0_v)
            W('g', 'sT', ck_p0_t)
            WC('g', ck_oa_total)
            WC('g', ck_ob_total)
            ckw1t = dma('g', 'dKb', cvAb.rearrange("p (a b) -> p a b", b=128),
                        d_w1[1, :, :, :])
            WC('g', ckw1t)
            dma('g', 'dOut', d_dbg[3, :, :],
                cvAb.bitcast(F32)[:, 0:1024])
            dma('g', 'dOut', d_dbg[4, :, :],
                cvAb.bitcast(F32)[:, 1024:2048])
            dma('g', 'dOut', d_dbg[0, :, :], aKfRe[:, 0, :])
            dma('g', 'dOut', d_dbg[1, :, :], aKfIm[:, 0, :])
            dma('g', 'dOut', d_dbg[2, :, :], aKfX[:, 0, :])
            dma('g', 'dOut', d_dbg[5, :, 0:1024], sb_ab[:, :])
            ckdx = dma('g', 'dOut', d_dbg[6, :, 0:512], sb_wre[:, :])
            WC('t', ckdx)

        if KCUT == 'p0':
            W('g', 'sV', ck_p0_v)
            W('g', 'sT', ck_p0_t)
            ck_dummy = dma('g', 'dOut', d_out[0:128, :], sb_dump[:, :])
            WC('g', ck_dummy)
            return _finish(sch, blk, sems, nc)

        ck_uh = {}

        def load_uhalf(h):
            if h > NBLK:
                return
            if h < 2:
                W('g', 'sV', ck_p0_v)
                W('g', 'sT', ck_p0_t)
            else:
                W('g', 'sT', sch.cnt['sT'])
                W('g', 'sA', sch.cnt['sA'])
            ck_uh[h] = dma('g', 'dU0' if h % 2 == 0 else 'dU1', aU[h % 2],
                           d_u[h * 512:(h + 1) * 512, :].rearrange(
                               "(a p) c -> p a c", p=128))

        load_uhalf(0)
        load_uhalf(1)

        def ln_emit(tile, outtile):
            st = sb_stats
            ckv = E('v', lambda e, ti=tile: e.tensor_reduce(st[:, 0:1], ti, AXX,
                                                            OP.add), inc=('sV', 1))
            W('s', 'sV', ckv)
            cks = E('s', lambda e, ti=tile: e.activation(sb_dump[:, :], ti, AF.Square,
                                                         accum_out=st[:, 1:2]),
                    inc=('sA', 1))
            W('v', 'sA', cks)
            E('v', lambda e: e.tensor_scalar(st[:, 2:3], st[:, 0:1], R1024,
                                             None, OP.mult))
            E('v', lambda e: e.tensor_scalar(st[:, 3:4], st[:, 1:2], R1024,
                                             None, OP.mult))
            ckv = E('v', lambda e: e.tensor_scalar(st[:, 8:9], st[:, 0:1], -R1024,
                                                   None, OP.mult), inc=('sV', 1))
            W('s', 'sV', ckv)
            cks = E('s', lambda e: e.activation(st[:, 4:5], st[:, 2:3], AF.Square),
                    inc=('sA', 1))
            W('v', 'sA', cks)
            ckv = E('v', lambda e: e.tensor_sub(st[:, 5:6], st[:, 3:4], st[:, 4:5]),
                    inc=('sV', 1))
            W('s', 'sV', ckv)
            cks = E('s', lambda e: e.activation(st[:, 6:7], st[:, 5:6], AF.Sqrt,
                                                bias=EPS), inc=('sA', 1))
            W('v', 'sA', cks)
            ckv = E('v', lambda e: e.reciprocal(st[:, 7:8], st[:, 6:7]),
                    inc=('sV', 1))
            W('s', 'sV', ckv)
            cks = E('s', lambda e: e.activation(st[:, 9:10], st[:, 8:9], AF.Copy,
                                                scale=st[:, 7:8]), inc=('sA', 1))
            W('v', 'sA', cks)
            return E('v', lambda e, ti=tile, o=outtile: e.scalar_tensor_tensor(
                o, ti, st[:, 7:8], st[:, 9:10].broadcast_to([128, 1024]),
                OP.mult, OP.add), inc=('sV', 1))

        ck_fwd_last = 0
        ck_inv_ev = 0
        zbufs = (aZasm, sb_ab)
        spill_ck = {}
        pend_ln = None
        ztile = 0

        def flush_ln():
            nonlocal pend_ln
            ti, buf = pend_ln
            ckln = ln_emit(buf[:, :], buf[:, :])
            W('g', 'sV', ckln)
            spill_ck[ti] = dma('g', 'dSp',
                               d_x1[ti * 128:(ti + 1) * 128, :], buf[:, :])
            pend_ln = None

        for blkid in range(NBLK):
            hA, hB = blkid, blkid + 1
            uA, uB = aU[hA % 2], aU[hB % 2]

            for chh in range(2):
                def usl(tc, uA=uA, uB=uB, c=chh):
                    return (uA if tc < 4 else uB)[:, tc % 4, c * 512:(c + 1) * 512]

                WC('s', ck_uh[hA])
                WC('s', ck_uh[hB])
                W('s', 'sT', ck_fwd_last)
                ck_ubf = None
                for tc in range(8):
                    ck_ubf = E('s', lambda e, t=tc, us=usl: e.activation(
                        aUbf[:, t, :], us(t), AF.Copy),
                        inc=('sA', 1) if tc == 7 else None)

                WC('t', ck_uh[hA])
                WC('t', ck_uh[hB])
                W('t', 'sA', ck_ubf)
                W('t', 'sV', ck_inv_ev)
                ck_uf_ev = {}
                for comp in range(2):
                    c32v, ccv = (aC32, aCc) if comp == 0 else (aS32, aSc)
                    for fc in range(4):
                        gi = comp * 4 + fc
                        g1 = ppA if gi % 2 == 0 else ppC
                        g2 = ppB if gi % 2 == 0 else ppD
                        W('t', 'sV', ck_uf_ev.get(gi - 2, 0))
                        ckt = None
                        for tc in range(4):
                            for half in range(2):
                                pp = g1 if half == 0 else g2
                                E('t', lambda e, p=pp, t=tc, hf=half, f=fc, us=usl,
                                  cv=c32v: e.matmul(
                                    p[:, :], cv[:, t, f * 128:(f + 1) * 128],
                                    us(t + 4 * hf), start=(t == 0), stop=False))
                        for tc in range(4):
                            for half in range(2):
                                pp = g1 if half == 0 else g2
                                ckt = E('t', lambda e, p=pp, t=tc, hf=half, f=fc,
                                        cv=ccv: e.matmul(
                                            p[:, :], cv[:, t, f * 128:(f + 1) * 128],
                                            aUbf[:, t + 4 * hf, :],
                                            start=False, stop=(t == 3)),
                                        inc=('sT', 1) if (half == 1 and tc == 3)
                                        else None)
                        W('v', 'sT', ckt)
                        E('v', lambda e, p=g2: e.tensor_copy(aScr[:, 0, :], p[:, :]))
                        ck_uf_ev[gi] = E(
                            'v', lambda e, p=g1, sl=gi: e.scalar_tensor_tensor(
                                aUf[:, sl, :], aScr[:, 0, :], sb_altcol[:, 0:1],
                                p[:, :], OP.mult, OP.add), inc=('sV', 1))
                ck_fwd_last = sch.cnt['sT']

                ck_yf = None
                for fc in range(4):
                    W('v', 'sV', ck_uf_ev[fc])
                    W('v', 'sV', ck_uf_ev[4 + fc])
                    sl = slice(chh * 512, (chh + 1) * 512)
                    kre = aKfRe[:, fc, sl]
                    kim = aKfX[:, 0, sl] if fc == 0 else aKfIm[:, fc, sl]
                    kreI = aKfX[:, 1, sl] if fc == 0 else aKfRe[:, fc, sl]
                    ure = aUf[:, fc, :]
                    uim = aUf[:, 4 + fc, :]
                    E('v', lambda e, k=kreI, u=uim: e.tensor_tensor(
                        aScr[:, 0, :], k, u, OP.mult))
                    E('v', lambda e, k=kim, u=ure: e.tensor_tensor(
                        aScr[:, 1, :], k, u, OP.mult))
                    E('v', lambda e, k=kre, u=ure: e.tensor_tensor(
                        aScr[:, 2, :], k, u, OP.mult))
                    E('v', lambda e, k=kim, u=uim: e.tensor_tensor(
                        aScr[:, 3, :], k, u, OP.mult))
                    E('v', lambda e, o=ure: e.tensor_sub(o, aScr[:, 2, :],
                                                         aScr[:, 3, :]))
                    E('v', lambda e, o=uim: e.tensor_add(o, aScr[:, 0, :],
                                                         aScr[:, 1, :]))
                    E('v', lambda e, o=ure, f=fc: e.tensor_copy(aYbf[:, f, :], o))
                    ck_yf = E('v', lambda e, o=uim, f=fc: e.tensor_copy(
                        aYbf[:, 4 + f, :], o), inc=('sV', 1))

                for tb in range(4):
                    pp = ppA if tb % 2 == 0 else ppB
                    W('t', 'sV', ck_yf)
                    W('t', 'sV', ck_inv_ev)
                    for comp in range(2):
                        civ = aCi32 if comp == 0 else aSi32
                        for fc in range(4):
                            E('t', lambda e, p=pp, f=fc, t=tb, cv=civ, cp=comp:
                              e.matmul(
                                p[:, :], cv[:, f, t * 128:(t + 1) * 128],
                                aUf[:, cp * 4 + f, :], start=(cp == 0 and f == 0),
                                stop=False))
                    ckt = None
                    for comp in range(2):
                        cicv = aCic if comp == 0 else aSic
                        for fc in range(4):
                            islast = (comp == 1 and fc == 3)
                            ckt = E('t', lambda e, p=pp, f=fc, t=tb, cv=cicv, cp=comp,
                                    lst=islast: e.matmul(
                                        p[:, :], cv[:, f, t * 128:(t + 1) * 128],
                                        aYbf[:, cp * 4 + f, :], start=False, stop=lst),
                                    inc=('sT', 1) if islast else None)
                    W('v', 'sT', ckt)
                    E('v', lambda e, t=tb, c=chh, u=uB: e.tensor_tensor(
                        aScr[:, 0, :], u[:, t, c * 512:(c + 1) * 512],
                        sb_dp1[:, c * 512:(c + 1) * 512], OP.mult))
                    if chh == 0:
                        ck_inv_ev = E('v', lambda e, p=pp, t=tb: e.tensor_add(
                            aZX[:, t, :], aScr[:, 0, :], p[:, :]), inc=('sV', 1))
                    else:
                        buf = zbufs[ztile % 2]
                        WC('v', spill_ck.get(ztile - 2))
                        E('v', lambda e, t=tb, b=buf: e.tensor_copy(
                            b[:, 0:512], aZX[:, t, :]))
                        ck_inv_ev = E('v', lambda e, p=pp, b=buf: e.tensor_add(
                            b[:, 512:1024], aScr[:, 0, :], p[:, :]),
                            inc=('sV', 1))
                        if pend_ln is not None:
                            flush_ln()
                        pend_ln = (ztile, buf)
                        ztile += 1
            load_uhalf(blkid + 2)
        flush_ln()

        if KCUT == 'p1':
            W('g', 'dSp', sch.cnt['dSp'])
            ck_dummy = dma('g', 'dOut', d_out[0:128, :], sb_dump[:, :])
            WC('g', ck_dummy)
            return _finish(sch, blk, sems, nc)

        ck_bar = {s: sch.cnt[s] for s in DMA_SEMS + ENG_SEMS}
        for eng in ('t', 'v', 's', 'y', 'g'):
            for sname, val in ck_bar.items():
                W(eng, sname, val)

        W0SEM = ('dW0a', 'dW0b')
        W1SEM = ('dW1a', 'dW1b')
        W2SEM = ('dW2a', 'dW2b', 'dW2c', 'dW2d')
        ck_w0_use = {}
        ck_w1_use = {}
        ck_w2_use = {}
        ck_w2b_use = {}
        ck_z2b = 0
        ck_out = {}
        def load_w0(p, oc, ck_w0_load, gate):
            W('y', 'sT', gate)
            ck_w0_load[oc] = dma('y', W0SEM[oc % 2], aW0[:, oc % 2, :, :],
                                 d_w0[oc, :, :, :])

        import os as _os2
        W1Q = _os2.environ.get('KW1Q', 'y')
        assert W1Q in ('y', 'g', 's')
        W1R1 = _os2.environ.get('KW1R1') == '1'

        def load_w1(p, oc, ck_w1_load, gate):
            W(W1Q, 'sT', gate)
            slot = 0 if W1R1 else oc % 2
            ck_w1_load[oc] = dma(
                W1Q, W1SEM[oc % 2],
                aW1_f32[:, slot * 2048:(slot + 1) * 2048],
                d_w1[oc, :, :, :].rearrange("q a b -> q (a b)").bitcast(F32))

        def load_w2(p, oc, ck_w2_load, gate):
            W('y', 'sT', gate)
            ck_w2_load[oc] = dma('y', W2SEM[oc % 4], aW2[:, oc % 4, :],
                                 d_w2[oc, :, :])

        for p in range(NPAIR):
            if p > 0:
                W('y', 'sV', ck_z2b)
                WC('y', ck_out[(p - 1, 1)])
            ck_x1_load = dma('y', 'dX', aX1tm,
                             d_x1[p * 512:(p + 1) * 512, :].rearrange(
                                 "(a q) c -> q a c", q=128))
            ck_w0_load = {}
            ck_w1_load = {}
            ck_w2_load = {}
            ck_w2b_load = {}
            for oc in range(2):
                load_w0(p, oc, ck_w0_load, ck_w0_use.get((p - 1, 30 + oc), 0))

            WC('t', ck_x1_load)
            ck_x1t = {}
            ck_tr = {}
            ck_tr_last = 0
            for tc in range(4):
                for fc in range(8):
                    i = tc * 8 + fc
                    pt = ppC if i % 2 == 0 else ppD
                    W('t', 'sV', ck_tr.get(i - 2, 0))
                    ckt = E('t', lambda e, q=pt, t=tc, f=fc: e.transpose(
                        q[:, 0:128], aX1tm[:, t, f * 128:(f + 1) * 128],
                        sb_ident[:, :]), inc=('sT', 1))
                    W('s', 'sT', ckt)
                    ck_x1t[i] = E('s', lambda e, q=pt, t=tc, f=fc: e.activation(
                        aX1hi[:, f, t * 128:(t + 1) * 128], q[:, 0:128], AF.Copy),
                        inc=('sA', 1))
                    W('v', 'sA', ck_x1t[i])
                    ck_tr[i] = E('v', lambda e, q=pt, t=tc, f=fc: e.tensor_sub(
                        aX1lo[:, f, t * 128:(t + 1) * 128], q[:, 0:128],
                        aX1hi[:, f, t * 128:(t + 1) * 128]), inc=('sV', 1))
                    ck_tr_last = ck_tr[i]

            ck_h1_ev = {}
            for oc in range(32):
                pp = ppA if oc % 2 == 0 else ppB
                W('t', 'sV', ck_h1_ev.get(oc - 2, 0))
                WC('t', ck_w0_load[oc])
                W('t', 'sV', ck_tr_last)
                ckt = None
                for ic in range(8):
                    E('t', lambda e, q=pp, o=oc, i=ic: e.matmul(
                        q[:, :], aW0[:, o % 2, i, :], aX1hi[:, i, :],
                        start=(i == 0), stop=False))
                    ckt = E('t', lambda e, q=pp, o=oc, i=ic: e.matmul(
                        q[:, :], aW0[:, o % 2, i, :], aX1lo[:, i, :],
                        start=False, stop=(i == 7)),
                        inc=('sT', 1) if ic == 7 else None)
                ck_w0_use[(p, oc)] = ckt
                if oc + 2 <= 31:
                    load_w0(p, oc + 2, ck_w0_load, ckt)
                elif oc == 30:
                    if W1R1:
                        load_w1(p, 0, ck_w1_load, ck_w1_use.get((p - 1, 31), 0))
                    else:
                        for o2 in range(2):
                            load_w1(p, o2, ck_w1_load,
                                    ck_w1_use.get((p - 1, 30 + o2), 0))
                    for o2 in range(4):
                        load_w2(p, o2, ck_w2_load,
                                ck_w2b_use.get((p - 1, 28 + o2), 0))
                W('s', 'sT', ckt)
                W('s', 'sV', ck_h1_ev.get(oc - 1, 0))
                cka = E('s', lambda e, q=pp: e.activation(aG1f[:, :], q[:, :],
                                                          AF.Relu), inc=('sA', 1))
                W('v', 'sA', cka)
                E('v', lambda e, o=oc: e.tensor_copy(aH1hi[:, o, :], aG1f[:, :]))
                ck_h1_ev[oc] = E('v', lambda e, o=oc: e.tensor_sub(
                    aH1lo[:, o, :], aG1f[:, :], aH1hi[:, o, :]), inc=('sV', 1))
            ck_h1_all = ck_h1_ev[31]
            if KDBG and p == 0:
                W('g', 'sV', ck_h1_all)
                h1flat = aH1hi.rearrange("q a b -> q (a b)").bitcast(F32)
                for di in range(4):
                    dma('g', 'dOut', d_dbg[11 + di, :, :],
                        h1flat[:, di * 1024:(di + 1) * 1024])
                x1flat = aX1hi.rearrange("q a b -> q (a b)").bitcast(F32)
                ckdx = dma('g', 'dOut', d_dbg[15, :, 0:1024], x1flat[:, 0:1024])
                WC('t', ckdx)

            ck_h2a_ev = {}
            ck_h2b_ev = {}
            ck_mm2a = {}

            def emit_mm2a(oc):
                sl = oc % 2
                W('t', 'sV', ck_h2a_ev[oc])
                WC('t', ck_w2_load[oc])
                if oc == 0:
                    W('t', 'sV', ck_z2b)
                ckt = None
                for hilo in range(2):
                    src = aH2Ahi if hilo == 0 else aH2Alo
                    for tb in range(2):
                        for chh in range(2):
                            ckt = E('t', lambda e, s2=sl, hl=hilo, t=tb, c=chh, o=oc,
                                    sr=src: e.matmul(
                                        pM[t * 2 + c][:, :],
                                        sr[:, s2, t * 128:(t + 1) * 128],
                                        aW2[:, o % 4, c * 512:(c + 1) * 512],
                                        start=(o == 0 and hl == 0),
                                        stop=(o == 31 and hl == 1)),
                                    inc=('sT', 1) if (hilo == 1 and tb == 1 and
                                                      chh == 1) else None)
                ck_mm2a[oc] = ckt
                ck_w2_use[(p, oc)] = ckt
                if oc + 4 <= 31:
                    load_w2(p, oc + 4, ck_w2_load, ckt)
                elif oc == 31:
                    for o2 in range(4):
                        W('y', 'sT', ck_w2_use[(p, 28 + o2)])
                        ck_w2b_load[o2] = dma('y', W2SEM[o2 % 4], aW2[:, o2 % 4, :],
                                              d_w2[o2, :, :])

            for oc in range(32):
                pp = ppA if oc % 2 == 0 else ppB
                sl = oc % 2
                W('t', 'sV', ck_h2a_ev.get(oc - 2, 0))
                W('t', 'sV', ck_h1_all)
                WC('t', ck_w1_load[oc])
                ckt = None
                w1slot = 0 if W1R1 else oc % 2
                for ic in range(32):
                    E('t', lambda e, q=pp, s2=w1slot, i=ic: e.matmul(
                        q[:, :], aW1[:, s2, i, :], aH1hi[:, i, :],
                        start=(i == 0), stop=False))
                    ckt = E('t', lambda e, q=pp, s2=w1slot, i=ic: e.matmul(
                        q[:, :], aW1[:, s2, i, :], aH1lo[:, i, :],
                        start=False, stop=(i == 31)),
                        inc=('sT', 1) if ic == 31 else None)
                ck_w1_use[(p, oc)] = ckt
                if W1R1 and oc + 1 <= 31:
                    load_w1(p, oc + 1, ck_w1_load, ckt)
                elif (not W1R1) and oc + 2 <= 31:
                    load_w1(p, oc + 2, ck_w1_load, ckt)
                W('s', 'sT', ckt)
                W('s', 'sV', ck_h2b_ev.get(oc - 1, 0))
                cka = E('s', lambda e, q=pp: e.activation(aH2f[:, :], q[:, :],
                                                          AF.Relu), inc=('sA', 1))
                W('v', 'sA', cka)
                W('v', 'sT', ck_mm2a.get(oc - 2, 0))
                E('v', lambda e, s2=sl: e.tensor_copy(aH2Ahi[:, s2, :],
                                                      aH2f[:, 0:256]))
                ck_h2a_ev[oc] = E('v', lambda e, s2=sl: e.tensor_sub(
                    aH2Alo[:, s2, :], aH2f[:, 0:256], aH2Ahi[:, s2, :]),
                    inc=('sV', 1))
                E('v', lambda e, o=oc: e.tensor_copy(aH2Bhi[:, o, :],
                                                     aH2f[:, 256:512]))
                ck_h2b_ev[oc] = E('v', lambda e, o=oc: e.tensor_sub(
                    aH2Blo[:, o, :], aH2f[:, 256:512], aH2Bhi[:, o, :]),
                    inc=('sV', 1))
                if oc >= 1:
                    emit_mm2a(oc - 1)
            emit_mm2a(31)

            ck_z2a = 0
            ck_z2a_stt = 0
            for tb in range(2):
                W('v', 'sT', ck_mm2a[31])
                if p > 0:
                    WC('v', ck_out.get((p - 1, 1)))
                ck_stt = None
                for chh in range(2):
                    ck_stt = E('v', lambda e, t=tb, c=chh: e.scalar_tensor_tensor(
                        aZ2[:, c * 512:(c + 1) * 512], pM[t * 2 + c][:, :],
                        sb_alpha[:, 0:1], aX1tm[:, t, c * 512:(c + 1) * 512],
                        OP.mult, OP.add),
                        inc=('sV', 1) if (chh == 1 and
                                          (tb == 1 or (KDBG and p == 0)))
                        else None)
                if tb == 1:
                    ck_z2a_stt = ck_stt
                if KDBG and p == 0 and tb == 0:
                    W('s', 'sV', ck_stt)
                    E('s', lambda e: e.activation(sb_dump[:, 0:512],
                                                  pM[0][:, :], AF.Copy))
                    ckps = E('s', lambda e: e.activation(sb_dump[:, 512:1024],
                                                         pM[1][:, :], AF.Copy),
                             inc=('sA', 1))
                    W('g', 'sA', ckps)
                    dma('g', 'dOut', d_dbg[7, :, :], aZ2[:, :])
                    dma('g', 'dOut', d_dbg[8, :, 0:512], sb_dump[:, 0:512])
                    E('g', lambda e: e.dma_start(
                        out=d_dbg[8, :, 512:513], in_=sb_alpha[:, 0:1],
                        allow_slow_non_contiguous=True), inc=('dOut', 16))
                    dma('g', 'dOut', d_dbg[9, :, 0:512], sb_dump[:, 512:1024])
                    ckdz = dma('g', 'dOut', d_dbg[10, :, :], aX1tm[:, 0, :])
                    WC('v', ckdz)
                    WC('t', ckdz)
                    WC('s', ckdz)
                ck_z2a = ln_emit(aZ2[:, :], aX1tm[:, tb, :])
            W('g', 'sV', ck_z2a)
            ck_out[(p, 0)] = dma('g', 'dOut',
                                 d_out[p * 512:p * 512 + 256, :].rearrange(
                                     "(a q) c -> q a c", q=128), aX1tm[:, 0:2, :])

            ck_mm2b = 0
            for oc in range(32):
                W('t', 'sV', ck_z2a_stt)
                WC('t', ck_w2b_load[oc])
                W('t', 'sV', ck_h2b_ev[31])
                ckt = None
                for hilo in range(2):
                    src = aH2Bhi if hilo == 0 else aH2Blo
                    for tb in range(2):
                        for chh in range(2):
                            ckt = E('t', lambda e, hl=hilo, t=tb, c=chh, o=oc,
                                    sr=src: e.matmul(
                                        pM[t * 2 + c][:, :],
                                        sr[:, o, t * 128:(t + 1) * 128],
                                        aW2[:, o % 4, c * 512:(c + 1) * 512],
                                        start=(o == 0 and hl == 0),
                                        stop=(o == 31 and hl == 1)),
                                    inc=('sT', 1) if (hilo == 1 and tb == 1 and
                                                      chh == 1) else None)
                ck_mm2b = ckt
                ck_w2b_use[(p, oc)] = ckt
                if oc + 4 <= 31:
                    W('y', 'sT', ckt)
                    ck_w2b_load[oc + 4] = dma('y', W2SEM[(oc + 4) % 4],
                                              aW2[:, (oc + 4) % 4, :],
                                              d_w2[oc + 4, :, :])
            for tb in range(2):
                W('v', 'sT', ck_mm2b)
                WC('v', ck_out[(p, 0)])
                for chh in range(2):
                    E('v', lambda e, t=tb, c=chh: e.scalar_tensor_tensor(
                        aZ2[:, c * 512:(c + 1) * 512], pM[t * 2 + c][:, :],
                        sb_alpha[:, 0:1], aX1tm[:, 2 + t, c * 512:(c + 1) * 512],
                        OP.mult, OP.add))
                ck_z2b = ln_emit(aZ2[:, :], aX1tm[:, 2 + tb, :])
            W('g', 'sV', ck_z2b)
            ck_out[(p, 1)] = dma('g', 'dOut',
                                 d_out[p * 512 + 256:(p + 1) * 512, :].rearrange(
                                     "(a q) c -> q a c", q=128), aX1tm[:, 2:4, :])

        W('g', 'dOut', sch.cnt['dOut'])
        return _finish(sch, blk, sems, nc)


def _finish(sch, blk, sems, nc):
    @blk.tensor
    def _(eng):
        sch.run('t', eng, sems)

    @blk.vector
    def _(eng):
        sch.run('v', eng, sems)

    @blk.scalar
    def _(eng):
        sch.run('s', eng, sems)

    @blk.sync
    def _(eng):
        sch.run('y', eng, sems)

    @blk.gpsimd
    def _(eng):
        sch.run('g', eng, sems)

    return nc


def _prep_inputs(inputs):
    u = np.asarray(inputs['L0_attn_u'])
    D = np.ascontiguousarray(np.asarray(inputs['L0_attn_D'], dtype=np.float32))
    lam_re = np.asarray(inputs['L0_attn_Lambda_re'], dtype=np.float32)
    lam_im = np.asarray(inputs['L0_attn_Lambda_im'], dtype=np.float32)
    log_dt = np.asarray(inputs['L0_attn_log_dt'], dtype=np.float32)
    W_re = np.asarray(inputs['L0_attn_W_re'], dtype=np.float32)
    W_im = np.asarray(inputs['L0_attn_W_im'], dtype=np.float32)
    k0 = np.asarray(inputs['L0_ffn_k0'])
    k1 = np.asarray(inputs['L0_ffn_k1'])
    k2 = np.asarray(inputs['L0_ffn_k2'])
    s0 = np.float32(np.asarray(inputs['L0_ffn_s0']))
    s1 = np.float32(np.asarray(inputs['L0_ffn_s1']))
    s2 = np.float32(np.asarray(inputs['L0_ffn_s2']))

    logdt_arr = np.empty((128, 512), np.float32)
    ld = log_dt.reshape(512, 2)
    logdt_arr[0::2, :] = ld[:, 0][None, :]
    logdt_arr[1::2, :] = ld[:, 1][None, :]
    lamre_col = np.ascontiguousarray(np.repeat(lam_re, 2)[:, None])
    lamim_col = np.ascontiguousarray(np.repeat(lam_im, 2)[:, None])
    wre_arr = np.empty((128, 512), np.float32)
    wim_arr = np.empty((128, 512), np.float32)
    wr = W_re.reshape(512, 2, 64)
    wi = W_im.reshape(512, 2, 64)
    for par in range(2):
        wre_arr[par::2, :] = wr[:, par, :].T
        wim_arr[par::2, :] = wi[:, par, :].T
    k0t = np.ascontiguousarray(k0[:1024]).view(np.uint8)
    k0b = np.ascontiguousarray(k0[1024:]).view(np.uint8)
    k1u = np.ascontiguousarray(k1).view(np.uint8)
    k2u = np.ascontiguousarray(k2).view(np.uint8)
    svec = np.broadcast_to(np.array([s0, s1, s2], np.float32), (128, 3)).copy()
    D_rep = np.broadcast_to(D, (128, H)).copy()

    consts = _host_consts()
    shared = dict(logdt_arr=logdt_arr, lamre_col=lamre_col, lamim_col=lamim_col,
                  wre_arr=wre_arr, wim_arr=wim_arr, k0t=k0t, k0b=k0b, k1=k1u,
                  k2=k2u, svec=svec, D_rep=D_rep)
    for nm, arr in consts.items():
        shared['c_' + nm] = np.ascontiguousarray(arr)

    in_maps = []
    for c in range(8):
        b, half = c // 2, c % 2
        tok0 = half * 4096
        if half == 0:
            halo = np.zeros((512, H), np.float32)
        else:
            halo = u[b, tok0 - 512:tok0]
        shard = np.concatenate([halo, u[b, tok0:tok0 + 4096]], axis=0)
        m = dict(shared)
        m['u_shard'] = np.ascontiguousarray(shard, dtype=np.float32)
        in_maps.append(m)
    return in_maps


_NC_CACHE = None


def kernel(**inputs):
    global _NC_CACHE
    if _NC_CACHE is None:
        _NC_CACHE = build_nc()
    nc = _NC_CACHE
    in_maps = _prep_inputs(inputs)
    res = run_bass_kernel_spmd(nc, in_maps, core_ids=list(range(8)))
    outs = [np.asarray(res.results[c]['out']) for c in range(8)]
    full = np.stack([np.concatenate([outs[2 * b], outs[2 * b + 1]], axis=0)
                     for b in range(4)], axis=0)
    return full.astype(np.float32)


# version 11
# speedup vs baseline: 1.1645x; 1.0122x over previous
from contextlib import ExitStack

import numpy as np
import ml_dtypes

import concourse.bass as bass
import concourse.mybir as mybir
from concourse.bass_utils import run_bass_kernel_spmd

F32 = mybir.dt.float32
BF16 = mybir.dt.bfloat16
I32 = mybir.dt.int32
U8 = mybir.dt.uint8
AF = mybir.ActivationFunctionType
OP = mybir.AluOpType
AXX = mybir.AxisListType.X

bf16 = ml_dtypes.bfloat16

H = 1024
TOK = 4096
NBLK = 8
NPAIR = 8
EPS = 1e-5
R1024 = float(np.float32(1.0 / 1024.0))

CHI = float(np.float32(2 * np.pi))
CLO = float(np.float32(2 * np.pi - np.float64(np.float32(2 * np.pi))))
I2HI = float(np.float32(1.0 / (2 * np.pi)))
I2LO = float(np.float32(1.0 / (2 * np.pi) - np.float64(np.float32(1.0 / (2 * np.pi)))))
PHHI = float(np.float32(np.pi / 2))
PHLO = float(np.float32(np.pi / 2 - np.float64(np.float32(np.pi / 2))))
PI_SAFE = float(np.nextafter(np.float32(np.pi), np.float32(0)))

DMA_SEMS = ('dC', 'dKa', 'dKb', 'dOa', 'dOb', 'dU0', 'dU1', 'dSp', 'dX',
            'dW0a', 'dW0b', 'dW1a', 'dW1b', 'dW2a', 'dW2b', 'dW2c', 'dW2d',
            'dOut')
ENG_SEMS = ('sT', 'sV', 'sA')


def _split_const(M64):
    m32 = M64.astype(np.float32)
    corr = (M64 - m32.astype(np.float64)).astype(bf16)
    return m32, corr


def _host_consts():
    t = np.arange(512, dtype=np.float64)
    f = np.arange(512, dtype=np.float64)
    ang = 2 * np.pi * np.outer(t, f) / 1024.0
    C32, Ccorr = _split_const(np.cos(ang))
    S32, Scorr = _split_const(-np.sin(ang))
    t2 = np.arange(512, dtype=np.float64)
    wf = np.where(f == 0, 1.0, 2.0)
    angi = 2 * np.pi * np.outer(f, 512 + t2) / 1024.0
    Ci32, Cicorr = _split_const(wf[:, None] * np.cos(angi) / 1024.0)
    Si32, Sicorr = _split_const(-wf[:, None] * np.sin(angi) / 1024.0)
    altcol = ((-1.0) ** np.arange(128, dtype=np.float64)).astype(np.float32)[:, None]
    S32[:, 0] = ((-1.0) ** t).astype(np.float32)
    Scorr[:, 0] = 0
    Si32[0, :] = (((-1.0) ** t2) / 1024.0).astype(np.float32)
    Sicorr[0, :] = 0
    iota32 = np.broadcast_to(np.arange(32, dtype=np.float32), (128, 32)).copy()
    iota16 = np.broadcast_to(np.arange(16, dtype=np.float32) * 32.0, (128, 16)).copy()
    ident = np.eye(128, dtype=np.float32)
    par = (np.arange(128) & 1)
    Mplus = np.zeros((128, 254), np.float32)
    Mplus[:, 126] = (par == 0)
    Mplus[:, 127] = (par == 1)
    Mminus = -Mplus
    cpool = np.broadcast_to(np.array([-1.0, 1.0, 2.0, EPS], np.float32),
                            (128, 4)).copy()
    return dict(C32=C32, Ccorr=Ccorr, S32=S32, Scorr=Scorr,
                Ci32=Ci32, Cicorr=Cicorr, Si32=Si32, Sicorr=Sicorr,
                altcol=altcol, iota32=iota32, iota16=iota16,
                ident=ident, Mplus=Mplus, Mminus=Mminus, cpool=cpool)


class Sched:
    ENGS = ('t', 'v', 's', 'y', 'g')

    def __init__(self):
        self.ops = {e: [] for e in self.ENGS}
        self.cnt = {s: 0 for s in DMA_SEMS + ENG_SEMS}
        self.lastwait = {}

    def wait(self, eng, sem, val):
        if val is None or val <= 0:
            return
        key = (eng, sem)
        if self.lastwait.get(key, -1) >= val:
            return
        self.lastwait[key] = val
        self.ops[eng].append(('w', sem, val))

    def waitc(self, eng, ck):
        if ck:
            self.wait(eng, ck[0], ck[1])

    def emit(self, eng, fn, inc=None):
        if inc is not None:
            sname, k = inc
            self.cnt[sname] += k
            post = self.cnt[sname]
            self.ops[eng].append(('i', fn, sname, k))
            return post
        self.ops[eng].append(('e', fn))
        return None

    def run(self, eng, proxy, sems):
        for op in self.ops[eng]:
            if op[0] == 'w':
                proxy.wait_ge(sems[op[1]], op[2])
            elif op[0] == 'e':
                op[1](proxy)
            else:
                op[1](proxy).then_inc(sems[op[2]], op[3])


class Arena:
    def __init__(self, handle, nf32):
        self.h = handle
        self.n = nf32
        self.off = 0

    @staticmethod
    def _shape(ap, shape):
        if len(shape) == 3:
            return ap.rearrange("p (a b) -> p a b", b=shape[2])
        if len(shape) == 4:
            return ap.rearrange("p (a b c) -> p a b c", b=shape[2], c=shape[3])
        return ap

    def f32(self, shape):
        n = int(np.prod(shape[1:]))
        ap = self.h[0:shape[0], self.off:self.off + n]
        self.off += n
        assert self.off <= self.n, f"arena overflow {self.off} > {self.n}"
        return self._shape(ap, shape)

    def bf(self, shape):
        n = int(np.prod(shape[1:]))
        slots = (n + 1) // 2
        ap = self.h[0:shape[0], self.off:self.off + slots].bitcast(BF16)
        self.off += slots
        assert self.off <= self.n, f"arena overflow {self.off} > {self.n}"
        return self._shape(ap, shape)


ARENA_N = 44500


def build_nc():
    nc = bass.Bass("TRN2", target_bir_lowering=False,
                   detect_race_conditions=False)

    d_u = nc.dram_tensor("u_shard", [TOK + 512, H], F32, kind="ExternalInput").ap()
    d_logdt = nc.dram_tensor("logdt_arr", [128, 512], F32, kind="ExternalInput").ap()
    d_lamre = nc.dram_tensor("lamre_col", [128, 1], F32, kind="ExternalInput").ap()
    d_lamim = nc.dram_tensor("lamim_col", [128, 1], F32, kind="ExternalInput").ap()
    d_wre = nc.dram_tensor("wre_arr", [128, 512], F32, kind="ExternalInput").ap()
    d_wim = nc.dram_tensor("wim_arr", [128, 512], F32, kind="ExternalInput").ap()
    d_k0t = nc.dram_tensor("k0t", [1024, 4096], U8, kind="ExternalInput").ap()
    d_k0b = nc.dram_tensor("k0b", [1024, 4096], U8, kind="ExternalInput").ap()
    d_k1 = nc.dram_tensor("k1", [4096, 4096], U8, kind="ExternalInput").ap()
    d_k2 = nc.dram_tensor("k2", [4096, 1024], U8, kind="ExternalInput").ap()
    d_svec = nc.dram_tensor("svec", [128, 3], F32, kind="ExternalInput").ap()
    d_Drep = nc.dram_tensor("D_rep", [128, H], F32, kind="ExternalInput").ap()
    CONSTS = _host_consts()
    dc = {}
    for name, arr in CONSTS.items():
        dt = BF16 if arr.dtype == bf16 else F32
        dc[name] = nc.dram_tensor("c_" + name, list(arr.shape), dt,
                                  kind="ExternalInput").ap()
    d_out = nc.dram_tensor("out", [TOK, H], F32, kind="ExternalOutput").ap()
    d_dbg = None
    d_x1 = nc.dram_tensor("x1scr", [TOK, H], F32, kind="Internal").ap()
    d_w0 = nc.dram_tensor("w0scr", [32, 128, 8, 128], BF16, kind="Internal").ap()
    d_w1 = nc.dram_tensor("w1scr", [32, 128, 32, 128], BF16, kind="Internal").ap()
    d_w2 = nc.dram_tensor("w2scr", [32, 128, 1024], BF16, kind="Internal").ap()

    import os
    KCUT = os.environ.get('KCUT', '')
    KDBG = os.environ.get('KDBG') == '1'

    sch = Sched()
    import os as _os
    if _os.environ.get('KDBG') == '1':
        d_dbg = nc.dram_tensor("dbg", [16, 128, 1024], F32,
                               kind="ExternalOutput").ap()

    with ExitStack() as _st:
        def sb(name, shape, dtype):
            return _st.enter_context(nc.sbuf_tensor(name, shape, dtype))

        sb_cpool = sb("cn_cpool", [128, 4], F32)
        nc.const_aps.register(F32, -1.0, sb_cpool[:, 0:1])
        nc.const_aps.register(F32, 1.0, sb_cpool[:, 1:2])
        nc.const_aps.register(F32, 2.0, sb_cpool[:, 2:3])
        nc.const_aps.register(F32, EPS, sb_cpool[:, 3:4])
        sb_altcol = sb("cn_altcol", [128, 1], F32)
        sb_iota32 = sb("cn_iota32", [128, 32], F32)
        sb_iota16 = sb("cn_iota16", [128, 16], F32)
        sb_ident = sb("cn_ident", [128, 128], F32)
        sb_mplus = sb("cn_mplus", [128, 254], F32)
        sb_mminus = sb("cn_mminus", [128, 254], F32)
        sb_logdt = sb("in_logdt", [128, 512], F32)
        sb_lamre = sb("in_lamre", [128, 1], F32)
        sb_lamim = sb("in_lamim", [128, 1], F32)
        sb_wre = sb("in_wre", [128, 512], F32)
        sb_wim = sb("in_wim", [128, 512], F32)
        sb_svec = sb("in_svec", [128, 3], F32)
        sb_drep = sb("in_drep", [128, H], F32)
        sb_ab = sb("t_ab", [128, 1024], F32)
        sb_alpha = sb("t_alpha", [128, 1], F32)
        sb_dp1 = sb("t_dp1", [128, H], F32)
        sb_dump = sb("t_dump", [128, 1024], F32)
        sb_stats = sb("t_stats", [128, 16], F32)
        sb_mi = sb("t_mi", [128, 1024], I32)
        sb_arena = sb("arena", [128, ARENA_N], F32)
        ppA = _st.enter_context(nc.psum_tensor("ppA", [128, 512], F32))
        ppB = _st.enter_context(nc.psum_tensor("ppB", [128, 512], F32))
        ppC = _st.enter_context(nc.psum_tensor("ppC", [128, 512], F32))
        ppD = _st.enter_context(nc.psum_tensor("ppD", [128, 512], F32))
        pM = [_st.enter_context(nc.psum_tensor(f"pM{i}", [128, 512], F32))
              for i in range(4)]
        pM0 = pM[0]
        sems = {}
        for s in DMA_SEMS + ENG_SEMS:
            sems[s] = _st.enter_context(nc.semaphore(s))
        blk = _st.enter_context(nc.Block())

        ar = Arena(sb_arena, ARENA_N)

        aC32 = ar.f32([128, 4, 512])
        aS32 = ar.f32([128, 4, 512])
        aCc = ar.bf([128, 4, 512])
        aSc = ar.bf([128, 4, 512])
        aCi32 = ar.f32([128, 4, 512])
        aSi32 = ar.f32([128, 4, 512])
        aCic = ar.bf([128, 4, 512])
        aSic = ar.bf([128, 4, 512])
        aKfRe = ar.f32([128, 4, 1024])
        aKfIm = ar.f32([128, 4, 1024])
        aKfX = ar.f32([128, 2, 1024])
        p1_base = ar.off

        aU = [ar.f32([128, 4, 1024]) for _ in range(2)]
        aUbf = ar.bf([128, 8, 512])
        aUf = ar.f32([128, 8, 512])
        aYbf = ar.bf([128, 8, 512])
        aZX = ar.f32([128, 4, 512])
        aZasm = ar.f32([128, 1024])
        aScr = ar.f32([128, 4, 512])
        p1_end = ar.off

        ar.off = p1_base
        cvA = ar.f32([128, 1024]).bitcast(U8)
        cvB = ar.f32([128, 1024]).bitcast(U8)
        cvAb = ar.f32([128, 2048]).bitcast(BF16)
        cvBb = ar.f32([128, 2048]).bitcast(BF16)

        ar.off = p1_base
        g = {}
        for nm, w in (('ApRe', 1024), ('ApIm', 1024), ('EA', 1024), ('snA', 1024),
                      ('csA', 1024), ('tA', 1024), ('tB', 1024), ('argI', 1024),
                      ('BRe', 512), ('BIm', 512), ('EB', 512), ('snB', 512),
                      ('csB', 512), ('SS1', 512), ('SS2', 512), ('khl', 512),
                      ('SS1b', 512), ('SS2b', 512)):
            g[nm] = ar.f32([128, w])
        aKlh = ar.f32([128, 4, 1024])
        aKbf = ar.bf([128, 4, 1024])
        assert ar.off <= p1_end, (ar.off, p1_end)
        ar.off = p1_end

        ar.off = 0
        aH1hi = ar.bf([128, 32, 512])
        aH1lo = ar.bf([128, 32, 512])
        aH2Bhi = ar.bf([128, 32, 256])
        aH2Blo = ar.bf([128, 32, 256])
        aH2f = ar.f32([128, 512])
        aH2Ahi = ar.bf([128, 2, 256])
        aH2Alo = ar.bf([128, 2, 256])
        aX1tm = ar.f32([128, 4, 1024])
        aX1hi = ar.bf([128, 8, 512])
        aX1lo = ar.bf([128, 8, 512])
        aW0 = ar.bf([128, 2, 8, 128])
        w1_off = ar.off
        aW1 = ar.bf([128, 2, 32, 128])
        aW1_f32 = sb_arena[0:128, w1_off:w1_off + 4096]
        aW2 = ar.bf([128, 4, 1024])
        aG1f = ar.f32([128, 512])
        aZ2 = ar.f32([128, 1024])
        assert ar.off <= ARENA_N, ar.off

        E = sch.emit
        W = sch.wait
        WC = sch.waitc

        def dma(q, sem, dst, src):
            post = E(q, lambda e, d=dst, r=src: e.dma_start(out=d, in_=r),
                     inc=(sem, 16))
            return (sem, post)

        dma('y', 'dC', aC32, dc['C32'].rearrange("(a p) f -> p a f", p=128))
        dma('y', 'dC', aS32, dc['S32'].rearrange("(a p) f -> p a f", p=128))
        dma('y', 'dC', aCc, dc['Ccorr'].rearrange("(a p) f -> p a f", p=128))
        dma('y', 'dC', aSc, dc['Scorr'].rearrange("(a p) f -> p a f", p=128))
        dma('y', 'dC', aCi32, dc['Ci32'].rearrange("(a p) f -> p a f", p=128))
        dma('y', 'dC', aSi32, dc['Si32'].rearrange("(a p) f -> p a f", p=128))
        dma('y', 'dC', aCic, dc['Cicorr'].rearrange("(a p) f -> p a f", p=128))
        dma('y', 'dC', aSic, dc['Sicorr'].rearrange("(a p) f -> p a f", p=128))
        for nm, dst in (('cpool', sb_cpool), ('altcol', sb_altcol),
                        ('iota32', sb_iota32), ('iota16', sb_iota16),
                        ('ident', sb_ident), ('Mplus', sb_mplus),
                        ('Mminus', sb_mminus)):
            dma('y', 'dC', dst[:, :], dc[nm])
        for src, dst in ((d_logdt, sb_logdt), (d_lamre, sb_lamre), (d_lamim, sb_lamim),
                         (d_wre, sb_wre), (d_wim, sb_wim), (d_svec, sb_svec),
                         (d_Drep, sb_drep)):
            dma('y', 'dC', dst[:, :], src)
        ck_consts = ('dC', sch.cnt['dC'])

        a_col = sb_ab[:, 0:512]
        b_col = sb_ab[:, 512:1024]
        WC('s', ck_consts)
        E('s', lambda e: e.activation(sb_logdt[:, :], sb_logdt[:, :], AF.Exp))
        E('s', lambda e: e.activation(sb_lamre[:, :], sb_lamre[:, :], AF.Exp))
        ck = E('s', lambda e: e.activation(sb_dp1[:, :], sb_drep[:, :], AF.Identity,
                                           bias=1.0), inc=('sA', 1))
        W('v', 'sA', ck)
        E('v', lambda e: e.tensor_mul(sb_stats[:, 0:1], sb_svec[:, 0:1],
                                      sb_svec[:, 1:2]))
        E('v', lambda e: e.tensor_scalar(a_col, sb_logdt[:, :], sb_lamre[:, 0:1],
                                         -1.0, OP.mult, OP.mult))
        E('v', lambda e: e.tensor_scalar(sb_stats[:, 1:2], sb_svec[:, 2:3], 8.0,
                                         None, OP.mult))
        E('v', lambda e: e.tensor_scalar(b_col, sb_logdt[:, :], sb_lamim[:, 0:1],
                                         None, OP.mult))
        E('v', lambda e: e.tensor_mul(sb_alpha[:, :], sb_stats[:, 0:1],
                                      sb_stats[:, 1:2]), inc=('sV', 1))

        if KCUT == 'p0a':
            W('g', 'sV', sch.cnt['sV'])
            ck_dummy = dma('g', 'dOut', d_out[0:128, :], sb_dump[:, :])
            WC('g', ck_dummy)
            return _finish(sch, blk, sems, nc)

        ck_w1out = {}
        ck_w1cvt = {}
        for ic in range(32):
            srcu, ksem = (cvA, 'dKa') if ic % 2 == 0 else (cvB, 'dKb')
            dstb, osem = (cvAb, 'dOa') if ic % 2 == 0 else (cvBb, 'dOb')
            W('y', 'sA', ck_w1cvt.get(ic - 2, 0))
            ckl = dma('y', ksem, srcu[:, 0:4096], d_k1[ic * 128:(ic + 1) * 128, :])
            WC('s', ckl)
            WC('s', ck_w1out.get(ic - 2))
            cka = E('s', lambda e, d=dstb, r=srcu: e.activation(
                d[:, 0:4096], r[:, 0:4096], AF.Identity, bias=-1.0, scale=2.0),
                inc=('sA', 1))
            ck_w1cvt[ic] = cka
            W('g', 'sA', cka)
            for ocg in range(4):
                ck_w1out[ic] = dma(
                    'g', osem,
                    d_w1[ocg * 8:(ocg + 1) * 8, :, ic, :].rearrange(
                        "o q c -> q o c"),
                    dstb[:, ocg * 1024:(ocg + 1) * 1024].rearrange(
                        "p (o c) -> p o c", c=128))
        ck_w0out = {}
        ck_w0cvt = 0
        for ic in range(8):
            W('y', 'sV', ck_w0cvt)
            if ic == 0:
                W('y', 'sA', ck_w1cvt[31])
            ckl0 = dma('y', 'dKa', cvA[:, 0:4096], d_k0t[ic * 128:(ic + 1) * 128, :])
            ckl1 = dma('y', 'dKb', cvB[:, 0:4096], d_k0b[ic * 128:(ic + 1) * 128, :])
            WC('v', ckl0)
            WC('v', ckl1)
            if ic == 0:
                WC('v', ck_w1out[30])
            WC('v', ck_w0out.get(ic - 1))
            ckv = E('v', lambda e: e.tensor_sub(cvAb[:, 0:4096], cvA[:, 0:4096],
                                                cvB[:, 0:4096]), inc=('sV', 1))
            ck_w0cvt = ckv
            W('g', 'sV', ckv)
            for ocg in range(4):
                ck_w0out[ic] = dma(
                    'g', 'dOa',
                    d_w0[ocg * 8:(ocg + 1) * 8, :, ic, :].rearrange(
                        "o q c -> q o c"),
                    cvAb[:, ocg * 1024:(ocg + 1) * 1024].rearrange(
                        "p (o c) -> p o c", c=128))
        ck_w2out = {}
        ck_w2cvt = 0
        for ic in range(32):
            W('y', 'sA', ck_w2cvt)
            if ic == 0:
                W('y', 'sV', ck_w0cvt)
            ckl = dma('y', 'dKa', cvA[:, 0:1024], d_k2[ic * 128:(ic + 1) * 128, :])
            WC('s', ckl)
            if ic == 0:
                WC('s', ck_w1out[31])
            WC('s', ck_w2out.get(ic - 1))
            cka = E('s', lambda e: e.activation(cvBb[:, 0:1024], cvA[:, 0:1024],
                                                AF.Identity, bias=-1.0, scale=2.0),
                    inc=('sA', 1))
            ck_w2cvt = cka
            W('g', 'sA', cka)
            ck_w2out[ic] = dma('g', 'dOb', d_w2[ic, :, :], cvBb[:, 0:1024])
        ck_oa_total = ('dOa', sch.cnt['dOa'])
        ck_ob_total = ('dOb', sch.cnt['dOb'])

        if KCUT == 'p0b':
            WC('g', ck_oa_total)
            WC('g', ck_ob_total)
            ck_dummy = dma('g', 'dOut', d_out[0:128, :], sb_dump[:, :])
            WC('g', ck_dummy)
            return _finish(sch, blk, sems, nc)

        def r3(ap, b):
            return ap.rearrange("p (a b) -> p a b", b=b)

        def sincos(argI, sn_out, cs_out, width):
            tA = g['tA'][:, 0:width]
            tB = g['tB'][:, 0:width]
            mi = sb_mi[:, 0:width]
            sg = sb_dump[:, 0:width]
            cks = None
            for mode in (0, 1):
                if mode == 0:
                    E('v', lambda e: e.tensor_scalar(tA, argI, I2LO, None, OP.mult))
                else:
                    W('v', 'sA', cks)
                    E('v', lambda e: e.tensor_scalar(tA, argI, I2LO, 0.25,
                                                     OP.mult, OP.add))
                E('v', lambda e: e.scalar_tensor_tensor(
                    tA, argI, I2HI, tA, OP.mult, OP.add))
                E('v', lambda e: e.tensor_copy(mi, tA))
                E('v', lambda e: e.tensor_copy(tB, mi))
                E('v', lambda e: e.tensor_sub(tA, tA, tB))
                E('v', lambda e: e.tensor_single_scalar(sg, tA, 0.5, OP.is_gt))
                E('v', lambda e: e.tensor_single_scalar(tA, tA, -0.5, OP.is_lt))
                E('v', lambda e: e.tensor_add(tB, tB, sg))
                E('v', lambda e: e.tensor_sub(tB, tB, tA))
                E('v', lambda e: e.scalar_tensor_tensor(tA, tB, -CHI, argI,
                                                        OP.mult, OP.add))
                E('v', lambda e: e.scalar_tensor_tensor(tA, tB, -CLO, tA,
                                                        OP.mult, OP.add))
                if mode == 1:
                    E('v', lambda e: e.tensor_scalar(tA, tA, PHHI, PHLO,
                                                     OP.add, OP.add))
                ckv = E('v', lambda e: e.tensor_scalar(tA, tA, PI_SAFE, -PI_SAFE,
                                                       OP.min, OP.max), inc=('sV', 1))
                W('s', 'sV', ckv)
                dst = sn_out if mode == 0 else cs_out
                cks = E('s', lambda e, d=dst: e.activation(d, tA, AF.Sin),
                        inc=('sA', 1))
            return cks

        WC('v', ck_oa_total)
        WC('v', ck_ob_total)
        WC('s', ck_oa_total)
        WC('s', ck_ob_total)

        KHG = int(os.environ.get('KHG', '16'))
        ck_ss_mm = 0
        ck_ssmm_hist = {}
        ck_klh_v = 0
        for hg in range(KHG):
            j0 = hg * 32
            a_sl = a_col[:, j0:j0 + 32].unsqueeze(2).broadcast_to([128, 32, 32])
            b_sl = b_col[:, j0:j0 + 32].unsqueeze(2).broadcast_to([128, 32, 32])
            wre_sl = sb_wre[:, j0:j0 + 32].unsqueeze(2).broadcast_to([128, 32, 32])
            wim_sl = sb_wim[:, j0:j0 + 32].unsqueeze(2).broadcast_to([128, 32, 32])
            i32b = sb_iota32[:, :].unsqueeze(1).broadcast_to([128, 32, 32])
            W('v', 'sA', sch.cnt['sA'])
            ckv = E('v', lambda e, a=a_sl, i=i32b: e.tensor_tensor(
                r3(g['ApRe'], 32), a, i, OP.mult), inc=('sV', 1))
            W('s', 'sV', ckv)
            E('s', lambda e: e.activation(g['EA'], g['ApRe'], AF.Exp))
            E('v', lambda e, b=b_sl, i=i32b: e.tensor_tensor(
                r3(g['argI'], 32), b, i, OP.mult))
            cks = sincos(g['argI'], g['snA'], g['csA'], 1024)
            W('v', 'sA', cks)
            E('v', lambda e, w=wre_sl: e.tensor_tensor(r3(g['tA'], 32),
                                                       r3(g['csA'], 32), w, OP.mult))
            E('v', lambda e, w=wim_sl: e.tensor_tensor(r3(g['tB'], 32),
                                                       r3(g['snA'], 32), w, OP.mult))
            E('v', lambda e: e.tensor_sub(g['tA'], g['tA'], g['tB']))
            E('v', lambda e: e.tensor_mul(g['ApRe'], g['tA'], g['EA']))
            E('v', lambda e, w=wre_sl: e.tensor_tensor(r3(g['tA'], 32),
                                                       r3(g['snA'], 32), w, OP.mult))
            E('v', lambda e, w=wim_sl: e.tensor_tensor(r3(g['tB'], 32),
                                                       r3(g['csA'], 32), w, OP.mult))
            E('v', lambda e: e.tensor_add(g['tA'], g['tA'], g['tB']))
            E('v', lambda e: e.tensor_mul(g['ApIm'], g['tA'], g['EA']))
            a_sb = a_col[:, j0:j0 + 32].unsqueeze(2).broadcast_to([128, 32, 16])
            b_sb = b_col[:, j0:j0 + 32].unsqueeze(2).broadcast_to([128, 32, 16])
            i16b = sb_iota16[:, :].unsqueeze(1).broadcast_to([128, 32, 16])
            ckv = E('v', lambda e, a=a_sb, i=i16b: e.tensor_tensor(
                r3(g['BRe'], 16), a, i, OP.mult), inc=('sV', 1))
            W('s', 'sV', ckv)
            E('s', lambda e: e.activation(g['EB'], g['BRe'], AF.Exp))
            E('v', lambda e, b=b_sb, i=i16b: e.tensor_tensor(
                r3(g['argI'][:, 0:512], 16), b, i, OP.mult))
            cks = sincos(g['argI'][:, 0:512], g['snB'], g['csB'], 512)
            W('v', 'sA', cks)
            E('v', lambda e: e.tensor_mul(g['BRe'], g['EB'], g['csB']))
            ckB = E('v', lambda e: e.tensor_mul(g['BIm'], g['EB'], g['snB']),
                    inc=('sV', 1))
            kp = ppA if (hg // 2) % 2 == 0 else ppB
            ApRe_j = r3(g['ApRe'], 32)
            ApIm_j = r3(g['ApIm'], 32)
            BRe_j = r3(g['BRe'], 16)
            BIm_j = r3(g['BIm'], 16)
            for jl in range(32):
                jfull = (hg % 2) * 32 + jl
                jg = hg * 32 + jl
                s1n = 'SS1' if jg % 2 == 0 else 'SS1b'
                s2n = 'SS2' if jg % 2 == 0 else 'SS2b'
                ss1 = r3(g[s1n], 32)
                ss2 = r3(g[s2n], 32)
                apre = ApRe_j[:, jl, :].unsqueeze(1).broadcast_to([128, 16, 32])
                apim = ApIm_j[:, jl, :].unsqueeze(1).broadcast_to([128, 16, 32])
                bre = BRe_j[:, jl, :].unsqueeze(2).broadcast_to([128, 16, 32])
                bim = BIm_j[:, jl, :].unsqueeze(2).broadcast_to([128, 16, 32])
                W('v', 'sV', ckB)
                W('v', 'sT', ck_ssmm_hist.get(jg - 2, 0))
                E('v', lambda e, a=apre, b=bre, o=ss1: e.tensor_tensor(o, a, b, OP.mult))
                ckss = E('v', lambda e, a=apim, b=bim, o=ss2: e.tensor_tensor(
                    o, a, b, OP.mult), inc=('sV', 1))
                W('t', 'sV', ckss)
                c0 = 126 - 2 * jfull
                E('t', lambda e, p=kp, c=c0, s1=s1n: e.matmul(
                    p[:, :], sb_mplus[:, c:c + 128], g[s1],
                    start=(c == 126), stop=False))
                ck_ss_mm = E('t', lambda e, p=kp, c=c0, s2=s2n,
                             lst=(jfull == 63): e.matmul(
                    p[:, :], sb_mminus[:, c:c + 128], g[s2],
                    start=False, stop=lst), inc=('sT', 1))
                ck_ssmm_hist[jg] = ck_ss_mm
            if hg % 2 == 1:
                grp = hg // 2
                W('v', 'sT', ck_ss_mm)
                ckv = E('v', lambda e, p=kp: e.tensor_copy(g['khl'], p[:, :]),
                        inc=('sV', 1))
                khl2 = r3(g['khl'], 128)
                ck_tev = {}
                for lc in range(4):
                    pt = ppC if lc % 2 == 0 else ppD
                    W('t', 'sV', ckv)
                    W('t', 'sV', ck_tev.get(lc - 2, 0))
                    ckt = E('t', lambda e, p=pt, l=lc: e.transpose(
                        p[:, 0:128], khl2[:, l, :], sb_ident[:, :]), inc=('sT', 1))
                    W('v', 'sT', ckt)
                    E('v', lambda e, p=pt, l=lc, gg=grp: e.tensor_copy(
                        aKlh[:, l, gg * 128:(gg + 1) * 128], p[:, 0:128]))
                    ck_klh_v = E('v', lambda e, p=pt, l=lc, gg=grp: e.tensor_copy(
                        aKbf[:, l, gg * 128:(gg + 1) * 128], p[:, 0:128]),
                        inc=('sV', 1))
                    ck_tev[lc] = ck_klh_v
        ck_k_done = ck_klh_v

        if KCUT == 'p0c':
            W('g', 'sV', sch.cnt['sV'])
            W('g', 'sT', sch.cnt['sT'])
            ck_dummy = dma('g', 'dOut', d_out[0:128, :], sb_dump[:, :])
            WC('g', ck_dummy)
            return _finish(sch, blk, sems, nc)

        ck_kf_ev = {}
        for comp in range(2):
            dst = aKfRe if comp == 0 else aKfIm
            c32v, ccv = (aC32, aCc) if comp == 0 else (aS32, aSc)
            for fc in range(4):
                for hh in range(2):
                    idx = comp * 8 + fc * 2 + hh
                    pp = ppA if idx % 2 == 0 else ppB
                    W('t', 'sV', ck_k_done)
                    W('t', 'sV', ck_kf_ev.get(idx - 2, 0))
                    for lc in range(4):
                        E('t', lambda e, p=pp, l=lc, f=fc, h=hh, cv=c32v: e.matmul(
                            p[:, :], cv[:, l, f * 128:(f + 1) * 128],
                            aKlh[:, l, h * 512:(h + 1) * 512],
                            start=(l == 0), stop=False))
                    ckt = None
                    for lc in range(4):
                        ckt = E('t', lambda e, p=pp, l=lc, f=fc, h=hh, cv=ccv:
                                e.matmul(
                                    p[:, :], cv[:, l, f * 128:(f + 1) * 128],
                                    aKbf[:, l, h * 512:(h + 1) * 512],
                                    start=False, stop=(l == 3)),
                                inc=('sT', 1) if lc == 3 else None)
                    W('v', 'sT', ckt)
                    ck_kf_ev[idx] = E(
                        'v', lambda e, p=pp, f=fc, h=hh, d=dst: e.tensor_copy(
                            d[:, f, h * 512:(h + 1) * 512], p[:, :]), inc=('sV', 1))
        ck_last_kf = max(ck_kf_ev.values())
        W('v', 'sV', ck_last_kf)
        E('v', lambda e: e.tensor_copy(aKfX[:, 0, :], aKfIm[:, 0, :]))
        E('v', lambda e: e.memset(aKfX[0:1, 0, :], 0.0))
        E('v', lambda e: e.tensor_copy(aKfX[:, 1, :], aKfRe[:, 0, :]))
        E('v', lambda e: e.tensor_copy(aKfX[0:1, 1, :], aKfIm[0:1, 0, :]),
          inc=('sV', 1))
        ck_p0_v = sch.cnt['sV']
        ck_p0_t = sch.cnt['sT']
        if KDBG:
            W('g', 'sV', ck_p0_v)
            W('g', 'sT', ck_p0_t)
            WC('g', ck_oa_total)
            WC('g', ck_ob_total)
            ckw1t = dma('g', 'dKb', cvAb.rearrange("p (a b) -> p a b", b=128),
                        d_w1[1, :, :, :])
            WC('g', ckw1t)
            dma('g', 'dOut', d_dbg[3, :, :],
                cvAb.bitcast(F32)[:, 0:1024])
            dma('g', 'dOut', d_dbg[4, :, :],
                cvAb.bitcast(F32)[:, 1024:2048])
            dma('g', 'dOut', d_dbg[0, :, :], aKfRe[:, 0, :])
            dma('g', 'dOut', d_dbg[1, :, :], aKfIm[:, 0, :])
            dma('g', 'dOut', d_dbg[2, :, :], aKfX[:, 0, :])
            dma('g', 'dOut', d_dbg[5, :, 0:1024], sb_ab[:, :])
            ckdx = dma('g', 'dOut', d_dbg[6, :, 0:512], sb_wre[:, :])
            WC('t', ckdx)

        if KCUT == 'p0':
            W('g', 'sV', ck_p0_v)
            W('g', 'sT', ck_p0_t)
            ck_dummy = dma('g', 'dOut', d_out[0:128, :], sb_dump[:, :])
            WC('g', ck_dummy)
            return _finish(sch, blk, sems, nc)

        ck_uh = {}

        def load_uhalf(h, ck_t=None, ck_s=None, ck_v=None):
            if h > NBLK:
                return
            if h < 2:
                W('g', 'sV', ck_p0_v)
                W('g', 'sT', ck_p0_t)
            else:
                W('g', 'sT', sch.cnt['sT'] if ck_t is None else ck_t)
                W('g', 'sA', sch.cnt['sA'] if ck_s is None else ck_s)
                if ck_v is not None:
                    W('g', 'sV', ck_v)
            ck_uh[h] = dma('g', 'dU0' if h % 2 == 0 else 'dU1', aU[h % 2],
                           d_u[h * 512:(h + 1) * 512, :].rearrange(
                               "(a p) c -> p a c", p=128))

        load_uhalf(0)
        load_uhalf(1)

        def ln_emit(tile, outtile):
            st = sb_stats
            ckv = E('v', lambda e, ti=tile: e.tensor_reduce(st[:, 0:1], ti, AXX,
                                                            OP.add), inc=('sV', 1))
            W('s', 'sV', ckv)
            cks = E('s', lambda e, ti=tile: e.activation(sb_dump[:, :], ti, AF.Square,
                                                         accum_out=st[:, 1:2]),
                    inc=('sA', 1))
            W('v', 'sA', cks)
            E('v', lambda e: e.tensor_scalar(st[:, 2:3], st[:, 0:1], R1024,
                                             None, OP.mult))
            E('v', lambda e: e.tensor_scalar(st[:, 3:4], st[:, 1:2], R1024,
                                             None, OP.mult))
            ckv = E('v', lambda e: e.tensor_scalar(st[:, 8:9], st[:, 0:1], -R1024,
                                                   None, OP.mult), inc=('sV', 1))
            W('s', 'sV', ckv)
            cks = E('s', lambda e: e.activation(st[:, 4:5], st[:, 2:3], AF.Square),
                    inc=('sA', 1))
            W('v', 'sA', cks)
            ckv = E('v', lambda e: e.tensor_sub(st[:, 5:6], st[:, 3:4], st[:, 4:5]),
                    inc=('sV', 1))
            W('s', 'sV', ckv)
            cks = E('s', lambda e: e.activation(st[:, 6:7], st[:, 5:6], AF.Sqrt,
                                                bias=EPS), inc=('sA', 1))
            W('v', 'sA', cks)
            ckv = E('v', lambda e: e.reciprocal(st[:, 7:8], st[:, 6:7]),
                    inc=('sV', 1))
            W('s', 'sV', ckv)
            cks = E('s', lambda e: e.activation(st[:, 9:10], st[:, 8:9], AF.Copy,
                                                scale=st[:, 7:8]), inc=('sA', 1))
            W('v', 'sA', cks)
            return E('v', lambda e, ti=tile, o=outtile: e.scalar_tensor_tensor(
                o, ti, st[:, 7:8], st[:, 9:10].broadcast_to([128, 1024]),
                OP.mult, OP.add), inc=('sV', 1))

        ck_fwd_last = 0
        ck_inv_ev = 0
        zbufs = (aZasm, sb_ab)
        spill_ck = {}
        pend_ln = None
        ztile = 0

        def flush_ln():
            nonlocal pend_ln
            ti, buf = pend_ln
            ckln = ln_emit(buf[:, :], buf[:, :])
            W('g', 'sV', ckln)
            spill_ck[ti] = dma('g', 'dSp',
                               d_x1[ti * 128:(ti + 1) * 128, :], buf[:, :])
            pend_ln = None

        for blkid in range(NBLK):
            hA, hB = blkid, blkid + 1
            uA, uB = aU[hA % 2], aU[hB % 2]
            ck_inv_prev = ck_inv_ev
            ck_ubf_last = 0

            for chh in range(2):
                def usl(tc, uA=uA, uB=uB, c=chh):
                    return (uA if tc < 4 else uB)[:, tc % 4, c * 512:(c + 1) * 512]

                WC('s', ck_uh[hA])
                WC('s', ck_uh[hB])
                W('s', 'sT', ck_fwd_last)
                ck_ubf = None
                for tc in range(8):
                    ck_ubf = E('s', lambda e, t=tc, us=usl: e.activation(
                        aUbf[:, t, :], us(t), AF.Copy),
                        inc=('sA', 1) if tc == 7 else None)
                ck_ubf_last = ck_ubf

                ck_yf_v = None
                ck_yf_s = None

                def emit_pw(fc, c=chh):
                    nonlocal ck_yf_v, ck_yf_s
                    sl = slice(c * 512, (c + 1) * 512)
                    kre = aKfRe[:, fc, sl]
                    kim = aKfX[:, 0, sl] if fc == 0 else aKfIm[:, fc, sl]
                    kreI = aKfX[:, 1, sl] if fc == 0 else aKfRe[:, fc, sl]
                    ure = aUf[:, fc, :]
                    uim = aUf[:, 4 + fc, :]
                    E('v', lambda e, k=kreI, u=uim: e.tensor_tensor(
                        aScr[:, 0, :], k, u, OP.mult))
                    E('v', lambda e, k=kim, u=ure: e.tensor_tensor(
                        aScr[:, 1, :], k, u, OP.mult))
                    E('v', lambda e, k=kre, u=ure: e.tensor_tensor(
                        aScr[:, 2, :], k, u, OP.mult))
                    E('v', lambda e, k=kim, u=uim: e.tensor_tensor(
                        aScr[:, 3, :], k, u, OP.mult))
                    E('v', lambda e, o=ure: e.tensor_sub(o, aScr[:, 2, :],
                                                         aScr[:, 3, :]))
                    ck_yf_v = E('v', lambda e, o=uim: e.tensor_add(
                        o, aScr[:, 0, :], aScr[:, 1, :]), inc=('sV', 1))
                    W('s', 'sV', ck_yf_v)
                    E('s', lambda e, o=ure, f=fc: e.activation(
                        aYbf[:, f, :], o, AF.Copy))
                    ck_yf_s = E('s', lambda e, o=uim, f=fc: e.activation(
                        aYbf[:, 4 + f, :], o, AF.Copy), inc=('sA', 1))

                WC('t', ck_uh[hA])
                WC('t', ck_uh[hB])
                W('t', 'sA', ck_ubf)
                W('t', 'sV', ck_inv_ev)
                ck_uf_ev = {}
                for comp in range(2):
                    c32v, ccv = (aC32, aCc) if comp == 0 else (aS32, aSc)
                    for fc in range(4):
                        gi = comp * 4 + fc
                        g1 = ppA if gi % 2 == 0 else ppC
                        g2 = ppB if gi % 2 == 0 else ppD
                        W('t', 'sV', ck_uf_ev.get(gi - 2, 0))
                        ckt = None
                        for tc in range(4):
                            for half in range(2):
                                pp = g1 if half == 0 else g2
                                E('t', lambda e, p=pp, t=tc, hf=half, f=fc, us=usl,
                                  cv=c32v: e.matmul(
                                    p[:, :], cv[:, t, f * 128:(f + 1) * 128],
                                    us(t + 4 * hf), start=(t == 0), stop=False))
                        for tc in range(4):
                            for half in range(2):
                                pp = g1 if half == 0 else g2
                                ckt = E('t', lambda e, p=pp, t=tc, hf=half, f=fc,
                                        cv=ccv: e.matmul(
                                            p[:, :], cv[:, t, f * 128:(f + 1) * 128],
                                            aUbf[:, t + 4 * hf, :],
                                            start=False, stop=(t == 3)),
                                        inc=('sT', 1) if (half == 1 and tc == 3)
                                        else None)
                        W('v', 'sT', ckt)
                        E('v', lambda e, p=g2: e.tensor_copy(aScr[:, 0, :], p[:, :]))
                        ck_uf_ev[gi] = E(
                            'v', lambda e, p=g1, sl=gi: e.scalar_tensor_tensor(
                                aUf[:, sl, :], aScr[:, 0, :], sb_altcol[:, 0:1],
                                p[:, :], OP.mult, OP.add), inc=('sV', 1))
                        if comp == 1:
                            emit_pw(fc)
                ck_fwd_last = sch.cnt['sT']
                if chh == 1:
                    load_uhalf(blkid + 2, ck_t=ck_fwd_last, ck_s=ck_ubf_last,
                               ck_v=ck_inv_prev)

                for tb in range(4):
                    pp = ppA if tb % 2 == 0 else ppB
                    W('t', 'sV', ck_yf_v)
                    W('t', 'sA', ck_yf_s)
                    W('t', 'sV', ck_inv_ev)
                    for comp in range(2):
                        civ = aCi32 if comp == 0 else aSi32
                        for fc in range(4):
                            E('t', lambda e, p=pp, f=fc, t=tb, cv=civ, cp=comp:
                              e.matmul(
                                p[:, :], cv[:, f, t * 128:(t + 1) * 128],
                                aUf[:, cp * 4 + f, :], start=(cp == 0 and f == 0),
                                stop=False))
                    ckt = None
                    for comp in range(2):
                        cicv = aCic if comp == 0 else aSic
                        for fc in range(4):
                            islast = (comp == 1 and fc == 3)
                            ckt = E('t', lambda e, p=pp, f=fc, t=tb, cv=cicv, cp=comp,
                                    lst=islast: e.matmul(
                                        p[:, :], cv[:, f, t * 128:(t + 1) * 128],
                                        aYbf[:, cp * 4 + f, :], start=False, stop=lst),
                                    inc=('sT', 1) if islast else None)
                    W('v', 'sT', ckt)
                    E('v', lambda e, t=tb, c=chh, u=uB: e.tensor_tensor(
                        aScr[:, 0, :], u[:, t, c * 512:(c + 1) * 512],
                        sb_dp1[:, c * 512:(c + 1) * 512], OP.mult))
                    if chh == 0:
                        ck_inv_ev = E('v', lambda e, p=pp, t=tb: e.tensor_add(
                            aZX[:, t, :], aScr[:, 0, :], p[:, :]), inc=('sV', 1))
                    else:
                        buf = zbufs[ztile % 2]
                        WC('v', spill_ck.get(ztile - 2))
                        E('v', lambda e, t=tb, b=buf: e.tensor_copy(
                            b[:, 0:512], aZX[:, t, :]))
                        ck_inv_ev = E('v', lambda e, p=pp, b=buf: e.tensor_add(
                            b[:, 512:1024], aScr[:, 0, :], p[:, :]),
                            inc=('sV', 1))
                        if pend_ln is not None:
                            flush_ln()
                        pend_ln = (ztile, buf)
                        ztile += 1
        flush_ln()

        if KCUT == 'p1':
            W('g', 'dSp', sch.cnt['dSp'])
            ck_dummy = dma('g', 'dOut', d_out[0:128, :], sb_dump[:, :])
            WC('g', ck_dummy)
            return _finish(sch, blk, sems, nc)

        ck_bar = {s: sch.cnt[s] for s in DMA_SEMS + ENG_SEMS}
        for eng in ('t', 'v', 's', 'y', 'g'):
            for sname, val in ck_bar.items():
                W(eng, sname, val)

        W0SEM = ('dW0a', 'dW0b')
        W1SEM = ('dW1a', 'dW1b')
        W2SEM = ('dW2a', 'dW2b', 'dW2c', 'dW2d')
        ck_w0_use = {}
        ck_w1_use = {}
        ck_w2_use = {}
        ck_w2b_use = {}
        ck_z2b = 0
        ck_out = {}
        def load_w0(p, oc, ck_w0_load, gate):
            W('y', 'sT', gate)
            ck_w0_load[oc] = dma('y', W0SEM[oc % 2], aW0[:, oc % 2, :, :],
                                 d_w0[oc, :, :, :])

        import os as _os2
        W1Q = _os2.environ.get('KW1Q', 'y')
        assert W1Q in ('y', 'g', 's')
        W1R1 = _os2.environ.get('KW1R1') == '1'

        def load_w1(p, oc, ck_w1_load, gate):
            W(W1Q, 'sT', gate)
            slot = 0 if W1R1 else oc % 2
            ck_w1_load[oc] = dma(
                W1Q, W1SEM[oc % 2],
                aW1_f32[:, slot * 2048:(slot + 1) * 2048],
                d_w1[oc, :, :, :].rearrange("q a b -> q (a b)").bitcast(F32))

        def load_w2(p, oc, ck_w2_load, gate):
            W('y', 'sT', gate)
            ck_w2_load[oc] = dma('y', W2SEM[oc % 4], aW2[:, oc % 4, :],
                                 d_w2[oc, :, :])

        for p in range(NPAIR):
            if p > 0:
                W('y', 'sV', ck_z2b)
                WC('y', ck_out[(p - 1, 1)])
            ck_x1_load = dma('y', 'dX', aX1tm,
                             d_x1[p * 512:(p + 1) * 512, :].rearrange(
                                 "(a q) c -> q a c", q=128))
            ck_w0_load = {}
            ck_w1_load = {}
            ck_w2_load = {}
            ck_w2b_load = {}
            for oc in range(2):
                load_w0(p, oc, ck_w0_load, ck_w0_use.get((p - 1, 30 + oc), 0))

            WC('t', ck_x1_load)
            ck_x1t = {}
            ck_tr = {}
            ck_tr_last = 0
            for tc in range(4):
                for fc in range(8):
                    i = tc * 8 + fc
                    pt = ppC if i % 2 == 0 else ppD
                    W('t', 'sV', ck_tr.get(i - 2, 0))
                    ckt = E('t', lambda e, q=pt, t=tc, f=fc: e.transpose(
                        q[:, 0:128], aX1tm[:, t, f * 128:(f + 1) * 128],
                        sb_ident[:, :]), inc=('sT', 1))
                    W('s', 'sT', ckt)
                    ck_x1t[i] = E('s', lambda e, q=pt, t=tc, f=fc: e.activation(
                        aX1hi[:, f, t * 128:(t + 1) * 128], q[:, 0:128], AF.Copy),
                        inc=('sA', 1))
                    W('v', 'sA', ck_x1t[i])
                    ck_tr[i] = E('v', lambda e, q=pt, t=tc, f=fc: e.tensor_sub(
                        aX1lo[:, f, t * 128:(t + 1) * 128], q[:, 0:128],
                        aX1hi[:, f, t * 128:(t + 1) * 128]), inc=('sV', 1))
                    ck_tr_last = ck_tr[i]

            ck_h1_ev = {}
            for oc in range(32):
                pp = ppA if oc % 2 == 0 else ppB
                W('t', 'sV', ck_h1_ev.get(oc - 2, 0))
                WC('t', ck_w0_load[oc])
                W('t', 'sV', ck_tr_last)
                ckt = None
                for ic in range(8):
                    E('t', lambda e, q=pp, o=oc, i=ic: e.matmul(
                        q[:, :], aW0[:, o % 2, i, :], aX1hi[:, i, :],
                        start=(i == 0), stop=False))
                    ckt = E('t', lambda e, q=pp, o=oc, i=ic: e.matmul(
                        q[:, :], aW0[:, o % 2, i, :], aX1lo[:, i, :],
                        start=False, stop=(i == 7)),
                        inc=('sT', 1) if ic == 7 else None)
                ck_w0_use[(p, oc)] = ckt
                if oc + 2 <= 31:
                    load_w0(p, oc + 2, ck_w0_load, ckt)
                elif oc == 30:
                    if W1R1:
                        load_w1(p, 0, ck_w1_load, ck_w1_use.get((p - 1, 31), 0))
                    else:
                        for o2 in range(2):
                            load_w1(p, o2, ck_w1_load,
                                    ck_w1_use.get((p - 1, 30 + o2), 0))
                    for o2 in range(4):
                        load_w2(p, o2, ck_w2_load,
                                ck_w2b_use.get((p - 1, 28 + o2), 0))
                W('s', 'sT', ckt)
                W('s', 'sV', ck_h1_ev.get(oc - 1, 0))
                cka = E('s', lambda e, q=pp: e.activation(aG1f[:, :], q[:, :],
                                                          AF.Relu), inc=('sA', 1))
                W('v', 'sA', cka)
                E('v', lambda e, o=oc: e.tensor_copy(aH1hi[:, o, :], aG1f[:, :]))
                ck_h1_ev[oc] = E('v', lambda e, o=oc: e.tensor_sub(
                    aH1lo[:, o, :], aG1f[:, :], aH1hi[:, o, :]), inc=('sV', 1))
            ck_h1_all = ck_h1_ev[31]
            if KDBG and p == 0:
                W('g', 'sV', ck_h1_all)
                h1flat = aH1hi.rearrange("q a b -> q (a b)").bitcast(F32)
                for di in range(4):
                    dma('g', 'dOut', d_dbg[11 + di, :, :],
                        h1flat[:, di * 1024:(di + 1) * 1024])
                x1flat = aX1hi.rearrange("q a b -> q (a b)").bitcast(F32)
                ckdx = dma('g', 'dOut', d_dbg[15, :, 0:1024], x1flat[:, 0:1024])
                WC('t', ckdx)

            ck_h2a_ev = {}
            ck_h2b_ev = {}
            ck_mm2a = {}

            def emit_mm2a(oc):
                sl = oc % 2
                W('t', 'sV', ck_h2a_ev[oc])
                WC('t', ck_w2_load[oc])
                if oc == 0:
                    W('t', 'sV', ck_z2b)
                ckt = None
                for hilo in range(2):
                    src = aH2Ahi if hilo == 0 else aH2Alo
                    for tb in range(2):
                        for chh in range(2):
                            ckt = E('t', lambda e, s2=sl, hl=hilo, t=tb, c=chh, o=oc,
                                    sr=src: e.matmul(
                                        pM[t * 2 + c][:, :],
                                        sr[:, s2, t * 128:(t + 1) * 128],
                                        aW2[:, o % 4, c * 512:(c + 1) * 512],
                                        start=(o == 0 and hl == 0),
                                        stop=(o == 31 and hl == 1)),
                                    inc=('sT', 1) if (hilo == 1 and tb == 1 and
                                                      chh == 1) else None)
                ck_mm2a[oc] = ckt
                ck_w2_use[(p, oc)] = ckt
                if oc + 4 <= 31:
                    load_w2(p, oc + 4, ck_w2_load, ckt)
                elif oc == 31:
                    for o2 in range(4):
                        W('y', 'sT', ck_w2_use[(p, 28 + o2)])
                        ck_w2b_load[o2] = dma('y', W2SEM[o2 % 4], aW2[:, o2 % 4, :],
                                              d_w2[o2, :, :])

            for oc in range(32):
                pp = ppA if oc % 2 == 0 else ppB
                sl = oc % 2
                W('t', 'sV', ck_h2a_ev.get(oc - 2, 0))
                W('t', 'sV', ck_h1_all)
                WC('t', ck_w1_load[oc])
                ckt = None
                w1slot = 0 if W1R1 else oc % 2
                for ic in range(32):
                    E('t', lambda e, q=pp, s2=w1slot, i=ic: e.matmul(
                        q[:, :], aW1[:, s2, i, :], aH1hi[:, i, :],
                        start=(i == 0), stop=False))
                    ckt = E('t', lambda e, q=pp, s2=w1slot, i=ic: e.matmul(
                        q[:, :], aW1[:, s2, i, :], aH1lo[:, i, :],
                        start=False, stop=(i == 31)),
                        inc=('sT', 1) if ic == 31 else None)
                ck_w1_use[(p, oc)] = ckt
                if W1R1 and oc + 1 <= 31:
                    load_w1(p, oc + 1, ck_w1_load, ckt)
                elif (not W1R1) and oc + 2 <= 31:
                    load_w1(p, oc + 2, ck_w1_load, ckt)
                W('s', 'sT', ckt)
                W('s', 'sV', ck_h2b_ev.get(oc - 1, 0))
                cka = E('s', lambda e, q=pp: e.activation(aH2f[:, :], q[:, :],
                                                          AF.Relu), inc=('sA', 1))
                W('v', 'sA', cka)
                W('v', 'sT', ck_mm2a.get(oc - 2, 0))
                E('v', lambda e, s2=sl: e.tensor_copy(aH2Ahi[:, s2, :],
                                                      aH2f[:, 0:256]))
                ck_h2a_ev[oc] = E('v', lambda e, s2=sl: e.tensor_sub(
                    aH2Alo[:, s2, :], aH2f[:, 0:256], aH2Ahi[:, s2, :]),
                    inc=('sV', 1))
                E('v', lambda e, o=oc: e.tensor_copy(aH2Bhi[:, o, :],
                                                     aH2f[:, 256:512]))
                ck_h2b_ev[oc] = E('v', lambda e, o=oc: e.tensor_sub(
                    aH2Blo[:, o, :], aH2f[:, 256:512], aH2Bhi[:, o, :]),
                    inc=('sV', 1))
                if oc >= 1:
                    emit_mm2a(oc - 1)
            emit_mm2a(31)

            ck_z2a = 0
            ck_z2a_stt = 0
            for tb in range(2):
                W('v', 'sT', ck_mm2a[31])
                if p > 0:
                    WC('v', ck_out.get((p - 1, 1)))
                ck_stt = None
                for chh in range(2):
                    ck_stt = E('v', lambda e, t=tb, c=chh: e.scalar_tensor_tensor(
                        aZ2[:, c * 512:(c + 1) * 512], pM[t * 2 + c][:, :],
                        sb_alpha[:, 0:1], aX1tm[:, t, c * 512:(c + 1) * 512],
                        OP.mult, OP.add),
                        inc=('sV', 1) if (chh == 1 and
                                          (tb == 1 or (KDBG and p == 0)))
                        else None)
                if tb == 1:
                    ck_z2a_stt = ck_stt
                if KDBG and p == 0 and tb == 0:
                    W('s', 'sV', ck_stt)
                    E('s', lambda e: e.activation(sb_dump[:, 0:512],
                                                  pM[0][:, :], AF.Copy))
                    ckps = E('s', lambda e: e.activation(sb_dump[:, 512:1024],
                                                         pM[1][:, :], AF.Copy),
                             inc=('sA', 1))
                    W('g', 'sA', ckps)
                    dma('g', 'dOut', d_dbg[7, :, :], aZ2[:, :])
                    dma('g', 'dOut', d_dbg[8, :, 0:512], sb_dump[:, 0:512])
                    E('g', lambda e: e.dma_start(
                        out=d_dbg[8, :, 512:513], in_=sb_alpha[:, 0:1],
                        allow_slow_non_contiguous=True), inc=('dOut', 16))
                    dma('g', 'dOut', d_dbg[9, :, 0:512], sb_dump[:, 512:1024])
                    ckdz = dma('g', 'dOut', d_dbg[10, :, :], aX1tm[:, 0, :])
                    WC('v', ckdz)
                    WC('t', ckdz)
                    WC('s', ckdz)
                ck_z2a = ln_emit(aZ2[:, :], aX1tm[:, tb, :])
            W('g', 'sV', ck_z2a)
            ck_out[(p, 0)] = dma('g', 'dOut',
                                 d_out[p * 512:p * 512 + 256, :].rearrange(
                                     "(a q) c -> q a c", q=128), aX1tm[:, 0:2, :])

            ck_mm2b = 0
            for oc in range(32):
                W('t', 'sV', ck_z2a_stt)
                WC('t', ck_w2b_load[oc])
                W('t', 'sV', ck_h2b_ev[31])
                ckt = None
                for hilo in range(2):
                    src = aH2Bhi if hilo == 0 else aH2Blo
                    for tb in range(2):
                        for chh in range(2):
                            ckt = E('t', lambda e, hl=hilo, t=tb, c=chh, o=oc,
                                    sr=src: e.matmul(
                                        pM[t * 2 + c][:, :],
                                        sr[:, o, t * 128:(t + 1) * 128],
                                        aW2[:, o % 4, c * 512:(c + 1) * 512],
                                        start=(o == 0 and hl == 0),
                                        stop=(o == 31 and hl == 1)),
                                    inc=('sT', 1) if (hilo == 1 and tb == 1 and
                                                      chh == 1) else None)
                ck_mm2b = ckt
                ck_w2b_use[(p, oc)] = ckt
                if oc + 4 <= 31:
                    W('y', 'sT', ckt)
                    ck_w2b_load[oc + 4] = dma('y', W2SEM[(oc + 4) % 4],
                                              aW2[:, (oc + 4) % 4, :],
                                              d_w2[oc + 4, :, :])
            for tb in range(2):
                W('v', 'sT', ck_mm2b)
                WC('v', ck_out[(p, 0)])
                for chh in range(2):
                    E('v', lambda e, t=tb, c=chh: e.scalar_tensor_tensor(
                        aZ2[:, c * 512:(c + 1) * 512], pM[t * 2 + c][:, :],
                        sb_alpha[:, 0:1], aX1tm[:, 2 + t, c * 512:(c + 1) * 512],
                        OP.mult, OP.add))
                ck_z2b = ln_emit(aZ2[:, :], aX1tm[:, 2 + tb, :])
            W('g', 'sV', ck_z2b)
            ck_out[(p, 1)] = dma('g', 'dOut',
                                 d_out[p * 512 + 256:(p + 1) * 512, :].rearrange(
                                     "(a q) c -> q a c", q=128), aX1tm[:, 2:4, :])

        W('g', 'dOut', sch.cnt['dOut'])
        return _finish(sch, blk, sems, nc)


def _finish(sch, blk, sems, nc):
    @blk.tensor
    def _(eng):
        sch.run('t', eng, sems)

    @blk.vector
    def _(eng):
        sch.run('v', eng, sems)

    @blk.scalar
    def _(eng):
        sch.run('s', eng, sems)

    @blk.sync
    def _(eng):
        sch.run('y', eng, sems)

    @blk.gpsimd
    def _(eng):
        sch.run('g', eng, sems)

    return nc


def _prep_inputs(inputs):
    u = np.asarray(inputs['L0_attn_u'])
    D = np.ascontiguousarray(np.asarray(inputs['L0_attn_D'], dtype=np.float32))
    lam_re = np.asarray(inputs['L0_attn_Lambda_re'], dtype=np.float32)
    lam_im = np.asarray(inputs['L0_attn_Lambda_im'], dtype=np.float32)
    log_dt = np.asarray(inputs['L0_attn_log_dt'], dtype=np.float32)
    W_re = np.asarray(inputs['L0_attn_W_re'], dtype=np.float32)
    W_im = np.asarray(inputs['L0_attn_W_im'], dtype=np.float32)
    k0 = np.asarray(inputs['L0_ffn_k0'])
    k1 = np.asarray(inputs['L0_ffn_k1'])
    k2 = np.asarray(inputs['L0_ffn_k2'])
    s0 = np.float32(np.asarray(inputs['L0_ffn_s0']))
    s1 = np.float32(np.asarray(inputs['L0_ffn_s1']))
    s2 = np.float32(np.asarray(inputs['L0_ffn_s2']))

    logdt_arr = np.empty((128, 512), np.float32)
    ld = log_dt.reshape(512, 2)
    logdt_arr[0::2, :] = ld[:, 0][None, :]
    logdt_arr[1::2, :] = ld[:, 1][None, :]
    lamre_col = np.ascontiguousarray(np.repeat(lam_re, 2)[:, None])
    lamim_col = np.ascontiguousarray(np.repeat(lam_im, 2)[:, None])
    wre_arr = np.empty((128, 512), np.float32)
    wim_arr = np.empty((128, 512), np.float32)
    wr = W_re.reshape(512, 2, 64)
    wi = W_im.reshape(512, 2, 64)
    for par in range(2):
        wre_arr[par::2, :] = wr[:, par, :].T
        wim_arr[par::2, :] = wi[:, par, :].T
    k0t = np.ascontiguousarray(k0[:1024]).view(np.uint8)
    k0b = np.ascontiguousarray(k0[1024:]).view(np.uint8)
    k1u = np.ascontiguousarray(k1).view(np.uint8)
    k2u = np.ascontiguousarray(k2).view(np.uint8)
    svec = np.broadcast_to(np.array([s0, s1, s2], np.float32), (128, 3)).copy()
    D_rep = np.broadcast_to(D, (128, H)).copy()

    consts = _host_consts()
    shared = dict(logdt_arr=logdt_arr, lamre_col=lamre_col, lamim_col=lamim_col,
                  wre_arr=wre_arr, wim_arr=wim_arr, k0t=k0t, k0b=k0b, k1=k1u,
                  k2=k2u, svec=svec, D_rep=D_rep)
    for nm, arr in consts.items():
        shared['c_' + nm] = np.ascontiguousarray(arr)

    in_maps = []
    for c in range(8):
        b, half = c // 2, c % 2
        tok0 = half * 4096
        if half == 0:
            halo = np.zeros((512, H), np.float32)
        else:
            halo = u[b, tok0 - 512:tok0]
        shard = np.concatenate([halo, u[b, tok0:tok0 + 4096]], axis=0)
        m = dict(shared)
        m['u_shard'] = np.ascontiguousarray(shard, dtype=np.float32)
        in_maps.append(m)
    return in_maps


_NC_CACHE = None


def kernel(**inputs):
    global _NC_CACHE
    if _NC_CACHE is None:
        _NC_CACHE = build_nc()
    nc = _NC_CACHE
    in_maps = _prep_inputs(inputs)
    res = run_bass_kernel_spmd(nc, in_maps, core_ids=list(range(8)))
    outs = [np.asarray(res.results[c]['out']) for c in range(8)]
    full = np.stack([np.concatenate([outs[2 * b], outs[2 * b + 1]], axis=0)
                     for b in range(4)], axis=0)
    return full.astype(np.float32)


# version 12
# speedup vs baseline: 1.1866x; 1.0190x over previous
from contextlib import ExitStack

import numpy as np
import ml_dtypes

import concourse.bass as bass
import concourse.mybir as mybir
from concourse.bass_utils import run_bass_kernel_spmd

F32 = mybir.dt.float32
BF16 = mybir.dt.bfloat16
I32 = mybir.dt.int32
U8 = mybir.dt.uint8
AF = mybir.ActivationFunctionType
OP = mybir.AluOpType
AXX = mybir.AxisListType.X

bf16 = ml_dtypes.bfloat16

H = 1024
TOK = 4096
NBLK = 8
NPAIR = 8
EPS = 1e-5
R1024 = float(np.float32(1.0 / 1024.0))

CHI = float(np.float32(2 * np.pi))
CLO = float(np.float32(2 * np.pi - np.float64(np.float32(2 * np.pi))))
I2HI = float(np.float32(1.0 / (2 * np.pi)))
I2LO = float(np.float32(1.0 / (2 * np.pi) - np.float64(np.float32(1.0 / (2 * np.pi)))))
PHHI = float(np.float32(np.pi / 2))
PHLO = float(np.float32(np.pi / 2 - np.float64(np.float32(np.pi / 2))))
PI_SAFE = float(np.nextafter(np.float32(np.pi), np.float32(0)))

DMA_SEMS = ('dC', 'dKa', 'dKb', 'dOa', 'dOb', 'dU0', 'dU1', 'dSp', 'dX',
            'dW0a', 'dW0b', 'dW1a', 'dW1b', 'dW2a', 'dW2b', 'dW2c', 'dW2d',
            'dOut')
ENG_SEMS = ('sT', 'sV', 'sA')


def _split_const(M64):
    m32 = M64.astype(np.float32)
    corr = (M64 - m32.astype(np.float64)).astype(bf16)
    return m32, corr


def _host_consts():
    t = np.arange(512, dtype=np.float64)
    f = np.arange(512, dtype=np.float64)
    ang = 2 * np.pi * np.outer(t, f) / 1024.0
    C32, Ccorr = _split_const(np.cos(ang))
    S32, Scorr = _split_const(-np.sin(ang))
    t2 = np.arange(512, dtype=np.float64)
    wf = np.where(f == 0, 1.0, 2.0)
    angi = 2 * np.pi * np.outer(f, 512 + t2) / 1024.0
    Ci32, Cicorr = _split_const(wf[:, None] * np.cos(angi) / 1024.0)
    Si32, Sicorr = _split_const(-wf[:, None] * np.sin(angi) / 1024.0)
    altcol = ((-1.0) ** np.arange(128, dtype=np.float64)).astype(np.float32)[:, None]
    S32[:, 0] = ((-1.0) ** t).astype(np.float32)
    Scorr[:, 0] = 0
    Si32[0, :] = (((-1.0) ** t2) / 1024.0).astype(np.float32)
    Sicorr[0, :] = 0
    iota32 = np.broadcast_to(np.arange(32, dtype=np.float32), (128, 32)).copy()
    iota16 = np.broadcast_to(np.arange(16, dtype=np.float32) * 32.0, (128, 16)).copy()
    ident = np.eye(128, dtype=np.float32)
    par = (np.arange(128) & 1)
    Mplus = np.zeros((128, 254), np.float32)
    Mplus[:, 126] = (par == 0)
    Mplus[:, 127] = (par == 1)
    Mminus = -Mplus
    cpool = np.broadcast_to(np.array([-1.0, 1.0, 2.0, EPS], np.float32),
                            (128, 4)).copy()
    return dict(C32=C32, Ccorr=Ccorr, S32=S32, Scorr=Scorr,
                Ci32=Ci32, Cicorr=Cicorr, Si32=Si32, Sicorr=Sicorr,
                altcol=altcol, iota32=iota32, iota16=iota16,
                ident=ident, Mplus=Mplus, Mminus=Mminus, cpool=cpool)


class Sched:
    ENGS = ('t', 'v', 's', 'y', 'g')

    def __init__(self):
        self.ops = {e: [] for e in self.ENGS}
        self.cnt = {s: 0 for s in DMA_SEMS + ENG_SEMS}
        self.lastwait = {}

    def wait(self, eng, sem, val):
        if val is None or val <= 0:
            return
        key = (eng, sem)
        if self.lastwait.get(key, -1) >= val:
            return
        self.lastwait[key] = val
        self.ops[eng].append(('w', sem, val))

    def waitc(self, eng, ck):
        if ck:
            self.wait(eng, ck[0], ck[1])

    def emit(self, eng, fn, inc=None):
        if inc is not None:
            sname, k = inc
            self.cnt[sname] += k
            post = self.cnt[sname]
            self.ops[eng].append(('i', fn, sname, k))
            return post
        self.ops[eng].append(('e', fn))
        return None

    def run(self, eng, proxy, sems):
        for op in self.ops[eng]:
            if op[0] == 'w':
                proxy.wait_ge(sems[op[1]], op[2])
            elif op[0] == 'e':
                op[1](proxy)
            else:
                op[1](proxy).then_inc(sems[op[2]], op[3])


class Arena:
    def __init__(self, handle, nf32):
        self.h = handle
        self.n = nf32
        self.off = 0

    @staticmethod
    def _shape(ap, shape):
        if len(shape) == 3:
            return ap.rearrange("p (a b) -> p a b", b=shape[2])
        if len(shape) == 4:
            return ap.rearrange("p (a b c) -> p a b c", b=shape[2], c=shape[3])
        return ap

    def f32(self, shape):
        n = int(np.prod(shape[1:]))
        ap = self.h[0:shape[0], self.off:self.off + n]
        self.off += n
        assert self.off <= self.n, f"arena overflow {self.off} > {self.n}"
        return self._shape(ap, shape)

    def bf(self, shape):
        n = int(np.prod(shape[1:]))
        slots = (n + 1) // 2
        ap = self.h[0:shape[0], self.off:self.off + slots].bitcast(BF16)
        self.off += slots
        assert self.off <= self.n, f"arena overflow {self.off} > {self.n}"
        return self._shape(ap, shape)


ARENA_N = 44500


def build_nc():
    nc = bass.Bass("TRN2", target_bir_lowering=False,
                   detect_race_conditions=False)

    d_u = nc.dram_tensor("u_shard", [TOK + 512, H], F32, kind="ExternalInput").ap()
    d_logdt = nc.dram_tensor("logdt_arr", [128, 512], F32, kind="ExternalInput").ap()
    d_lamre = nc.dram_tensor("lamre_col", [128, 1], F32, kind="ExternalInput").ap()
    d_lamim = nc.dram_tensor("lamim_col", [128, 1], F32, kind="ExternalInput").ap()
    d_wre = nc.dram_tensor("wre_arr", [128, 512], F32, kind="ExternalInput").ap()
    d_wim = nc.dram_tensor("wim_arr", [128, 512], F32, kind="ExternalInput").ap()
    d_k0t = nc.dram_tensor("k0t", [1024, 4096], U8, kind="ExternalInput").ap()
    d_k0b = nc.dram_tensor("k0b", [1024, 4096], U8, kind="ExternalInput").ap()
    d_k1 = nc.dram_tensor("k1", [4096, 4096], U8, kind="ExternalInput").ap()
    d_k2 = nc.dram_tensor("k2", [4096, 1024], U8, kind="ExternalInput").ap()
    d_svec = nc.dram_tensor("svec", [128, 3], F32, kind="ExternalInput").ap()
    d_Drep = nc.dram_tensor("D_rep", [128, H], F32, kind="ExternalInput").ap()
    CONSTS = _host_consts()
    dc = {}
    for name, arr in CONSTS.items():
        dt = BF16 if arr.dtype == bf16 else F32
        dc[name] = nc.dram_tensor("c_" + name, list(arr.shape), dt,
                                  kind="ExternalInput").ap()
    d_out = nc.dram_tensor("out", [TOK, H], F32, kind="ExternalOutput").ap()
    d_dbg = None
    d_x1 = nc.dram_tensor("x1scr", [TOK, H], F32, kind="Internal").ap()
    d_w0 = nc.dram_tensor("w0scr", [32, 128, 8, 128], BF16, kind="Internal").ap()
    d_w1 = nc.dram_tensor("w1scr", [32, 128, 32, 128], BF16, kind="Internal").ap()
    d_w2 = nc.dram_tensor("w2scr", [32, 128, 1024], BF16, kind="Internal").ap()

    import os
    KCUT = os.environ.get('KCUT', '')
    KDBG = os.environ.get('KDBG') == '1'

    sch = Sched()
    import os as _os
    if _os.environ.get('KDBG') == '1':
        d_dbg = nc.dram_tensor("dbg", [16, 128, 1024], F32,
                               kind="ExternalOutput").ap()

    with ExitStack() as _st:
        def sb(name, shape, dtype):
            return _st.enter_context(nc.sbuf_tensor(name, shape, dtype))

        sb_cpool = sb("cn_cpool", [128, 4], F32)
        nc.const_aps.register(F32, -1.0, sb_cpool[:, 0:1])
        nc.const_aps.register(F32, 1.0, sb_cpool[:, 1:2])
        nc.const_aps.register(F32, 2.0, sb_cpool[:, 2:3])
        nc.const_aps.register(F32, EPS, sb_cpool[:, 3:4])
        sb_altcol = sb("cn_altcol", [128, 1], F32)
        sb_iota32 = sb("cn_iota32", [128, 32], F32)
        sb_iota16 = sb("cn_iota16", [128, 16], F32)
        sb_ident = sb("cn_ident", [128, 128], F32)
        sb_mplus = sb("cn_mplus", [128, 254], F32)
        sb_mminus = sb("cn_mminus", [128, 254], F32)
        sb_logdt = sb("in_logdt", [128, 512], F32)
        sb_lamre = sb("in_lamre", [128, 1], F32)
        sb_lamim = sb("in_lamim", [128, 1], F32)
        sb_wre = sb("in_wre", [128, 512], F32)
        sb_wim = sb("in_wim", [128, 512], F32)
        sb_svec = sb("in_svec", [128, 3], F32)
        sb_drep = sb("in_drep", [128, H], F32)
        sb_ab = sb("t_ab", [128, 1024], F32)
        sb_alpha = sb("t_alpha", [128, 1], F32)
        sb_dp1 = sb("t_dp1", [128, H], F32)
        sb_dump = sb("t_dump", [128, 1024], F32)
        sb_stats = sb("t_stats", [128, 16], F32)
        sb_mi = sb("t_mi", [128, 1024], I32)
        sb_arena = sb("arena", [128, ARENA_N], F32)
        ppA = _st.enter_context(nc.psum_tensor("ppA", [128, 512], F32))
        ppB = _st.enter_context(nc.psum_tensor("ppB", [128, 512], F32))
        ppC = _st.enter_context(nc.psum_tensor("ppC", [128, 512], F32))
        ppD = _st.enter_context(nc.psum_tensor("ppD", [128, 512], F32))
        pM = [_st.enter_context(nc.psum_tensor(f"pM{i}", [128, 512], F32))
              for i in range(4)]
        pM0 = pM[0]
        sems = {}
        for s in DMA_SEMS + ENG_SEMS:
            sems[s] = _st.enter_context(nc.semaphore(s))
        blk = _st.enter_context(nc.Block())

        ar = Arena(sb_arena, ARENA_N)

        aC32 = ar.f32([128, 4, 512])
        aS32 = ar.f32([128, 4, 512])
        aCc = ar.bf([128, 4, 512])
        aSc = ar.bf([128, 4, 512])
        aCi32 = ar.f32([128, 4, 512])
        aSi32 = ar.f32([128, 4, 512])
        aCic = ar.bf([128, 4, 512])
        aSic = ar.bf([128, 4, 512])
        aKfRe = ar.f32([128, 4, 1024])
        aKfIm = ar.f32([128, 4, 1024])
        aKfX = ar.f32([128, 2, 1024])
        p1_base = ar.off

        aU = [ar.f32([128, 4, 1024]) for _ in range(2)]
        aUbf = ar.bf([128, 8, 512])
        aUf = ar.f32([128, 8, 512])
        aYbf = ar.bf([128, 8, 512])
        aZX = ar.f32([128, 4, 512])
        aZasm = ar.f32([128, 1024])
        aScr = ar.f32([128, 4, 512])
        p1_end = ar.off

        ar.off = p1_base
        cvA = ar.f32([128, 1024]).bitcast(U8)
        cvB = ar.f32([128, 1024]).bitcast(U8)
        cvAb = ar.f32([128, 2048]).bitcast(BF16)
        cvBb = ar.f32([128, 2048]).bitcast(BF16)

        ar.off = p1_base
        g = {}
        for nm, w in (('ApRe', 1024), ('ApIm', 1024), ('EA', 1024), ('snA', 1024),
                      ('csA', 1024), ('tA', 1024), ('tB', 1024), ('argI', 1024),
                      ('BRe', 512), ('BIm', 512), ('EB', 512), ('snB', 512),
                      ('csB', 512), ('SS1', 512), ('SS2', 512), ('khl', 512),
                      ('SS1b', 512), ('SS2b', 512)):
            g[nm] = ar.f32([128, w])
        aKlh = ar.f32([128, 4, 1024])
        aKbf = ar.bf([128, 4, 1024])
        assert ar.off <= p1_end, (ar.off, p1_end)
        ar.off = p1_end

        ar.off = 0
        aH1hi = ar.bf([128, 32, 512])
        aH1lo = ar.bf([128, 32, 512])
        aH2Bhi = ar.bf([128, 32, 256])
        aH2Blo = ar.bf([128, 32, 256])
        aH2f = ar.f32([128, 512])
        aH2Ahi = ar.bf([128, 2, 256])
        aH2Alo = ar.bf([128, 2, 256])
        aX1tm = ar.f32([128, 4, 1024])
        aX1hi = ar.bf([128, 8, 512])
        aX1lo = ar.bf([128, 8, 512])
        aW0 = ar.bf([128, 2, 8, 128])
        w1_off = ar.off
        aW1 = ar.bf([128, 2, 32, 128])
        aW1_f32 = sb_arena[0:128, w1_off:w1_off + 4096]
        aW2 = ar.bf([128, 4, 1024])
        aG1f = ar.f32([128, 512])
        aZ2 = ar.f32([128, 1024])
        assert ar.off <= ARENA_N, ar.off

        E = sch.emit
        W = sch.wait
        WC = sch.waitc

        def dma(q, sem, dst, src):
            post = E(q, lambda e, d=dst, r=src: e.dma_start(out=d, in_=r),
                     inc=(sem, 16))
            return (sem, post)

        dma('y', 'dC', aC32, dc['C32'].rearrange("(a p) f -> p a f", p=128))
        dma('y', 'dC', aS32, dc['S32'].rearrange("(a p) f -> p a f", p=128))
        dma('y', 'dC', aCc, dc['Ccorr'].rearrange("(a p) f -> p a f", p=128))
        dma('y', 'dC', aSc, dc['Scorr'].rearrange("(a p) f -> p a f", p=128))
        dma('y', 'dC', aCi32, dc['Ci32'].rearrange("(a p) f -> p a f", p=128))
        dma('y', 'dC', aSi32, dc['Si32'].rearrange("(a p) f -> p a f", p=128))
        dma('y', 'dC', aCic, dc['Cicorr'].rearrange("(a p) f -> p a f", p=128))
        dma('y', 'dC', aSic, dc['Sicorr'].rearrange("(a p) f -> p a f", p=128))
        for nm, dst in (('cpool', sb_cpool), ('altcol', sb_altcol),
                        ('iota32', sb_iota32), ('iota16', sb_iota16),
                        ('ident', sb_ident), ('Mplus', sb_mplus),
                        ('Mminus', sb_mminus)):
            dma('y', 'dC', dst[:, :], dc[nm])
        for src, dst in ((d_logdt, sb_logdt), (d_lamre, sb_lamre), (d_lamim, sb_lamim),
                         (d_wre, sb_wre), (d_wim, sb_wim), (d_svec, sb_svec),
                         (d_Drep, sb_drep)):
            dma('y', 'dC', dst[:, :], src)
        ck_consts = ('dC', sch.cnt['dC'])

        a_col = sb_ab[:, 0:512]
        b_col = sb_ab[:, 512:1024]
        WC('s', ck_consts)
        E('s', lambda e: e.activation(sb_logdt[:, :], sb_logdt[:, :], AF.Exp))
        E('s', lambda e: e.activation(sb_lamre[:, :], sb_lamre[:, :], AF.Exp))
        ck = E('s', lambda e: e.activation(sb_dp1[:, :], sb_drep[:, :], AF.Identity,
                                           bias=1.0), inc=('sA', 1))
        W('v', 'sA', ck)
        E('v', lambda e: e.tensor_mul(sb_stats[:, 0:1], sb_svec[:, 0:1],
                                      sb_svec[:, 1:2]))
        E('v', lambda e: e.tensor_scalar(a_col, sb_logdt[:, :], sb_lamre[:, 0:1],
                                         -1.0, OP.mult, OP.mult))
        E('v', lambda e: e.tensor_scalar(sb_stats[:, 1:2], sb_svec[:, 2:3], 8.0,
                                         None, OP.mult))
        E('v', lambda e: e.tensor_scalar(b_col, sb_logdt[:, :], sb_lamim[:, 0:1],
                                         None, OP.mult))
        E('v', lambda e: e.tensor_mul(sb_alpha[:, :], sb_stats[:, 0:1],
                                      sb_stats[:, 1:2]), inc=('sV', 1))

        if KCUT == 'p0a':
            W('g', 'sV', sch.cnt['sV'])
            ck_dummy = dma('g', 'dOut', d_out[0:128, :], sb_dump[:, :])
            WC('g', ck_dummy)
            return _finish(sch, blk, sems, nc)

        ck_w1out = {}
        ck_w1cvt = {}
        for ic in range(32):
            srcu, ksem = (cvA, 'dKa') if ic % 2 == 0 else (cvB, 'dKb')
            dstb, osem = (cvAb, 'dOa') if ic % 2 == 0 else (cvBb, 'dOb')
            W('y', 'sA', ck_w1cvt.get(ic - 2, 0))
            ckl = dma('y', ksem, srcu[:, 0:4096], d_k1[ic * 128:(ic + 1) * 128, :])
            WC('s', ckl)
            WC('s', ck_w1out.get(ic - 2))
            cka = E('s', lambda e, d=dstb, r=srcu: e.activation(
                d[:, 0:4096], r[:, 0:4096], AF.Identity, bias=-1.0, scale=2.0),
                inc=('sA', 1))
            ck_w1cvt[ic] = cka
            W('g', 'sA', cka)
            for ocg in range(4):
                ck_w1out[ic] = dma(
                    'g', osem,
                    d_w1[ocg * 8:(ocg + 1) * 8, :, ic, :].rearrange(
                        "o q c -> q o c"),
                    dstb[:, ocg * 1024:(ocg + 1) * 1024].rearrange(
                        "p (o c) -> p o c", c=128))
        ck_w0out = {}
        ck_w0cvt = 0
        for ic in range(8):
            W('y', 'sV', ck_w0cvt)
            if ic == 0:
                W('y', 'sA', ck_w1cvt[31])
            ckl0 = dma('y', 'dKa', cvA[:, 0:4096], d_k0t[ic * 128:(ic + 1) * 128, :])
            ckl1 = dma('y', 'dKb', cvB[:, 0:4096], d_k0b[ic * 128:(ic + 1) * 128, :])
            WC('v', ckl0)
            WC('v', ckl1)
            if ic == 0:
                WC('v', ck_w1out[30])
            WC('v', ck_w0out.get(ic - 1))
            ckv = E('v', lambda e: e.tensor_sub(cvAb[:, 0:4096], cvA[:, 0:4096],
                                                cvB[:, 0:4096]), inc=('sV', 1))
            ck_w0cvt = ckv
            W('g', 'sV', ckv)
            for ocg in range(4):
                ck_w0out[ic] = dma(
                    'g', 'dOa',
                    d_w0[ocg * 8:(ocg + 1) * 8, :, ic, :].rearrange(
                        "o q c -> q o c"),
                    cvAb[:, ocg * 1024:(ocg + 1) * 1024].rearrange(
                        "p (o c) -> p o c", c=128))
        ck_w2out = {}
        ck_w2cvt = 0
        for ic in range(32):
            W('y', 'sA', ck_w2cvt)
            if ic == 0:
                W('y', 'sV', ck_w0cvt)
            ckl = dma('y', 'dKa', cvA[:, 0:1024], d_k2[ic * 128:(ic + 1) * 128, :])
            WC('s', ckl)
            if ic == 0:
                WC('s', ck_w1out[31])
            WC('s', ck_w2out.get(ic - 1))
            cka = E('s', lambda e: e.activation(cvBb[:, 0:1024], cvA[:, 0:1024],
                                                AF.Identity, bias=-1.0, scale=2.0),
                    inc=('sA', 1))
            ck_w2cvt = cka
            W('g', 'sA', cka)
            ck_w2out[ic] = dma('g', 'dOb', d_w2[ic, :, :], cvBb[:, 0:1024])
        ck_oa_total = ('dOa', sch.cnt['dOa'])
        ck_ob_total = ('dOb', sch.cnt['dOb'])

        if KCUT == 'p0b':
            WC('g', ck_oa_total)
            WC('g', ck_ob_total)
            ck_dummy = dma('g', 'dOut', d_out[0:128, :], sb_dump[:, :])
            WC('g', ck_dummy)
            return _finish(sch, blk, sems, nc)

        def r3(ap, b):
            return ap.rearrange("p (a b) -> p a b", b=b)

        def sincos(argI, sn_out, cs_out, width):
            tA = g['tA'][:, 0:width]
            tB = g['tB'][:, 0:width]
            mi = sb_mi[:, 0:width]
            sg = sb_dump[:, 0:width]
            cks = None
            for mode in (0, 1):
                if mode == 0:
                    E('v', lambda e: e.tensor_scalar(tA, argI, I2LO, None, OP.mult))
                else:
                    W('v', 'sA', cks)
                    E('v', lambda e: e.tensor_scalar(tA, argI, I2LO, 0.25,
                                                     OP.mult, OP.add))
                E('v', lambda e: e.scalar_tensor_tensor(
                    tA, argI, I2HI, tA, OP.mult, OP.add))
                E('v', lambda e: e.tensor_copy(mi, tA))
                E('v', lambda e: e.tensor_copy(tB, mi))
                E('v', lambda e: e.tensor_sub(tA, tA, tB))
                E('v', lambda e: e.tensor_single_scalar(sg, tA, 0.5, OP.is_gt))
                E('v', lambda e: e.tensor_single_scalar(tA, tA, -0.5, OP.is_lt))
                E('v', lambda e: e.tensor_add(tB, tB, sg))
                E('v', lambda e: e.tensor_sub(tB, tB, tA))
                E('v', lambda e: e.scalar_tensor_tensor(tA, tB, -CHI, argI,
                                                        OP.mult, OP.add))
                E('v', lambda e: e.scalar_tensor_tensor(tA, tB, -CLO, tA,
                                                        OP.mult, OP.add))
                if mode == 1:
                    E('v', lambda e: e.tensor_scalar(tA, tA, PHHI, PHLO,
                                                     OP.add, OP.add))
                ckv = E('v', lambda e: e.tensor_scalar(tA, tA, PI_SAFE, -PI_SAFE,
                                                       OP.min, OP.max), inc=('sV', 1))
                W('s', 'sV', ckv)
                dst = sn_out if mode == 0 else cs_out
                cks = E('s', lambda e, d=dst: e.activation(d, tA, AF.Sin),
                        inc=('sA', 1))
            return cks

        WC('v', ck_oa_total)
        WC('v', ck_ob_total)
        WC('s', ck_oa_total)
        WC('s', ck_ob_total)

        KHG = int(os.environ.get('KHG', '16'))
        ck_ss_mm = 0
        ck_ssmm_hist = {}
        ck_klh_v = 0
        for hg in range(KHG):
            j0 = hg * 32
            a_sl = a_col[:, j0:j0 + 32].unsqueeze(2).broadcast_to([128, 32, 32])
            b_sl = b_col[:, j0:j0 + 32].unsqueeze(2).broadcast_to([128, 32, 32])
            wre_sl = sb_wre[:, j0:j0 + 32].unsqueeze(2).broadcast_to([128, 32, 32])
            wim_sl = sb_wim[:, j0:j0 + 32].unsqueeze(2).broadcast_to([128, 32, 32])
            i32b = sb_iota32[:, :].unsqueeze(1).broadcast_to([128, 32, 32])
            W('v', 'sA', sch.cnt['sA'])
            ckv = E('v', lambda e, a=a_sl, i=i32b: e.tensor_tensor(
                r3(g['ApRe'], 32), a, i, OP.mult), inc=('sV', 1))
            W('s', 'sV', ckv)
            E('s', lambda e: e.activation(g['EA'], g['ApRe'], AF.Exp))
            E('v', lambda e, b=b_sl, i=i32b: e.tensor_tensor(
                r3(g['argI'], 32), b, i, OP.mult))
            cks = sincos(g['argI'], g['snA'], g['csA'], 1024)
            W('v', 'sA', cks)
            E('v', lambda e, w=wre_sl: e.tensor_tensor(r3(g['tA'], 32),
                                                       r3(g['csA'], 32), w, OP.mult))
            E('v', lambda e, w=wim_sl: e.tensor_tensor(r3(g['tB'], 32),
                                                       r3(g['snA'], 32), w, OP.mult))
            E('v', lambda e: e.tensor_sub(g['tA'], g['tA'], g['tB']))
            E('v', lambda e: e.tensor_mul(g['ApRe'], g['tA'], g['EA']))
            E('v', lambda e, w=wre_sl: e.tensor_tensor(r3(g['tA'], 32),
                                                       r3(g['snA'], 32), w, OP.mult))
            E('v', lambda e, w=wim_sl: e.tensor_tensor(r3(g['tB'], 32),
                                                       r3(g['csA'], 32), w, OP.mult))
            E('v', lambda e: e.tensor_add(g['tA'], g['tA'], g['tB']))
            E('v', lambda e: e.tensor_mul(g['ApIm'], g['tA'], g['EA']))
            a_sb = a_col[:, j0:j0 + 32].unsqueeze(2).broadcast_to([128, 32, 16])
            b_sb = b_col[:, j0:j0 + 32].unsqueeze(2).broadcast_to([128, 32, 16])
            i16b = sb_iota16[:, :].unsqueeze(1).broadcast_to([128, 32, 16])
            ckv = E('v', lambda e, a=a_sb, i=i16b: e.tensor_tensor(
                r3(g['BRe'], 16), a, i, OP.mult), inc=('sV', 1))
            W('s', 'sV', ckv)
            E('s', lambda e: e.activation(g['EB'], g['BRe'], AF.Exp))
            E('v', lambda e, b=b_sb, i=i16b: e.tensor_tensor(
                r3(g['argI'][:, 0:512], 16), b, i, OP.mult))
            cks = sincos(g['argI'][:, 0:512], g['snB'], g['csB'], 512)
            W('v', 'sA', cks)
            E('v', lambda e: e.tensor_mul(g['BRe'], g['EB'], g['csB']))
            ckB = E('v', lambda e: e.tensor_mul(g['BIm'], g['EB'], g['snB']),
                    inc=('sV', 1))
            kp = ppA if (hg // 2) % 2 == 0 else ppB
            ApRe_j = r3(g['ApRe'], 32)
            ApIm_j = r3(g['ApIm'], 32)
            BRe_j = r3(g['BRe'], 16)
            BIm_j = r3(g['BIm'], 16)
            for jl in range(32):
                jfull = (hg % 2) * 32 + jl
                jg = hg * 32 + jl
                s1n = 'SS1' if jg % 2 == 0 else 'SS1b'
                s2n = 'SS2' if jg % 2 == 0 else 'SS2b'
                ss1 = r3(g[s1n], 32)
                ss2 = r3(g[s2n], 32)
                apre = ApRe_j[:, jl, :].unsqueeze(1).broadcast_to([128, 16, 32])
                apim = ApIm_j[:, jl, :].unsqueeze(1).broadcast_to([128, 16, 32])
                bre = BRe_j[:, jl, :].unsqueeze(2).broadcast_to([128, 16, 32])
                bim = BIm_j[:, jl, :].unsqueeze(2).broadcast_to([128, 16, 32])
                W('v', 'sV', ckB)
                W('v', 'sT', ck_ssmm_hist.get(jg - 2, 0))
                E('v', lambda e, a=apre, b=bre, o=ss1: e.tensor_tensor(o, a, b, OP.mult))
                ckss = E('v', lambda e, a=apim, b=bim, o=ss2: e.tensor_tensor(
                    o, a, b, OP.mult), inc=('sV', 1))
                W('t', 'sV', ckss)
                c0 = 126 - 2 * jfull
                E('t', lambda e, p=kp, c=c0, s1=s1n: e.matmul(
                    p[:, :], sb_mplus[:, c:c + 128], g[s1],
                    start=(c == 126), stop=False))
                ck_ss_mm = E('t', lambda e, p=kp, c=c0, s2=s2n,
                             lst=(jfull == 63): e.matmul(
                    p[:, :], sb_mminus[:, c:c + 128], g[s2],
                    start=False, stop=lst), inc=('sT', 1))
                ck_ssmm_hist[jg] = ck_ss_mm
            if hg % 2 == 1:
                grp = hg // 2
                W('v', 'sT', ck_ss_mm)
                ckv = E('v', lambda e, p=kp: e.tensor_copy(g['khl'], p[:, :]),
                        inc=('sV', 1))
                khl2 = r3(g['khl'], 128)
                ck_tev = {}
                for lc in range(4):
                    pt = ppC if lc % 2 == 0 else ppD
                    W('t', 'sV', ckv)
                    W('t', 'sV', ck_tev.get(lc - 2, 0))
                    ckt = E('t', lambda e, p=pt, l=lc: e.transpose(
                        p[:, 0:128], khl2[:, l, :], sb_ident[:, :]), inc=('sT', 1))
                    W('v', 'sT', ckt)
                    E('v', lambda e, p=pt, l=lc, gg=grp: e.tensor_copy(
                        aKlh[:, l, gg * 128:(gg + 1) * 128], p[:, 0:128]))
                    ck_klh_v = E('v', lambda e, p=pt, l=lc, gg=grp: e.tensor_copy(
                        aKbf[:, l, gg * 128:(gg + 1) * 128], p[:, 0:128]),
                        inc=('sV', 1))
                    ck_tev[lc] = ck_klh_v
        ck_k_done = ck_klh_v

        if KCUT == 'p0c':
            W('g', 'sV', sch.cnt['sV'])
            W('g', 'sT', sch.cnt['sT'])
            ck_dummy = dma('g', 'dOut', d_out[0:128, :], sb_dump[:, :])
            WC('g', ck_dummy)
            return _finish(sch, blk, sems, nc)

        ck_kf_ev = {}
        for comp in range(2):
            dst = aKfRe if comp == 0 else aKfIm
            c32v, ccv = (aC32, aCc) if comp == 0 else (aS32, aSc)
            for fc in range(4):
                for hh in range(2):
                    idx = comp * 8 + fc * 2 + hh
                    pp = ppA if idx % 2 == 0 else ppB
                    W('t', 'sV', ck_k_done)
                    W('t', 'sV', ck_kf_ev.get(idx - 2, 0))
                    for lc in range(4):
                        E('t', lambda e, p=pp, l=lc, f=fc, h=hh, cv=c32v: e.matmul(
                            p[:, :], cv[:, l, f * 128:(f + 1) * 128],
                            aKlh[:, l, h * 512:(h + 1) * 512],
                            start=(l == 0), stop=False))
                    ckt = None
                    for lc in range(4):
                        ckt = E('t', lambda e, p=pp, l=lc, f=fc, h=hh, cv=ccv:
                                e.matmul(
                                    p[:, :], cv[:, l, f * 128:(f + 1) * 128],
                                    aKbf[:, l, h * 512:(h + 1) * 512],
                                    start=False, stop=(l == 3)),
                                inc=('sT', 1) if lc == 3 else None)
                    W('v', 'sT', ckt)
                    ck_kf_ev[idx] = E(
                        'v', lambda e, p=pp, f=fc, h=hh, d=dst: e.tensor_copy(
                            d[:, f, h * 512:(h + 1) * 512], p[:, :]), inc=('sV', 1))
        ck_last_kf = max(ck_kf_ev.values())
        W('v', 'sV', ck_last_kf)
        E('v', lambda e: e.tensor_copy(aKfX[:, 0, :], aKfIm[:, 0, :]))
        E('v', lambda e: e.memset(aKfX[0:1, 0, :], 0.0))
        E('v', lambda e: e.tensor_copy(aKfX[:, 1, :], aKfRe[:, 0, :]))
        E('v', lambda e: e.tensor_copy(aKfX[0:1, 1, :], aKfIm[0:1, 0, :]),
          inc=('sV', 1))
        ck_p0_v = sch.cnt['sV']
        ck_p0_t = sch.cnt['sT']
        if KDBG:
            W('g', 'sV', ck_p0_v)
            W('g', 'sT', ck_p0_t)
            WC('g', ck_oa_total)
            WC('g', ck_ob_total)
            ckw1t = dma('g', 'dKb', cvAb.rearrange("p (a b) -> p a b", b=128),
                        d_w1[1, :, :, :])
            WC('g', ckw1t)
            dma('g', 'dOut', d_dbg[3, :, :],
                cvAb.bitcast(F32)[:, 0:1024])
            dma('g', 'dOut', d_dbg[4, :, :],
                cvAb.bitcast(F32)[:, 1024:2048])
            dma('g', 'dOut', d_dbg[0, :, :], aKfRe[:, 0, :])
            dma('g', 'dOut', d_dbg[1, :, :], aKfIm[:, 0, :])
            dma('g', 'dOut', d_dbg[2, :, :], aKfX[:, 0, :])
            dma('g', 'dOut', d_dbg[5, :, 0:1024], sb_ab[:, :])
            ckdx = dma('g', 'dOut', d_dbg[6, :, 0:512], sb_wre[:, :])
            WC('t', ckdx)

        if KCUT == 'p0':
            W('g', 'sV', ck_p0_v)
            W('g', 'sT', ck_p0_t)
            ck_dummy = dma('g', 'dOut', d_out[0:128, :], sb_dump[:, :])
            WC('g', ck_dummy)
            return _finish(sch, blk, sems, nc)

        ck_uh = {}

        def load_uhalf(h, ck_t=None, ck_s=None, ck_v=None):
            if h > NBLK:
                return
            if h < 2:
                W('g', 'sV', ck_p0_v)
                W('g', 'sT', ck_p0_t)
            else:
                W('g', 'sT', sch.cnt['sT'] if ck_t is None else ck_t)
                W('g', 'sA', sch.cnt['sA'] if ck_s is None else ck_s)
                if ck_v is not None:
                    W('g', 'sV', ck_v)
            ck_uh[h] = dma('g', 'dU0' if h % 2 == 0 else 'dU1', aU[h % 2],
                           d_u[h * 512:(h + 1) * 512, :].rearrange(
                               "(a p) c -> p a c", p=128))

        load_uhalf(0)
        load_uhalf(1)

        def ln_emit(tile, outtile):
            st = sb_stats
            ckv = E('v', lambda e, ti=tile: e.tensor_reduce(st[:, 0:1], ti, AXX,
                                                            OP.add), inc=('sV', 1))
            W('s', 'sV', ckv)
            cks = E('s', lambda e, ti=tile: e.activation(sb_dump[:, :], ti, AF.Square,
                                                         accum_out=st[:, 1:2]),
                    inc=('sA', 1))
            W('v', 'sA', cks)
            E('v', lambda e: e.tensor_scalar(st[:, 2:3], st[:, 0:1], R1024,
                                             None, OP.mult))
            E('v', lambda e: e.tensor_scalar(st[:, 3:4], st[:, 1:2], R1024,
                                             None, OP.mult))
            ckv = E('v', lambda e: e.tensor_scalar(st[:, 8:9], st[:, 0:1], -R1024,
                                                   None, OP.mult), inc=('sV', 1))
            W('s', 'sV', ckv)
            cks = E('s', lambda e: e.activation(st[:, 4:5], st[:, 2:3], AF.Square),
                    inc=('sA', 1))
            W('v', 'sA', cks)
            ckv = E('v', lambda e: e.tensor_sub(st[:, 5:6], st[:, 3:4], st[:, 4:5]),
                    inc=('sV', 1))
            W('s', 'sV', ckv)
            cks = E('s', lambda e: e.activation(st[:, 6:7], st[:, 5:6], AF.Sqrt,
                                                bias=EPS), inc=('sA', 1))
            W('v', 'sA', cks)
            ckv = E('v', lambda e: e.reciprocal(st[:, 7:8], st[:, 6:7]),
                    inc=('sV', 1))
            W('s', 'sV', ckv)
            cks = E('s', lambda e: e.activation(st[:, 9:10], st[:, 8:9], AF.Copy,
                                                scale=st[:, 7:8]), inc=('sA', 1))
            W('v', 'sA', cks)
            return E('v', lambda e, ti=tile, o=outtile: e.scalar_tensor_tensor(
                o, ti, st[:, 7:8], st[:, 9:10].broadcast_to([128, 1024]),
                OP.mult, OP.add), inc=('sV', 1))

        ck_fwd_last = 0
        ck_inv_ev = 0
        zbufs = (aZasm, sb_ab)
        spill_ck = {}
        pend_ln = None
        ztile = 0

        def flush_ln():
            nonlocal pend_ln
            ti, buf = pend_ln
            ckln = ln_emit(buf[:, :], buf[:, :])
            W('g', 'sV', ckln)
            spill_ck[ti] = dma('g', 'dSp',
                               d_x1[ti * 128:(ti + 1) * 128, :], buf[:, :])
            pend_ln = None

        for blkid in range(NBLK):
            hA, hB = blkid, blkid + 1
            uA, uB = aU[hA % 2], aU[hB % 2]
            ck_inv_prev = ck_inv_ev
            ck_ubf_last = 0

            for chh in range(2):
                def usl(tc, uA=uA, uB=uB, c=chh):
                    return (uA if tc < 4 else uB)[:, tc % 4, c * 512:(c + 1) * 512]

                WC('s', ck_uh[hA])
                WC('s', ck_uh[hB])
                W('s', 'sT', ck_fwd_last)
                ck_ubf = None
                for tc in range(8):
                    ck_ubf = E('s', lambda e, t=tc, us=usl: e.activation(
                        aUbf[:, t, :], us(t), AF.Copy),
                        inc=('sA', 1) if tc == 7 else None)
                ck_ubf_last = ck_ubf

                ck_yf_v = None
                ck_yf_s = None

                def emit_pw(fc, c=chh):
                    nonlocal ck_yf_v, ck_yf_s
                    sl = slice(c * 512, (c + 1) * 512)
                    kre = aKfRe[:, fc, sl]
                    kim = aKfX[:, 0, sl] if fc == 0 else aKfIm[:, fc, sl]
                    kreI = aKfX[:, 1, sl] if fc == 0 else aKfRe[:, fc, sl]
                    ure = aUf[:, fc, :]
                    uim = aUf[:, 4 + fc, :]
                    E('v', lambda e, k=kreI, u=uim: e.tensor_tensor(
                        aScr[:, 0, :], k, u, OP.mult))
                    E('v', lambda e, k=kim, u=ure: e.tensor_tensor(
                        aScr[:, 1, :], k, u, OP.mult))
                    E('v', lambda e, k=kre, u=ure: e.tensor_tensor(
                        aScr[:, 2, :], k, u, OP.mult))
                    E('v', lambda e, k=kim, u=uim: e.tensor_tensor(
                        aScr[:, 3, :], k, u, OP.mult))
                    E('v', lambda e, o=ure: e.tensor_sub(o, aScr[:, 2, :],
                                                         aScr[:, 3, :]))
                    ck_yf_v = E('v', lambda e, o=uim: e.tensor_add(
                        o, aScr[:, 0, :], aScr[:, 1, :]), inc=('sV', 1))
                    W('s', 'sV', ck_yf_v)
                    E('s', lambda e, o=ure, f=fc: e.activation(
                        aYbf[:, f, :], o, AF.Copy))
                    ck_yf_s = E('s', lambda e, o=uim, f=fc: e.activation(
                        aYbf[:, 4 + f, :], o, AF.Copy), inc=('sA', 1))

                WC('t', ck_uh[hA])
                WC('t', ck_uh[hB])
                W('t', 'sA', ck_ubf)
                W('t', 'sV', ck_inv_ev)
                ck_uf_ev = {}
                for comp in range(2):
                    c32v, ccv = (aC32, aCc) if comp == 0 else (aS32, aSc)
                    for fc in range(4):
                        gi = comp * 4 + fc
                        g1 = ppA if gi % 2 == 0 else ppC
                        g2 = ppB if gi % 2 == 0 else ppD
                        W('t', 'sV', ck_uf_ev.get(gi - 2, 0))
                        ckt = None
                        for tc in range(4):
                            for half in range(2):
                                pp = g1 if half == 0 else g2
                                E('t', lambda e, p=pp, t=tc, hf=half, f=fc, us=usl,
                                  cv=c32v: e.matmul(
                                    p[:, :], cv[:, t, f * 128:(f + 1) * 128],
                                    us(t + 4 * hf), start=(t == 0), stop=False))
                        for tc in range(4):
                            for half in range(2):
                                pp = g1 if half == 0 else g2
                                ckt = E('t', lambda e, p=pp, t=tc, hf=half, f=fc,
                                        cv=ccv: e.matmul(
                                            p[:, :], cv[:, t, f * 128:(f + 1) * 128],
                                            aUbf[:, t + 4 * hf, :],
                                            start=False, stop=(t == 3)),
                                        inc=('sT', 1) if (half == 1 and tc == 3)
                                        else None)
                        W('v', 'sT', ckt)
                        E('v', lambda e, p=g2: e.tensor_copy(aScr[:, 0, :], p[:, :]))
                        ck_uf_ev[gi] = E(
                            'v', lambda e, p=g1, sl=gi: e.scalar_tensor_tensor(
                                aUf[:, sl, :], aScr[:, 0, :], sb_altcol[:, 0:1],
                                p[:, :], OP.mult, OP.add), inc=('sV', 1))
                        if comp == 1:
                            emit_pw(fc)
                ck_fwd_last = sch.cnt['sT']
                if chh == 1:
                    load_uhalf(blkid + 2, ck_t=ck_fwd_last, ck_s=ck_ubf_last,
                               ck_v=ck_inv_prev)

                for tb in range(4):
                    pp = ppA if tb % 2 == 0 else ppB
                    W('t', 'sV', ck_yf_v)
                    W('t', 'sA', ck_yf_s)
                    W('t', 'sV', ck_inv_ev)
                    for comp in range(2):
                        civ = aCi32 if comp == 0 else aSi32
                        for fc in range(4):
                            E('t', lambda e, p=pp, f=fc, t=tb, cv=civ, cp=comp:
                              e.matmul(
                                p[:, :], cv[:, f, t * 128:(t + 1) * 128],
                                aUf[:, cp * 4 + f, :], start=(cp == 0 and f == 0),
                                stop=False))
                    ckt = None
                    for comp in range(2):
                        cicv = aCic if comp == 0 else aSic
                        for fc in range(4):
                            islast = (comp == 1 and fc == 3)
                            ckt = E('t', lambda e, p=pp, f=fc, t=tb, cv=cicv, cp=comp,
                                    lst=islast: e.matmul(
                                        p[:, :], cv[:, f, t * 128:(t + 1) * 128],
                                        aYbf[:, cp * 4 + f, :], start=False, stop=lst),
                                    inc=('sT', 1) if islast else None)
                    W('v', 'sT', ckt)
                    E('v', lambda e, t=tb, c=chh, u=uB: e.tensor_tensor(
                        aScr[:, 0, :], u[:, t, c * 512:(c + 1) * 512],
                        sb_dp1[:, c * 512:(c + 1) * 512], OP.mult))
                    if chh == 0:
                        ck_inv_ev = E('v', lambda e, p=pp, t=tb: e.tensor_add(
                            aZX[:, t, :], aScr[:, 0, :], p[:, :]), inc=('sV', 1))
                    else:
                        buf = zbufs[ztile % 2]
                        WC('v', spill_ck.get(ztile - 2))
                        E('v', lambda e, t=tb, b=buf: e.tensor_copy(
                            b[:, 0:512], aZX[:, t, :]))
                        ck_inv_ev = E('v', lambda e, p=pp, b=buf: e.tensor_add(
                            b[:, 512:1024], aScr[:, 0, :], p[:, :]),
                            inc=('sV', 1))
                        if pend_ln is not None:
                            flush_ln()
                        pend_ln = (ztile, buf)
                        ztile += 1
        flush_ln()

        if KCUT == 'p1':
            W('g', 'dSp', sch.cnt['dSp'])
            ck_dummy = dma('g', 'dOut', d_out[0:128, :], sb_dump[:, :])
            WC('g', ck_dummy)
            return _finish(sch, blk, sems, nc)

        ck_bar = {s: sch.cnt[s] for s in DMA_SEMS + ENG_SEMS}
        for eng in ('t', 'v', 's', 'y', 'g'):
            for sname, val in ck_bar.items():
                W(eng, sname, val)

        W0SEM = ('dW0a', 'dW0b')
        W1SEM = ('dW1a', 'dW1b')
        W2SEM = ('dW2a', 'dW2b', 'dW2c', 'dW2d')
        ck_w0_use = {}
        ck_w1_use = {}
        ck_w2_use = {}
        ck_w2b_use = {}
        ck_z2b = 0
        ck_out = {}
        def load_w0(p, oc, ck_w0_load, gate):
            W('y', 'sT', gate)
            ck_w0_load[oc] = dma('y', W0SEM[oc % 2], aW0[:, oc % 2, :, :],
                                 d_w0[oc, :, :, :])

        import os as _os2
        W1Q = _os2.environ.get('KW1Q', 'y')
        assert W1Q in ('y', 'g', 's')
        W1R1 = _os2.environ.get('KW1R1') == '1'

        def load_w1(p, oc, ck_w1_load, gate):
            W(W1Q, 'sT', gate)
            slot = 0 if W1R1 else oc % 2
            ck_w1_load[oc] = dma(
                W1Q, W1SEM[oc % 2],
                aW1_f32[:, slot * 2048:(slot + 1) * 2048],
                d_w1[oc, :, :, :].rearrange("q a b -> q (a b)").bitcast(F32))

        def load_w2(p, oc, ck_w2_load, gate):
            W('y', 'sT', gate)
            ck_w2_load[oc] = dma('y', W2SEM[oc % 4], aW2[:, oc % 4, :],
                                 d_w2[oc, :, :])

        ck_x1_half = {}
        pend_tr = {}

        def emit_transposes(i0, i1, ck_x1t, ck_tr):
            for i in range(i0, i1):
                tc, fc = divmod(i, 8)
                pt = ppC if i % 2 == 0 else ppD
                W('t', 'sV', ck_tr.get(i - 2, 0))
                ckt = E('t', lambda e, q=pt, t=tc, f=fc: e.transpose(
                    q[:, 0:128], aX1tm[:, t, f * 128:(f + 1) * 128],
                    sb_ident[:, :]), inc=('sT', 1))
                W('s', 'sT', ckt)
                ck_x1t[i] = E('s', lambda e, q=pt, t=tc, f=fc: e.activation(
                    aX1hi[:, f, t * 128:(t + 1) * 128], q[:, 0:128], AF.Copy),
                    inc=('sA', 1))
                W('v', 'sA', ck_x1t[i])
                ck_tr[i] = E('v', lambda e, q=pt, t=tc, f=fc: e.tensor_sub(
                    aX1lo[:, f, t * 128:(t + 1) * 128], q[:, 0:128],
                    aX1hi[:, f, t * 128:(t + 1) * 128]), inc=('sV', 1))

        def load_x1_half(p, half, gate_out=None):
            if gate_out is not None:
                WC('y', gate_out)
            r0 = p * 512 + half * 256
            ck_x1_half[(p, half)] = dma(
                'y', 'dU0' if half == 0 else 'dU1',
                aX1tm[:, half * 2:half * 2 + 2, :],
                d_x1[r0:r0 + 256, :].rearrange("(a q) c -> q a c", q=128))

        for p in range(NPAIR):
            if p == 0:
                load_x1_half(0, 0)
                load_x1_half(0, 1)
            ck_w0_load = {}
            ck_w1_load = {}
            ck_w2_load = {}
            ck_w2b_load = {}
            for oc in range(2):
                load_w0(p, oc, ck_w0_load, ck_w0_use.get((p - 1, 30 + oc), 0))

            if p == 0:
                ck_x1t = {}
                ck_tr = {}
                WC('t', ck_x1_half[(0, 0)])
                emit_transposes(0, 16, ck_x1t, ck_tr)
            else:
                ck_x1t, ck_tr = pend_tr.pop(p)
            WC('t', ck_x1_half[(p, 1)])
            emit_transposes(16, 32, ck_x1t, ck_tr)
            ck_tr_last = ck_tr[31]

            ck_h1_ev = {}
            for oc in range(32):
                pp = ppA if oc % 2 == 0 else ppB
                W('t', 'sV', ck_h1_ev.get(oc - 2, 0))
                WC('t', ck_w0_load[oc])
                W('t', 'sV', ck_tr_last)
                ckt = None
                for ic in range(8):
                    E('t', lambda e, q=pp, o=oc, i=ic: e.matmul(
                        q[:, :], aW0[:, o % 2, i, :], aX1hi[:, i, :],
                        start=(i == 0), stop=False))
                    ckt = E('t', lambda e, q=pp, o=oc, i=ic: e.matmul(
                        q[:, :], aW0[:, o % 2, i, :], aX1lo[:, i, :],
                        start=False, stop=(i == 7)),
                        inc=('sT', 1) if ic == 7 else None)
                ck_w0_use[(p, oc)] = ckt
                if oc + 2 <= 31:
                    load_w0(p, oc + 2, ck_w0_load, ckt)
                elif oc == 30:
                    if W1R1:
                        load_w1(p, 0, ck_w1_load, ck_w1_use.get((p - 1, 31), 0))
                    else:
                        for o2 in range(2):
                            load_w1(p, o2, ck_w1_load,
                                    ck_w1_use.get((p - 1, 30 + o2), 0))
                    for o2 in range(4):
                        load_w2(p, o2, ck_w2_load,
                                ck_w2b_use.get((p - 1, 28 + o2), 0))
                W('s', 'sT', ckt)
                W('s', 'sV', ck_h1_ev.get(oc - 1, 0))
                cka = E('s', lambda e, q=pp: e.activation(aG1f[:, :], q[:, :],
                                                          AF.Relu), inc=('sA', 1))
                W('v', 'sA', cka)
                E('v', lambda e, o=oc: e.tensor_copy(aH1hi[:, o, :], aG1f[:, :]))
                ck_h1_ev[oc] = E('v', lambda e, o=oc: e.tensor_sub(
                    aH1lo[:, o, :], aG1f[:, :], aH1hi[:, o, :]), inc=('sV', 1))
            ck_h1_all = ck_h1_ev[31]
            if KDBG and p == 0:
                W('g', 'sV', ck_h1_all)
                h1flat = aH1hi.rearrange("q a b -> q (a b)").bitcast(F32)
                for di in range(4):
                    dma('g', 'dOut', d_dbg[11 + di, :, :],
                        h1flat[:, di * 1024:(di + 1) * 1024])
                x1flat = aX1hi.rearrange("q a b -> q (a b)").bitcast(F32)
                ckdx = dma('g', 'dOut', d_dbg[15, :, 0:1024], x1flat[:, 0:1024])
                WC('t', ckdx)

            ck_h2a_ev = {}
            ck_h2b_ev = {}
            ck_mm2a = {}

            def emit_mm2a(oc):
                sl = oc % 2
                W('t', 'sV', ck_h2a_ev[oc])
                WC('t', ck_w2_load[oc])
                if oc == 0:
                    W('t', 'sV', ck_z2b)
                ckt = None
                for hilo in range(2):
                    src = aH2Ahi if hilo == 0 else aH2Alo
                    for tb in range(2):
                        for chh in range(2):
                            ckt = E('t', lambda e, s2=sl, hl=hilo, t=tb, c=chh, o=oc,
                                    sr=src: e.matmul(
                                        pM[t * 2 + c][:, :],
                                        sr[:, s2, t * 128:(t + 1) * 128],
                                        aW2[:, o % 4, c * 512:(c + 1) * 512],
                                        start=(o == 0 and hl == 0),
                                        stop=(o == 31 and hl == 1)),
                                    inc=('sT', 1) if (hilo == 1 and tb == 1 and
                                                      chh == 1) else None)
                ck_mm2a[oc] = ckt
                ck_w2_use[(p, oc)] = ckt
                if oc + 4 <= 31:
                    load_w2(p, oc + 4, ck_w2_load, ckt)
                elif oc == 31:
                    for o2 in range(4):
                        W('y', 'sT', ck_w2_use[(p, 28 + o2)])
                        ck_w2b_load[o2] = dma('y', W2SEM[o2 % 4], aW2[:, o2 % 4, :],
                                              d_w2[o2, :, :])

            for oc in range(32):
                pp = ppA if oc % 2 == 0 else ppB
                sl = oc % 2
                W('t', 'sV', ck_h2a_ev.get(oc - 2, 0))
                W('t', 'sV', ck_h1_all)
                WC('t', ck_w1_load[oc])
                ckt = None
                w1slot = 0 if W1R1 else oc % 2
                for ic in range(32):
                    E('t', lambda e, q=pp, s2=w1slot, i=ic: e.matmul(
                        q[:, :], aW1[:, s2, i, :], aH1hi[:, i, :],
                        start=(i == 0), stop=False))
                    ckt = E('t', lambda e, q=pp, s2=w1slot, i=ic: e.matmul(
                        q[:, :], aW1[:, s2, i, :], aH1lo[:, i, :],
                        start=False, stop=(i == 31)),
                        inc=('sT', 1) if ic == 31 else None)
                ck_w1_use[(p, oc)] = ckt
                if W1R1 and oc + 1 <= 31:
                    load_w1(p, oc + 1, ck_w1_load, ckt)
                elif (not W1R1) and oc + 2 <= 31:
                    load_w1(p, oc + 2, ck_w1_load, ckt)
                W('s', 'sT', ckt)
                W('s', 'sV', ck_h2b_ev.get(oc - 1, 0))
                cka = E('s', lambda e, q=pp: e.activation(aH2f[:, :], q[:, :],
                                                          AF.Relu), inc=('sA', 1))
                W('v', 'sA', cka)
                W('v', 'sT', ck_mm2a.get(oc - 2, 0))
                E('v', lambda e, s2=sl: e.tensor_copy(aH2Ahi[:, s2, :],
                                                      aH2f[:, 0:256]))
                ck_h2a_ev[oc] = E('v', lambda e, s2=sl: e.tensor_sub(
                    aH2Alo[:, s2, :], aH2f[:, 0:256], aH2Ahi[:, s2, :]),
                    inc=('sV', 1))
                E('v', lambda e, o=oc: e.tensor_copy(aH2Bhi[:, o, :],
                                                     aH2f[:, 256:512]))
                ck_h2b_ev[oc] = E('v', lambda e, o=oc: e.tensor_sub(
                    aH2Blo[:, o, :], aH2f[:, 256:512], aH2Bhi[:, o, :]),
                    inc=('sV', 1))
                if oc >= 1:
                    emit_mm2a(oc - 1)
            emit_mm2a(31)

            ck_z2a = 0
            ck_z2a_stt = 0
            for tb in range(2):
                W('v', 'sT', ck_mm2a[31])
                if p > 0:
                    WC('v', ck_out.get((p - 1, 1)))
                ck_stt = None
                for chh in range(2):
                    ck_stt = E('v', lambda e, t=tb, c=chh: e.scalar_tensor_tensor(
                        aZ2[:, c * 512:(c + 1) * 512], pM[t * 2 + c][:, :],
                        sb_alpha[:, 0:1], aX1tm[:, t, c * 512:(c + 1) * 512],
                        OP.mult, OP.add),
                        inc=('sV', 1) if (chh == 1 and
                                          (tb == 1 or (KDBG and p == 0)))
                        else None)
                if tb == 1:
                    ck_z2a_stt = ck_stt
                if KDBG and p == 0 and tb == 0:
                    W('s', 'sV', ck_stt)
                    E('s', lambda e: e.activation(sb_dump[:, 0:512],
                                                  pM[0][:, :], AF.Copy))
                    ckps = E('s', lambda e: e.activation(sb_dump[:, 512:1024],
                                                         pM[1][:, :], AF.Copy),
                             inc=('sA', 1))
                    W('g', 'sA', ckps)
                    dma('g', 'dOut', d_dbg[7, :, :], aZ2[:, :])
                    dma('g', 'dOut', d_dbg[8, :, 0:512], sb_dump[:, 0:512])
                    E('g', lambda e: e.dma_start(
                        out=d_dbg[8, :, 512:513], in_=sb_alpha[:, 0:1],
                        allow_slow_non_contiguous=True), inc=('dOut', 16))
                    dma('g', 'dOut', d_dbg[9, :, 0:512], sb_dump[:, 512:1024])
                    ckdz = dma('g', 'dOut', d_dbg[10, :, :], aX1tm[:, 0, :])
                    WC('v', ckdz)
                    WC('t', ckdz)
                    WC('s', ckdz)
                ck_z2a = ln_emit(aZ2[:, :], aX1tm[:, tb, :])
            W('g', 'sV', ck_z2a)
            ck_out[(p, 0)] = dma('g', 'dOut',
                                 d_out[p * 512:p * 512 + 256, :].rearrange(
                                     "(a q) c -> q a c", q=128), aX1tm[:, 0:2, :])
            if p + 1 < NPAIR:
                load_x1_half(p + 1, 0, gate_out=ck_out[(p, 0)])

            ck_mm2b = 0
            for oc in range(32):
                W('t', 'sV', ck_z2a_stt)
                WC('t', ck_w2b_load[oc])
                W('t', 'sV', ck_h2b_ev[31])
                ckt = None
                for hilo in range(2):
                    src = aH2Bhi if hilo == 0 else aH2Blo
                    for tb in range(2):
                        for chh in range(2):
                            ckt = E('t', lambda e, hl=hilo, t=tb, c=chh, o=oc,
                                    sr=src: e.matmul(
                                        pM[t * 2 + c][:, :],
                                        sr[:, o, t * 128:(t + 1) * 128],
                                        aW2[:, o % 4, c * 512:(c + 1) * 512],
                                        start=(o == 0 and hl == 0),
                                        stop=(o == 31 and hl == 1)),
                                    inc=('sT', 1) if (hilo == 1 and tb == 1 and
                                                      chh == 1) else None)
                ck_mm2b = ckt
                ck_w2b_use[(p, oc)] = ckt
                if oc + 4 <= 31:
                    W('y', 'sT', ckt)
                    ck_w2b_load[oc + 4] = dma('y', W2SEM[(oc + 4) % 4],
                                              aW2[:, (oc + 4) % 4, :],
                                              d_w2[oc + 4, :, :])
                if oc == 16 and p + 1 < NPAIR:
                    ntr_x1t = {}
                    ntr_tr = {}
                    WC('t', ck_x1_half[(p + 1, 0)])
                    emit_transposes(0, 16, ntr_x1t, ntr_tr)
                    pend_tr[p + 1] = (ntr_x1t, ntr_tr)
            for tb in range(2):
                W('v', 'sT', ck_mm2b)
                WC('v', ck_out[(p, 0)])
                for chh in range(2):
                    E('v', lambda e, t=tb, c=chh: e.scalar_tensor_tensor(
                        aZ2[:, c * 512:(c + 1) * 512], pM[t * 2 + c][:, :],
                        sb_alpha[:, 0:1], aX1tm[:, 2 + t, c * 512:(c + 1) * 512],
                        OP.mult, OP.add))
                ck_z2b = ln_emit(aZ2[:, :], aX1tm[:, 2 + tb, :])
            W('g', 'sV', ck_z2b)
            ck_out[(p, 1)] = dma('g', 'dOut',
                                 d_out[p * 512 + 256:(p + 1) * 512, :].rearrange(
                                     "(a q) c -> q a c", q=128), aX1tm[:, 2:4, :])
            if p + 1 < NPAIR:
                load_x1_half(p + 1, 1, gate_out=ck_out[(p, 1)])

        W('g', 'dOut', sch.cnt['dOut'])
        return _finish(sch, blk, sems, nc)


def _finish(sch, blk, sems, nc):
    @blk.tensor
    def _(eng):
        sch.run('t', eng, sems)

    @blk.vector
    def _(eng):
        sch.run('v', eng, sems)

    @blk.scalar
    def _(eng):
        sch.run('s', eng, sems)

    @blk.sync
    def _(eng):
        sch.run('y', eng, sems)

    @blk.gpsimd
    def _(eng):
        sch.run('g', eng, sems)

    return nc


def _prep_inputs(inputs):
    u = np.asarray(inputs['L0_attn_u'])
    D = np.ascontiguousarray(np.asarray(inputs['L0_attn_D'], dtype=np.float32))
    lam_re = np.asarray(inputs['L0_attn_Lambda_re'], dtype=np.float32)
    lam_im = np.asarray(inputs['L0_attn_Lambda_im'], dtype=np.float32)
    log_dt = np.asarray(inputs['L0_attn_log_dt'], dtype=np.float32)
    W_re = np.asarray(inputs['L0_attn_W_re'], dtype=np.float32)
    W_im = np.asarray(inputs['L0_attn_W_im'], dtype=np.float32)
    k0 = np.asarray(inputs['L0_ffn_k0'])
    k1 = np.asarray(inputs['L0_ffn_k1'])
    k2 = np.asarray(inputs['L0_ffn_k2'])
    s0 = np.float32(np.asarray(inputs['L0_ffn_s0']))
    s1 = np.float32(np.asarray(inputs['L0_ffn_s1']))
    s2 = np.float32(np.asarray(inputs['L0_ffn_s2']))

    logdt_arr = np.empty((128, 512), np.float32)
    ld = log_dt.reshape(512, 2)
    logdt_arr[0::2, :] = ld[:, 0][None, :]
    logdt_arr[1::2, :] = ld[:, 1][None, :]
    lamre_col = np.ascontiguousarray(np.repeat(lam_re, 2)[:, None])
    lamim_col = np.ascontiguousarray(np.repeat(lam_im, 2)[:, None])
    wre_arr = np.empty((128, 512), np.float32)
    wim_arr = np.empty((128, 512), np.float32)
    wr = W_re.reshape(512, 2, 64)
    wi = W_im.reshape(512, 2, 64)
    for par in range(2):
        wre_arr[par::2, :] = wr[:, par, :].T
        wim_arr[par::2, :] = wi[:, par, :].T
    k0t = np.ascontiguousarray(k0[:1024]).view(np.uint8)
    k0b = np.ascontiguousarray(k0[1024:]).view(np.uint8)
    k1u = np.ascontiguousarray(k1).view(np.uint8)
    k2u = np.ascontiguousarray(k2).view(np.uint8)
    svec = np.broadcast_to(np.array([s0, s1, s2], np.float32), (128, 3)).copy()
    D_rep = np.broadcast_to(D, (128, H)).copy()

    consts = _host_consts()
    shared = dict(logdt_arr=logdt_arr, lamre_col=lamre_col, lamim_col=lamim_col,
                  wre_arr=wre_arr, wim_arr=wim_arr, k0t=k0t, k0b=k0b, k1=k1u,
                  k2=k2u, svec=svec, D_rep=D_rep)
    for nm, arr in consts.items():
        shared['c_' + nm] = np.ascontiguousarray(arr)

    in_maps = []
    for c in range(8):
        b, half = c // 2, c % 2
        tok0 = half * 4096
        if half == 0:
            halo = np.zeros((512, H), np.float32)
        else:
            halo = u[b, tok0 - 512:tok0]
        shard = np.concatenate([halo, u[b, tok0:tok0 + 4096]], axis=0)
        m = dict(shared)
        m['u_shard'] = np.ascontiguousarray(shard, dtype=np.float32)
        in_maps.append(m)
    return in_maps


_NC_CACHE = None


def kernel(**inputs):
    global _NC_CACHE
    if _NC_CACHE is None:
        _NC_CACHE = build_nc()
    nc = _NC_CACHE
    in_maps = _prep_inputs(inputs)
    res = run_bass_kernel_spmd(nc, in_maps, core_ids=list(range(8)))
    outs = [np.asarray(res.results[c]['out']) for c in range(8)]
    full = np.stack([np.concatenate([outs[2 * b], outs[2 * b + 1]], axis=0)
                     for b in range(4)], axis=0)
    return full.astype(np.float32)


# version 13
# speedup vs baseline: 1.2066x; 1.0168x over previous
from contextlib import ExitStack

import numpy as np
import ml_dtypes

import concourse.bass as bass
import concourse.mybir as mybir
from concourse.bass_utils import run_bass_kernel_spmd

F32 = mybir.dt.float32
BF16 = mybir.dt.bfloat16
I32 = mybir.dt.int32
U8 = mybir.dt.uint8
AF = mybir.ActivationFunctionType
OP = mybir.AluOpType
AXX = mybir.AxisListType.X

bf16 = ml_dtypes.bfloat16

H = 1024
TOK = 4096
NBLK = 8
NPAIR = 8
EPS = 1e-5
R1024 = float(np.float32(1.0 / 1024.0))

CHI = float(np.float32(2 * np.pi))
CLO = float(np.float32(2 * np.pi - np.float64(np.float32(2 * np.pi))))
I2HI = float(np.float32(1.0 / (2 * np.pi)))
I2LO = float(np.float32(1.0 / (2 * np.pi) - np.float64(np.float32(1.0 / (2 * np.pi)))))
PHHI = float(np.float32(np.pi / 2))
PHLO = float(np.float32(np.pi / 2 - np.float64(np.float32(np.pi / 2))))
PI_SAFE = float(np.nextafter(np.float32(np.pi), np.float32(0)))

DMA_SEMS = ('dC', 'dKa', 'dKb', 'dOa', 'dOb', 'dU0', 'dU1', 'dSp', 'dX',
            'dW0a', 'dW0b', 'dW1a', 'dW1b', 'dW2a', 'dW2b', 'dW2c', 'dW2d',
            'dOut')
ENG_SEMS = ('sT', 'sV', 'sA')


def _split_const(M64):
    m32 = M64.astype(np.float32)
    corr = (M64 - m32.astype(np.float64)).astype(bf16)
    return m32, corr


def _host_consts():
    t = np.arange(512, dtype=np.float64)
    f = np.arange(512, dtype=np.float64)
    ang = 2 * np.pi * np.outer(t, f) / 1024.0
    C32, Ccorr = _split_const(np.cos(ang))
    S32, Scorr = _split_const(-np.sin(ang))
    t2 = np.arange(512, dtype=np.float64)
    wf = np.where(f == 0, 1.0, 2.0)
    angi = 2 * np.pi * np.outer(f, 512 + t2) / 1024.0
    Ci32, Cicorr = _split_const(wf[:, None] * np.cos(angi) / 1024.0)
    Si32, Sicorr = _split_const(-wf[:, None] * np.sin(angi) / 1024.0)
    altcol = ((-1.0) ** np.arange(128, dtype=np.float64)).astype(np.float32)[:, None]
    S32[:, 0] = ((-1.0) ** t).astype(np.float32)
    Scorr[:, 0] = 0
    Si32[0, :] = (((-1.0) ** t2) / 1024.0).astype(np.float32)
    Sicorr[0, :] = 0
    iota32 = np.broadcast_to(np.arange(32, dtype=np.float32), (128, 32)).copy()
    iota16 = np.broadcast_to(np.arange(16, dtype=np.float32) * 32.0, (128, 16)).copy()
    ident = np.eye(128, dtype=np.float32)
    par = (np.arange(128) & 1)
    Mplus = np.zeros((128, 254), np.float32)
    Mplus[:, 126] = (par == 0)
    Mplus[:, 127] = (par == 1)
    Mminus = -Mplus
    cpool = np.broadcast_to(np.array([-1.0, 1.0, 2.0, EPS], np.float32),
                            (128, 4)).copy()
    return dict(C32=C32, Ccorr=Ccorr, S32=S32, Scorr=Scorr,
                Ci32=Ci32, Cicorr=Cicorr, Si32=Si32, Sicorr=Sicorr,
                altcol=altcol, iota32=iota32, iota16=iota16,
                ident=ident, Mplus=Mplus, Mminus=Mminus, cpool=cpool)


class Sched:
    ENGS = ('t', 'v', 's', 'y', 'g')

    def __init__(self):
        self.ops = {e: [] for e in self.ENGS}
        self.cnt = {s: 0 for s in DMA_SEMS + ENG_SEMS}
        self.lastwait = {}

    def wait(self, eng, sem, val):
        if val is None or val <= 0:
            return
        key = (eng, sem)
        if self.lastwait.get(key, -1) >= val:
            return
        self.lastwait[key] = val
        self.ops[eng].append(('w', sem, val))

    def waitc(self, eng, ck):
        if ck:
            self.wait(eng, ck[0], ck[1])

    def emit(self, eng, fn, inc=None):
        if inc is not None:
            sname, k = inc
            self.cnt[sname] += k
            post = self.cnt[sname]
            self.ops[eng].append(('i', fn, sname, k))
            return post
        self.ops[eng].append(('e', fn))
        return None

    def run(self, eng, proxy, sems):
        for op in self.ops[eng]:
            if op[0] == 'w':
                proxy.wait_ge(sems[op[1]], op[2])
            elif op[0] == 'e':
                op[1](proxy)
            else:
                op[1](proxy).then_inc(sems[op[2]], op[3])


class Arena:
    def __init__(self, handle, nf32):
        self.h = handle
        self.n = nf32
        self.off = 0

    @staticmethod
    def _shape(ap, shape):
        if len(shape) == 3:
            return ap.rearrange("p (a b) -> p a b", b=shape[2])
        if len(shape) == 4:
            return ap.rearrange("p (a b c) -> p a b c", b=shape[2], c=shape[3])
        return ap

    def f32(self, shape):
        n = int(np.prod(shape[1:]))
        ap = self.h[0:shape[0], self.off:self.off + n]
        self.off += n
        assert self.off <= self.n, f"arena overflow {self.off} > {self.n}"
        return self._shape(ap, shape)

    def bf(self, shape):
        n = int(np.prod(shape[1:]))
        slots = (n + 1) // 2
        ap = self.h[0:shape[0], self.off:self.off + slots].bitcast(BF16)
        self.off += slots
        assert self.off <= self.n, f"arena overflow {self.off} > {self.n}"
        return self._shape(ap, shape)


ARENA_N = 44500


def build_nc():
    nc = bass.Bass("TRN2", target_bir_lowering=False,
                   detect_race_conditions=False)

    d_u = nc.dram_tensor("u_shard", [TOK + 512, H], F32, kind="ExternalInput").ap()
    d_logdt = nc.dram_tensor("logdt_arr", [128, 512], F32, kind="ExternalInput").ap()
    d_lamre = nc.dram_tensor("lamre_col", [128, 1], F32, kind="ExternalInput").ap()
    d_lamim = nc.dram_tensor("lamim_col", [128, 1], F32, kind="ExternalInput").ap()
    d_wre = nc.dram_tensor("wre_arr", [128, 512], F32, kind="ExternalInput").ap()
    d_wim = nc.dram_tensor("wim_arr", [128, 512], F32, kind="ExternalInput").ap()
    d_k0t = nc.dram_tensor("k0t", [1024, 4096], U8, kind="ExternalInput").ap()
    d_k0b = nc.dram_tensor("k0b", [1024, 4096], U8, kind="ExternalInput").ap()
    d_k1 = nc.dram_tensor("k1", [4096, 4096], U8, kind="ExternalInput").ap()
    d_k2 = nc.dram_tensor("k2", [4096, 1024], U8, kind="ExternalInput").ap()
    d_svec = nc.dram_tensor("svec", [128, 3], F32, kind="ExternalInput").ap()
    d_Drep = nc.dram_tensor("D_rep", [128, H], F32, kind="ExternalInput").ap()
    CONSTS = _host_consts()
    dc = {}
    for name, arr in CONSTS.items():
        dt = BF16 if arr.dtype == bf16 else F32
        dc[name] = nc.dram_tensor("c_" + name, list(arr.shape), dt,
                                  kind="ExternalInput").ap()
    d_out = nc.dram_tensor("out", [TOK, H], F32, kind="ExternalOutput").ap()
    d_dbg = None
    d_x1 = nc.dram_tensor("x1scr", [TOK, H], F32, kind="Internal").ap()
    d_w0 = nc.dram_tensor("w0scr", [32, 128, 8, 128], BF16, kind="Internal").ap()
    d_w1 = nc.dram_tensor("w1scr", [32, 128, 32, 128], BF16, kind="Internal").ap()
    d_w2 = nc.dram_tensor("w2scr", [32, 128, 1024], BF16, kind="Internal").ap()

    import os
    KCUT = os.environ.get('KCUT', '')
    KDBG = os.environ.get('KDBG') == '1'

    sch = Sched()
    import os as _os
    if _os.environ.get('KDBG') == '1':
        d_dbg = nc.dram_tensor("dbg", [16, 128, 1024], F32,
                               kind="ExternalOutput").ap()

    with ExitStack() as _st:
        def sb(name, shape, dtype):
            return _st.enter_context(nc.sbuf_tensor(name, shape, dtype))

        sb_cpool = sb("cn_cpool", [128, 4], F32)
        nc.const_aps.register(F32, -1.0, sb_cpool[:, 0:1])
        nc.const_aps.register(F32, 1.0, sb_cpool[:, 1:2])
        nc.const_aps.register(F32, 2.0, sb_cpool[:, 2:3])
        nc.const_aps.register(F32, EPS, sb_cpool[:, 3:4])
        sb_altcol = sb("cn_altcol", [128, 1], F32)
        sb_iota32 = sb("cn_iota32", [128, 32], F32)
        sb_iota16 = sb("cn_iota16", [128, 16], F32)
        sb_ident = sb("cn_ident", [128, 128], F32)
        sb_mplus = sb("cn_mplus", [128, 254], F32)
        sb_mminus = sb("cn_mminus", [128, 254], F32)
        sb_logdt = sb("in_logdt", [128, 512], F32)
        sb_lamre = sb("in_lamre", [128, 1], F32)
        sb_lamim = sb("in_lamim", [128, 1], F32)
        sb_wre = sb("in_wre", [128, 512], F32)
        sb_wim = sb("in_wim", [128, 512], F32)
        sb_svec = sb("in_svec", [128, 3], F32)
        sb_drep = sb("in_drep", [128, H], F32)
        sb_ab = sb("t_ab", [128, 1024], F32)
        sb_alpha = sb("t_alpha", [128, 1], F32)
        sb_dp1 = sb("t_dp1", [128, H], F32)
        sb_dump = sb("t_dump", [128, 1024], F32)
        sb_stats = sb("t_stats", [128, 16], F32)
        sb_mi = sb("t_mi", [128, 1024], I32)
        sb_arena = sb("arena", [128, ARENA_N], F32)
        ppA = _st.enter_context(nc.psum_tensor("ppA", [128, 512], F32))
        ppB = _st.enter_context(nc.psum_tensor("ppB", [128, 512], F32))
        ppC = _st.enter_context(nc.psum_tensor("ppC", [128, 512], F32))
        ppD = _st.enter_context(nc.psum_tensor("ppD", [128, 512], F32))
        pM = [_st.enter_context(nc.psum_tensor(f"pM{i}", [128, 512], F32))
              for i in range(4)]
        pM0 = pM[0]
        sems = {}
        for s in DMA_SEMS + ENG_SEMS:
            sems[s] = _st.enter_context(nc.semaphore(s))
        blk = _st.enter_context(nc.Block())

        ar = Arena(sb_arena, ARENA_N)

        aC32 = ar.f32([128, 4, 512])
        aS32 = ar.f32([128, 4, 512])
        aCc = ar.bf([128, 4, 512])
        aSc = ar.bf([128, 4, 512])
        aCi32 = ar.f32([128, 4, 512])
        aSi32 = ar.f32([128, 4, 512])
        aCic = ar.bf([128, 4, 512])
        aSic = ar.bf([128, 4, 512])
        aKfRe = ar.f32([128, 4, 1024])
        aKfIm = ar.f32([128, 4, 1024])
        aKfX = ar.f32([128, 2, 1024])
        p1_base = ar.off

        aU = [ar.f32([128, 4, 1024]) for _ in range(2)]
        aUbf = ar.bf([128, 8, 512])
        aUf = ar.f32([128, 8, 512])
        aYbf = ar.bf([128, 8, 512])
        aZX = ar.f32([128, 4, 512])
        aZasm = ar.f32([128, 1024])
        aScr = ar.f32([128, 4, 512])
        p1_end = ar.off

        ar.off = p1_base
        cvA = ar.f32([128, 1024]).bitcast(U8)
        cvB = ar.f32([128, 1024]).bitcast(U8)
        cvAb = ar.f32([128, 2048]).bitcast(BF16)
        cvBb = ar.f32([128, 2048]).bitcast(BF16)

        ar.off = p1_base
        g = {}
        for nm, w in (('ApRe', 1024), ('ApIm', 1024), ('EA', 1024), ('snA', 1024),
                      ('csA', 1024), ('tA', 1024), ('tB', 1024), ('argI', 1024),
                      ('BRe', 512), ('BIm', 512), ('EB', 512), ('snB', 512),
                      ('csB', 512), ('SS1', 512), ('SS2', 512), ('khl', 512),
                      ('SS1b', 512), ('SS2b', 512)):
            g[nm] = ar.f32([128, w])
        aKlh = ar.f32([128, 4, 1024])
        aKbf = ar.bf([128, 4, 1024])
        assert ar.off <= p1_end, (ar.off, p1_end)
        ar.off = p1_end

        ar.off = 0
        aH1hi = ar.bf([128, 32, 512])
        aH1lo = ar.bf([128, 32, 512])
        aH2Bhi = ar.bf([128, 32, 256])
        aH2Blo = ar.bf([128, 32, 256])
        aH2f = ar.f32([128, 512])
        aH2Ahi = ar.bf([128, 2, 256])
        aH2Alo = ar.bf([128, 2, 256])
        aX1tm = ar.f32([128, 4, 1024])
        aX1hi = ar.bf([128, 8, 512])
        aX1lo = ar.bf([128, 8, 512])
        aW0 = ar.bf([128, 2, 8, 128])
        w1_off = ar.off
        aW1 = ar.bf([128, 2, 32, 128])
        aW1_f32 = sb_arena[0:128, w1_off:w1_off + 4096]
        aW2 = ar.bf([128, 4, 1024])
        aG1f = ar.f32([128, 512])
        aZ2 = ar.f32([128, 1024])
        assert ar.off <= ARENA_N, ar.off

        E = sch.emit
        W = sch.wait
        WC = sch.waitc

        def dma(q, sem, dst, src):
            post = E(q, lambda e, d=dst, r=src: e.dma_start(out=d, in_=r),
                     inc=(sem, 16))
            return (sem, post)

        dma('y', 'dC', aC32, dc['C32'].rearrange("(a p) f -> p a f", p=128))
        dma('y', 'dC', aS32, dc['S32'].rearrange("(a p) f -> p a f", p=128))
        dma('y', 'dC', aCc, dc['Ccorr'].rearrange("(a p) f -> p a f", p=128))
        dma('y', 'dC', aSc, dc['Scorr'].rearrange("(a p) f -> p a f", p=128))
        dma('y', 'dC', aCi32, dc['Ci32'].rearrange("(a p) f -> p a f", p=128))
        dma('y', 'dC', aSi32, dc['Si32'].rearrange("(a p) f -> p a f", p=128))
        dma('y', 'dC', aCic, dc['Cicorr'].rearrange("(a p) f -> p a f", p=128))
        dma('y', 'dC', aSic, dc['Sicorr'].rearrange("(a p) f -> p a f", p=128))
        for nm, dst in (('cpool', sb_cpool), ('altcol', sb_altcol),
                        ('iota32', sb_iota32), ('iota16', sb_iota16),
                        ('ident', sb_ident), ('Mplus', sb_mplus),
                        ('Mminus', sb_mminus)):
            dma('y', 'dC', dst[:, :], dc[nm])
        for src, dst in ((d_logdt, sb_logdt), (d_lamre, sb_lamre), (d_lamim, sb_lamim),
                         (d_wre, sb_wre), (d_wim, sb_wim), (d_svec, sb_svec),
                         (d_Drep, sb_drep)):
            dma('y', 'dC', dst[:, :], src)
        ck_consts = ('dC', sch.cnt['dC'])

        a_col = sb_ab[:, 0:512]
        b_col = sb_ab[:, 512:1024]
        WC('s', ck_consts)
        E('s', lambda e: e.activation(sb_logdt[:, :], sb_logdt[:, :], AF.Exp))
        E('s', lambda e: e.activation(sb_lamre[:, :], sb_lamre[:, :], AF.Exp))
        ck = E('s', lambda e: e.activation(sb_dp1[:, :], sb_drep[:, :], AF.Identity,
                                           bias=1.0), inc=('sA', 1))
        W('v', 'sA', ck)
        E('v', lambda e: e.tensor_mul(sb_stats[:, 0:1], sb_svec[:, 0:1],
                                      sb_svec[:, 1:2]))
        E('v', lambda e: e.tensor_scalar(a_col, sb_logdt[:, :], sb_lamre[:, 0:1],
                                         -1.0, OP.mult, OP.mult))
        E('v', lambda e: e.tensor_scalar(sb_stats[:, 1:2], sb_svec[:, 2:3], 8.0,
                                         None, OP.mult))
        E('v', lambda e: e.tensor_scalar(b_col, sb_logdt[:, :], sb_lamim[:, 0:1],
                                         None, OP.mult))
        E('v', lambda e: e.tensor_mul(sb_alpha[:, :], sb_stats[:, 0:1],
                                      sb_stats[:, 1:2]), inc=('sV', 1))

        if KCUT == 'p0a':
            W('g', 'sV', sch.cnt['sV'])
            ck_dummy = dma('g', 'dOut', d_out[0:128, :], sb_dump[:, :])
            WC('g', ck_dummy)
            return _finish(sch, blk, sems, nc)

        ck_w1out = {}
        ck_w1cvt = {}
        for ic in range(32):
            srcu, ksem = (cvA, 'dKa') if ic % 2 == 0 else (cvB, 'dKb')
            dstb, osem = (cvAb, 'dOa') if ic % 2 == 0 else (cvBb, 'dOb')
            W('y', 'sA', ck_w1cvt.get(ic - 2, 0))
            ckl = dma('y', ksem, srcu[:, 0:4096], d_k1[ic * 128:(ic + 1) * 128, :])
            WC('s', ckl)
            WC('s', ck_w1out.get(ic - 2))
            cka = E('s', lambda e, d=dstb, r=srcu: e.activation(
                d[:, 0:4096], r[:, 0:4096], AF.Identity, bias=-1.0, scale=2.0),
                inc=('sA', 1))
            ck_w1cvt[ic] = cka
            W('g', 'sA', cka)
            for ocg in range(4):
                ck_w1out[ic] = dma(
                    'g', osem,
                    d_w1[ocg * 8:(ocg + 1) * 8, :, ic, :].rearrange(
                        "o q c -> q o c"),
                    dstb[:, ocg * 1024:(ocg + 1) * 1024].rearrange(
                        "p (o c) -> p o c", c=128))
        ck_w0out = {}
        ck_w0cvt = 0
        for ic in range(8):
            W('y', 'sV', ck_w0cvt)
            if ic == 0:
                W('y', 'sA', ck_w1cvt[31])
            ckl0 = dma('y', 'dKa', cvA[:, 0:4096], d_k0t[ic * 128:(ic + 1) * 128, :])
            ckl1 = dma('y', 'dKb', cvB[:, 0:4096], d_k0b[ic * 128:(ic + 1) * 128, :])
            WC('v', ckl0)
            WC('v', ckl1)
            if ic == 0:
                WC('v', ck_w1out[30])
            WC('v', ck_w0out.get(ic - 1))
            ckv = E('v', lambda e: e.tensor_sub(cvAb[:, 0:4096], cvA[:, 0:4096],
                                                cvB[:, 0:4096]), inc=('sV', 1))
            ck_w0cvt = ckv
            W('g', 'sV', ckv)
            for ocg in range(4):
                ck_w0out[ic] = dma(
                    'g', 'dOa',
                    d_w0[ocg * 8:(ocg + 1) * 8, :, ic, :].rearrange(
                        "o q c -> q o c"),
                    cvAb[:, ocg * 1024:(ocg + 1) * 1024].rearrange(
                        "p (o c) -> p o c", c=128))
        ck_w2out = {}
        ck_w2cvt = 0
        for ic in range(32):
            W('y', 'sA', ck_w2cvt)
            if ic == 0:
                W('y', 'sV', ck_w0cvt)
            ckl = dma('y', 'dKa', cvA[:, 0:1024], d_k2[ic * 128:(ic + 1) * 128, :])
            WC('s', ckl)
            if ic == 0:
                WC('s', ck_w1out[31])
            WC('s', ck_w2out.get(ic - 1))
            cka = E('s', lambda e: e.activation(cvBb[:, 0:1024], cvA[:, 0:1024],
                                                AF.Identity, bias=-1.0, scale=2.0),
                    inc=('sA', 1))
            ck_w2cvt = cka
            W('g', 'sA', cka)
            ck_w2out[ic] = dma('g', 'dOb', d_w2[ic, :, :], cvBb[:, 0:1024])
        ck_oa_total = ('dOa', sch.cnt['dOa'])
        ck_ob_total = ('dOb', sch.cnt['dOb'])

        if KCUT == 'p0b':
            WC('g', ck_oa_total)
            WC('g', ck_ob_total)
            ck_dummy = dma('g', 'dOut', d_out[0:128, :], sb_dump[:, :])
            WC('g', ck_dummy)
            return _finish(sch, blk, sems, nc)

        def r3(ap, b):
            return ap.rearrange("p (a b) -> p a b", b=b)

        def sincos(argI, sn_out, cs_out, width):
            tA = g['tA'][:, 0:width]
            tB = g['tB'][:, 0:width]
            mi = sb_mi[:, 0:width]
            sg = sb_dump[:, 0:width]
            cks = None
            for mode in (0, 1):
                if mode == 0:
                    E('v', lambda e: e.tensor_scalar(tA, argI, I2LO, None, OP.mult))
                else:
                    W('v', 'sA', cks)
                    E('v', lambda e: e.tensor_scalar(tA, argI, I2LO, 0.25,
                                                     OP.mult, OP.add))
                E('v', lambda e: e.scalar_tensor_tensor(
                    tA, argI, I2HI, tA, OP.mult, OP.add))
                E('v', lambda e: e.tensor_copy(mi, tA))
                E('v', lambda e: e.tensor_copy(tB, mi))
                E('v', lambda e: e.tensor_sub(tA, tA, tB))
                E('v', lambda e: e.tensor_single_scalar(sg, tA, 0.5, OP.is_gt))
                E('v', lambda e: e.tensor_single_scalar(tA, tA, -0.5, OP.is_lt))
                E('v', lambda e: e.tensor_add(tB, tB, sg))
                E('v', lambda e: e.tensor_sub(tB, tB, tA))
                E('v', lambda e: e.scalar_tensor_tensor(tA, tB, -CHI, argI,
                                                        OP.mult, OP.add))
                E('v', lambda e: e.scalar_tensor_tensor(tA, tB, -CLO, tA,
                                                        OP.mult, OP.add))
                if mode == 1:
                    E('v', lambda e: e.tensor_scalar(tA, tA, PHHI, PHLO,
                                                     OP.add, OP.add))
                ckv = E('v', lambda e: e.tensor_scalar(tA, tA, PI_SAFE, -PI_SAFE,
                                                       OP.min, OP.max), inc=('sV', 1))
                W('s', 'sV', ckv)
                dst = sn_out if mode == 0 else cs_out
                cks = E('s', lambda e, d=dst: e.activation(d, tA, AF.Sin),
                        inc=('sA', 1))
            return cks

        WC('v', ck_oa_total)
        WC('v', ck_ob_total)
        WC('s', ck_oa_total)
        WC('s', ck_ob_total)

        KHG = int(os.environ.get('KHG', '16'))
        ck_ss_mm = 0
        ck_ssmm_hist = {}
        ck_klh_v = 0
        for hg in range(KHG):
            j0 = hg * 32
            a_sl = a_col[:, j0:j0 + 32].unsqueeze(2).broadcast_to([128, 32, 32])
            b_sl = b_col[:, j0:j0 + 32].unsqueeze(2).broadcast_to([128, 32, 32])
            wre_sl = sb_wre[:, j0:j0 + 32].unsqueeze(2).broadcast_to([128, 32, 32])
            wim_sl = sb_wim[:, j0:j0 + 32].unsqueeze(2).broadcast_to([128, 32, 32])
            i32b = sb_iota32[:, :].unsqueeze(1).broadcast_to([128, 32, 32])
            W('v', 'sA', sch.cnt['sA'])
            ckv = E('v', lambda e, a=a_sl, i=i32b: e.tensor_tensor(
                r3(g['ApRe'], 32), a, i, OP.mult), inc=('sV', 1))
            W('s', 'sV', ckv)
            E('s', lambda e: e.activation(g['EA'], g['ApRe'], AF.Exp))
            E('v', lambda e, b=b_sl, i=i32b: e.tensor_tensor(
                r3(g['argI'], 32), b, i, OP.mult))
            cks = sincos(g['argI'], g['snA'], g['csA'], 1024)
            W('v', 'sA', cks)
            E('v', lambda e, w=wre_sl: e.tensor_tensor(r3(g['tA'], 32),
                                                       r3(g['csA'], 32), w, OP.mult))
            E('v', lambda e, w=wim_sl: e.tensor_tensor(r3(g['tB'], 32),
                                                       r3(g['snA'], 32), w, OP.mult))
            E('v', lambda e: e.tensor_sub(g['tA'], g['tA'], g['tB']))
            E('v', lambda e: e.tensor_mul(g['ApRe'], g['tA'], g['EA']))
            E('v', lambda e, w=wre_sl: e.tensor_tensor(r3(g['tA'], 32),
                                                       r3(g['snA'], 32), w, OP.mult))
            E('v', lambda e, w=wim_sl: e.tensor_tensor(r3(g['tB'], 32),
                                                       r3(g['csA'], 32), w, OP.mult))
            E('v', lambda e: e.tensor_add(g['tA'], g['tA'], g['tB']))
            E('v', lambda e: e.tensor_mul(g['ApIm'], g['tA'], g['EA']))
            a_sb = a_col[:, j0:j0 + 32].unsqueeze(2).broadcast_to([128, 32, 16])
            b_sb = b_col[:, j0:j0 + 32].unsqueeze(2).broadcast_to([128, 32, 16])
            i16b = sb_iota16[:, :].unsqueeze(1).broadcast_to([128, 32, 16])
            ckv = E('v', lambda e, a=a_sb, i=i16b: e.tensor_tensor(
                r3(g['BRe'], 16), a, i, OP.mult), inc=('sV', 1))
            W('s', 'sV', ckv)
            E('s', lambda e: e.activation(g['EB'], g['BRe'], AF.Exp))
            E('v', lambda e, b=b_sb, i=i16b: e.tensor_tensor(
                r3(g['argI'][:, 0:512], 16), b, i, OP.mult))
            cks = sincos(g['argI'][:, 0:512], g['snB'], g['csB'], 512)
            W('v', 'sA', cks)
            E('v', lambda e: e.tensor_mul(g['BRe'], g['EB'], g['csB']))
            ckB = E('v', lambda e: e.tensor_mul(g['BIm'], g['EB'], g['snB']),
                    inc=('sV', 1))
            kp = ppA if (hg // 2) % 2 == 0 else ppB
            ApRe_j = r3(g['ApRe'], 32)
            ApIm_j = r3(g['ApIm'], 32)
            BRe_j = r3(g['BRe'], 16)
            BIm_j = r3(g['BIm'], 16)
            for jl in range(32):
                jfull = (hg % 2) * 32 + jl
                jg = hg * 32 + jl
                s1n = 'SS1' if jg % 2 == 0 else 'SS1b'
                s2n = 'SS2' if jg % 2 == 0 else 'SS2b'
                ss1 = r3(g[s1n], 32)
                ss2 = r3(g[s2n], 32)
                apre = ApRe_j[:, jl, :].unsqueeze(1).broadcast_to([128, 16, 32])
                apim = ApIm_j[:, jl, :].unsqueeze(1).broadcast_to([128, 16, 32])
                bre = BRe_j[:, jl, :].unsqueeze(2).broadcast_to([128, 16, 32])
                bim = BIm_j[:, jl, :].unsqueeze(2).broadcast_to([128, 16, 32])
                W('v', 'sV', ckB)
                W('v', 'sT', ck_ssmm_hist.get(jg - 2, 0))
                E('v', lambda e, a=apre, b=bre, o=ss1: e.tensor_tensor(o, a, b, OP.mult))
                ckss = E('v', lambda e, a=apim, b=bim, o=ss2: e.tensor_tensor(
                    o, a, b, OP.mult), inc=('sV', 1))
                W('t', 'sV', ckss)
                c0 = 126 - 2 * jfull
                E('t', lambda e, p=kp, c=c0, s1=s1n: e.matmul(
                    p[:, :], sb_mplus[:, c:c + 128], g[s1],
                    start=(c == 126), stop=False))
                ck_ss_mm = E('t', lambda e, p=kp, c=c0, s2=s2n,
                             lst=(jfull == 63): e.matmul(
                    p[:, :], sb_mminus[:, c:c + 128], g[s2],
                    start=False, stop=lst), inc=('sT', 1))
                ck_ssmm_hist[jg] = ck_ss_mm
            if hg % 2 == 1:
                grp = hg // 2
                W('v', 'sT', ck_ss_mm)
                ckv = E('v', lambda e, p=kp: e.tensor_copy(g['khl'], p[:, :]),
                        inc=('sV', 1))
                khl2 = r3(g['khl'], 128)
                ck_tev = {}
                for lc in range(4):
                    pt = ppC if lc % 2 == 0 else ppD
                    W('t', 'sV', ckv)
                    W('t', 'sV', ck_tev.get(lc - 2, 0))
                    ckt = E('t', lambda e, p=pt, l=lc: e.transpose(
                        p[:, 0:128], khl2[:, l, :], sb_ident[:, :]), inc=('sT', 1))
                    W('v', 'sT', ckt)
                    E('v', lambda e, p=pt, l=lc, gg=grp: e.tensor_copy(
                        aKlh[:, l, gg * 128:(gg + 1) * 128], p[:, 0:128]))
                    ck_klh_v = E('v', lambda e, p=pt, l=lc, gg=grp: e.tensor_copy(
                        aKbf[:, l, gg * 128:(gg + 1) * 128], p[:, 0:128]),
                        inc=('sV', 1))
                    ck_tev[lc] = ck_klh_v
        ck_k_done = ck_klh_v

        if KCUT == 'p0c':
            W('g', 'sV', sch.cnt['sV'])
            W('g', 'sT', sch.cnt['sT'])
            ck_dummy = dma('g', 'dOut', d_out[0:128, :], sb_dump[:, :])
            WC('g', ck_dummy)
            return _finish(sch, blk, sems, nc)

        ck_kf_ev = {}
        for comp in range(2):
            dst = aKfRe if comp == 0 else aKfIm
            c32v, ccv = (aC32, aCc) if comp == 0 else (aS32, aSc)
            for fc in range(4):
                for hh in range(2):
                    idx = comp * 8 + fc * 2 + hh
                    pp = ppA if idx % 2 == 0 else ppB
                    W('t', 'sV', ck_k_done)
                    W('t', 'sV', ck_kf_ev.get(idx - 2, 0))
                    for lc in range(4):
                        E('t', lambda e, p=pp, l=lc, f=fc, h=hh, cv=c32v: e.matmul(
                            p[:, :], cv[:, l, f * 128:(f + 1) * 128],
                            aKlh[:, l, h * 512:(h + 1) * 512],
                            start=(l == 0), stop=False))
                    ckt = None
                    for lc in range(4):
                        ckt = E('t', lambda e, p=pp, l=lc, f=fc, h=hh, cv=ccv:
                                e.matmul(
                                    p[:, :], cv[:, l, f * 128:(f + 1) * 128],
                                    aKbf[:, l, h * 512:(h + 1) * 512],
                                    start=False, stop=(l == 3)),
                                inc=('sT', 1) if lc == 3 else None)
                    W('v', 'sT', ckt)
                    ck_kf_ev[idx] = E(
                        'v', lambda e, p=pp, f=fc, h=hh, d=dst: e.tensor_copy(
                            d[:, f, h * 512:(h + 1) * 512], p[:, :]), inc=('sV', 1))
        ck_last_kf = max(ck_kf_ev.values())
        W('v', 'sV', ck_last_kf)
        E('v', lambda e: e.tensor_copy(aKfX[:, 0, :], aKfIm[:, 0, :]))
        E('v', lambda e: e.memset(aKfX[0:1, 0, :], 0.0))
        E('v', lambda e: e.tensor_copy(aKfX[:, 1, :], aKfRe[:, 0, :]))
        E('v', lambda e: e.tensor_copy(aKfX[0:1, 1, :], aKfIm[0:1, 0, :]),
          inc=('sV', 1))
        ck_p0_v = sch.cnt['sV']
        ck_p0_t = sch.cnt['sT']
        if KDBG:
            W('g', 'sV', ck_p0_v)
            W('g', 'sT', ck_p0_t)
            WC('g', ck_oa_total)
            WC('g', ck_ob_total)
            ckw1t = dma('g', 'dKb', cvAb.rearrange("p (a b) -> p a b", b=128),
                        d_w1[1, :, :, :])
            WC('g', ckw1t)
            dma('g', 'dOut', d_dbg[3, :, :],
                cvAb.bitcast(F32)[:, 0:1024])
            dma('g', 'dOut', d_dbg[4, :, :],
                cvAb.bitcast(F32)[:, 1024:2048])
            dma('g', 'dOut', d_dbg[0, :, :], aKfRe[:, 0, :])
            dma('g', 'dOut', d_dbg[1, :, :], aKfIm[:, 0, :])
            dma('g', 'dOut', d_dbg[2, :, :], aKfX[:, 0, :])
            dma('g', 'dOut', d_dbg[5, :, 0:1024], sb_ab[:, :])
            ckdx = dma('g', 'dOut', d_dbg[6, :, 0:512], sb_wre[:, :])
            WC('t', ckdx)

        if KCUT == 'p0':
            W('g', 'sV', ck_p0_v)
            W('g', 'sT', ck_p0_t)
            ck_dummy = dma('g', 'dOut', d_out[0:128, :], sb_dump[:, :])
            WC('g', ck_dummy)
            return _finish(sch, blk, sems, nc)

        ck_uh = {}

        def load_uhalf(h, ck_t=None, ck_s=None, ck_v=None):
            if h > NBLK:
                return
            if h < 2:
                W('g', 'sV', ck_p0_v)
                W('g', 'sT', ck_p0_t)
            else:
                W('g', 'sT', sch.cnt['sT'] if ck_t is None else ck_t)
                W('g', 'sA', sch.cnt['sA'] if ck_s is None else ck_s)
                if ck_v is not None:
                    W('g', 'sV', ck_v)
            ck_uh[h] = dma('g', 'dU0' if h % 2 == 0 else 'dU1', aU[h % 2],
                           d_u[h * 512:(h + 1) * 512, :].rearrange(
                               "(a p) c -> p a c", p=128))

        load_uhalf(0)
        load_uhalf(1)

        def ln_emit(tile, outtile):
            st = sb_stats
            ckv = E('v', lambda e, ti=tile: e.tensor_reduce(st[:, 0:1], ti, AXX,
                                                            OP.add), inc=('sV', 1))
            W('s', 'sV', ckv)
            cks = E('s', lambda e, ti=tile: e.activation(sb_dump[:, :], ti, AF.Square,
                                                         accum_out=st[:, 1:2]),
                    inc=('sA', 1))
            W('v', 'sA', cks)
            E('v', lambda e: e.tensor_scalar(st[:, 2:3], st[:, 0:1], R1024,
                                             None, OP.mult))
            E('v', lambda e: e.tensor_scalar(st[:, 3:4], st[:, 1:2], R1024,
                                             None, OP.mult))
            ckv = E('v', lambda e: e.tensor_scalar(st[:, 8:9], st[:, 0:1], -R1024,
                                                   None, OP.mult), inc=('sV', 1))
            W('s', 'sV', ckv)
            cks = E('s', lambda e: e.activation(st[:, 4:5], st[:, 2:3], AF.Square),
                    inc=('sA', 1))
            W('v', 'sA', cks)
            ckv = E('v', lambda e: e.tensor_sub(st[:, 5:6], st[:, 3:4], st[:, 4:5]),
                    inc=('sV', 1))
            W('s', 'sV', ckv)
            cks = E('s', lambda e: e.activation(st[:, 6:7], st[:, 5:6], AF.Sqrt,
                                                bias=EPS), inc=('sA', 1))
            W('v', 'sA', cks)
            ckv = E('v', lambda e: e.reciprocal(st[:, 7:8], st[:, 6:7]),
                    inc=('sV', 1))
            W('s', 'sV', ckv)
            cks = E('s', lambda e: e.activation(st[:, 9:10], st[:, 8:9], AF.Copy,
                                                scale=st[:, 7:8]), inc=('sA', 1))
            W('v', 'sA', cks)
            return E('v', lambda e, ti=tile, o=outtile: e.scalar_tensor_tensor(
                o, ti, st[:, 7:8], st[:, 9:10].broadcast_to([128, 1024]),
                OP.mult, OP.add), inc=('sV', 1))

        ck_fwd_last = 0
        ck_inv_ev = 0
        zbufs = (aZasm, sb_ab)
        spill_ck = {}
        pend_ln = None
        ztile = 0

        def flush_ln():
            nonlocal pend_ln
            ti, buf = pend_ln
            ckln = ln_emit(buf[:, :], buf[:, :])
            W('g', 'sV', ckln)
            spill_ck[ti] = dma('g', 'dSp',
                               d_x1[ti * 128:(ti + 1) * 128, :], buf[:, :])
            pend_ln = None

        for blkid in range(NBLK):
            hA, hB = blkid, blkid + 1
            uA, uB = aU[hA % 2], aU[hB % 2]
            ck_inv_prev = ck_inv_ev
            ck_ubf_last = 0

            for chh in range(2):
                def usl(tc, uA=uA, uB=uB, c=chh):
                    return (uA if tc < 4 else uB)[:, tc % 4, c * 512:(c + 1) * 512]

                WC('s', ck_uh[hA])
                WC('s', ck_uh[hB])
                W('s', 'sT', ck_fwd_last)
                ck_ubf = None
                for tc in range(8):
                    ck_ubf = E('s', lambda e, t=tc, us=usl: e.activation(
                        aUbf[:, t, :], us(t), AF.Copy),
                        inc=('sA', 1) if tc == 7 else None)
                ck_ubf_last = ck_ubf

                ck_yf_v = None
                ck_yf_s = None

                def emit_pw(fc, c=chh):
                    nonlocal ck_yf_v, ck_yf_s
                    sl = slice(c * 512, (c + 1) * 512)
                    kre = aKfRe[:, fc, sl]
                    kim = aKfX[:, 0, sl] if fc == 0 else aKfIm[:, fc, sl]
                    kreI = aKfX[:, 1, sl] if fc == 0 else aKfRe[:, fc, sl]
                    ure = aUf[:, fc, :]
                    uim = aUf[:, 4 + fc, :]
                    E('v', lambda e, k=kreI, u=uim: e.tensor_tensor(
                        aScr[:, 0, :], k, u, OP.mult))
                    E('v', lambda e, k=kim, u=ure: e.tensor_tensor(
                        aScr[:, 1, :], k, u, OP.mult))
                    E('v', lambda e, k=kre, u=ure: e.tensor_tensor(
                        aScr[:, 2, :], k, u, OP.mult))
                    E('v', lambda e, k=kim, u=uim: e.tensor_tensor(
                        aScr[:, 3, :], k, u, OP.mult))
                    E('v', lambda e, o=ure: e.tensor_sub(o, aScr[:, 2, :],
                                                         aScr[:, 3, :]))
                    ck_yf_v = E('v', lambda e, o=uim: e.tensor_add(
                        o, aScr[:, 0, :], aScr[:, 1, :]), inc=('sV', 1))
                    W('s', 'sV', ck_yf_v)
                    E('s', lambda e, o=ure, f=fc: e.activation(
                        aYbf[:, f, :], o, AF.Copy))
                    ck_yf_s = E('s', lambda e, o=uim, f=fc: e.activation(
                        aYbf[:, 4 + f, :], o, AF.Copy), inc=('sA', 1))

                WC('t', ck_uh[hA])
                WC('t', ck_uh[hB])
                W('t', 'sA', ck_ubf)
                W('t', 'sV', ck_inv_ev)
                ck_uf_ev = {}
                for comp in range(2):
                    c32v, ccv = (aC32, aCc) if comp == 0 else (aS32, aSc)
                    for fc in range(4):
                        gi = comp * 4 + fc
                        g1 = ppA if gi % 2 == 0 else ppC
                        g2 = ppB if gi % 2 == 0 else ppD
                        W('t', 'sV', ck_uf_ev.get(gi - 2, 0))
                        ckt = None
                        for tc in range(4):
                            for half in range(2):
                                pp = g1 if half == 0 else g2
                                E('t', lambda e, p=pp, t=tc, hf=half, f=fc, us=usl,
                                  cv=c32v: e.matmul(
                                    p[:, :], cv[:, t, f * 128:(f + 1) * 128],
                                    us(t + 4 * hf), start=(t == 0), stop=False))
                        for tc in range(4):
                            for half in range(2):
                                pp = g1 if half == 0 else g2
                                ckt = E('t', lambda e, p=pp, t=tc, hf=half, f=fc,
                                        cv=ccv: e.matmul(
                                            p[:, :], cv[:, t, f * 128:(f + 1) * 128],
                                            aUbf[:, t + 4 * hf, :],
                                            start=False, stop=(t == 3)),
                                        inc=('sT', 1) if (half == 1 and tc == 3)
                                        else None)
                        W('v', 'sT', ckt)
                        E('v', lambda e, p=g2: e.tensor_copy(aScr[:, 0, :], p[:, :]))
                        ck_uf_ev[gi] = E(
                            'v', lambda e, p=g1, sl=gi: e.scalar_tensor_tensor(
                                aUf[:, sl, :], aScr[:, 0, :], sb_altcol[:, 0:1],
                                p[:, :], OP.mult, OP.add), inc=('sV', 1))
                        if comp == 1:
                            emit_pw(fc)
                ck_fwd_last = sch.cnt['sT']
                if chh == 1:
                    load_uhalf(blkid + 2, ck_t=ck_fwd_last, ck_s=ck_ubf_last,
                               ck_v=ck_inv_prev)

                for tb in range(4):
                    pp = ppA if tb % 2 == 0 else ppB
                    W('t', 'sV', ck_yf_v)
                    W('t', 'sA', ck_yf_s)
                    W('t', 'sV', ck_inv_ev)
                    for comp in range(2):
                        civ = aCi32 if comp == 0 else aSi32
                        for fc in range(4):
                            E('t', lambda e, p=pp, f=fc, t=tb, cv=civ, cp=comp:
                              e.matmul(
                                p[:, :], cv[:, f, t * 128:(t + 1) * 128],
                                aUf[:, cp * 4 + f, :], start=(cp == 0 and f == 0),
                                stop=False))
                    ckt = None
                    for comp in range(2):
                        cicv = aCic if comp == 0 else aSic
                        for fc in range(4):
                            islast = (comp == 1 and fc == 3)
                            ckt = E('t', lambda e, p=pp, f=fc, t=tb, cv=cicv, cp=comp,
                                    lst=islast: e.matmul(
                                        p[:, :], cv[:, f, t * 128:(t + 1) * 128],
                                        aYbf[:, cp * 4 + f, :], start=False, stop=lst),
                                    inc=('sT', 1) if islast else None)
                    W('v', 'sT', ckt)
                    E('v', lambda e, t=tb, c=chh, u=uB: e.tensor_tensor(
                        aScr[:, 0, :], u[:, t, c * 512:(c + 1) * 512],
                        sb_dp1[:, c * 512:(c + 1) * 512], OP.mult))
                    if chh == 0:
                        ck_inv_ev = E('v', lambda e, p=pp, t=tb: e.tensor_add(
                            aZX[:, t, :], aScr[:, 0, :], p[:, :]), inc=('sV', 1))
                    else:
                        buf = zbufs[ztile % 2]
                        WC('v', spill_ck.get(ztile - 2))
                        E('v', lambda e, t=tb, b=buf: e.tensor_copy(
                            b[:, 0:512], aZX[:, t, :]))
                        ck_inv_ev = E('v', lambda e, p=pp, b=buf: e.tensor_add(
                            b[:, 512:1024], aScr[:, 0, :], p[:, :]),
                            inc=('sV', 1))
                        if pend_ln is not None:
                            flush_ln()
                        pend_ln = (ztile, buf)
                        ztile += 1
        flush_ln()

        if KCUT == 'p1':
            W('g', 'dSp', sch.cnt['dSp'])
            ck_dummy = dma('g', 'dOut', d_out[0:128, :], sb_dump[:, :])
            WC('g', ck_dummy)
            return _finish(sch, blk, sems, nc)

        ck_bar = {s: sch.cnt[s] for s in DMA_SEMS + ENG_SEMS}
        for eng in ('t', 'v', 's', 'y', 'g'):
            for sname, val in ck_bar.items():
                W(eng, sname, val)

        W0SEM = ('dW0a', 'dW0b')
        W1SEM = ('dW1a', 'dW1b')
        W2SEM = ('dW2a', 'dW2b', 'dW2c', 'dW2d')
        ck_w0_use = {}
        ck_w1_use = {}
        ck_w2_use = {}
        ck_w2b_use = {}
        ck_z2b = 0
        ck_out = {}
        def load_w0(p, oc, ck_w0_load, gate):
            W('y', 'sT', gate)
            ck_w0_load[oc] = dma('y', W0SEM[oc % 2], aW0[:, oc % 2, :, :],
                                 d_w0[oc, :, :, :])

        import os as _os2
        W1Q = _os2.environ.get('KW1Q', 'y')
        assert W1Q in ('y', 'g', 's')
        W1R1 = _os2.environ.get('KW1R1') == '1'

        def load_w1(p, oc, ck_w1_load, gate):
            W(W1Q, 'sT', gate)
            slot = 0 if W1R1 else oc % 2
            ck_w1_load[oc] = dma(
                W1Q, W1SEM[oc % 2],
                aW1_f32[:, slot * 2048:(slot + 1) * 2048],
                d_w1[oc, :, :, :].rearrange("q a b -> q (a b)").bitcast(F32))

        def load_w2(p, oc, ck_w2_load, gate):
            W('y', 'sT', gate)
            ck_w2_load[oc] = dma('y', W2SEM[oc % 4], aW2[:, oc % 4, :],
                                 d_w2[oc, :, :])

        ck_x1_half = {}
        pend_tr = {}

        def emit_transposes(i0, i1, ck_x1t, ck_tr):
            for i in range(i0, i1):
                tc, fc = divmod(i, 8)
                pt = ppC if i % 2 == 0 else ppD
                W('t', 'sV', ck_tr.get(i - 2, 0))
                ckt = E('t', lambda e, q=pt, t=tc, f=fc: e.transpose(
                    q[:, 0:128], aX1tm[:, t, f * 128:(f + 1) * 128],
                    sb_ident[:, :]), inc=('sT', 1))
                W('s', 'sT', ckt)
                ck_x1t[i] = E('s', lambda e, q=pt, t=tc, f=fc: e.activation(
                    aX1hi[:, f, t * 128:(t + 1) * 128], q[:, 0:128], AF.Copy),
                    inc=('sA', 1))
                W('v', 'sA', ck_x1t[i])
                ck_tr[i] = E('v', lambda e, q=pt, t=tc, f=fc: e.tensor_sub(
                    aX1lo[:, f, t * 128:(t + 1) * 128], q[:, 0:128],
                    aX1hi[:, f, t * 128:(t + 1) * 128]), inc=('sV', 1))

        def load_x1_half(p, half, gate_out=None):
            if gate_out is not None:
                WC('g', gate_out)
            r0 = p * 512 + half * 256
            ck_x1_half[(p, half)] = dma(
                'g', 'dU0' if half == 0 else 'dU1',
                aX1tm[:, half * 2:half * 2 + 2, :],
                d_x1[r0:r0 + 256, :].rearrange("(a q) c -> q a c", q=128))

        for p in range(NPAIR):
            if p == 0:
                load_x1_half(0, 0)
                load_x1_half(0, 1)
            ck_w0_load = {}
            ck_w1_load = {}
            ck_w2_load = {}
            ck_w2b_load = {}
            for oc in range(2):
                load_w0(p, oc, ck_w0_load, ck_w0_use.get((p - 1, 30 + oc), 0))

            if p == 0:
                ck_x1t = {}
                ck_tr = {}
                WC('t', ck_x1_half[(0, 0)])
                emit_transposes(0, 16, ck_x1t, ck_tr)
            else:
                ck_x1t, ck_tr = pend_tr.pop(p)
            WC('t', ck_x1_half[(p, 1)])
            emit_transposes(16, 32, ck_x1t, ck_tr)
            ck_tr_last = ck_tr[31]

            ck_h1_ev = {}
            for oc in range(32):
                pp = ppA if oc % 2 == 0 else ppB
                W('t', 'sV', ck_h1_ev.get(oc - 2, 0))
                WC('t', ck_w0_load[oc])
                W('t', 'sV', ck_tr_last)
                ckt = None
                for ic in range(8):
                    E('t', lambda e, q=pp, o=oc, i=ic: e.matmul(
                        q[:, :], aW0[:, o % 2, i, :], aX1hi[:, i, :],
                        start=(i == 0), stop=False))
                    ckt = E('t', lambda e, q=pp, o=oc, i=ic: e.matmul(
                        q[:, :], aW0[:, o % 2, i, :], aX1lo[:, i, :],
                        start=False, stop=(i == 7)),
                        inc=('sT', 1) if ic == 7 else None)
                ck_w0_use[(p, oc)] = ckt
                if oc + 2 <= 31:
                    load_w0(p, oc + 2, ck_w0_load, ckt)
                elif oc == 30:
                    if W1R1:
                        load_w1(p, 0, ck_w1_load, ck_w1_use.get((p - 1, 31), 0))
                    else:
                        for o2 in range(2):
                            load_w1(p, o2, ck_w1_load,
                                    ck_w1_use.get((p - 1, 30 + o2), 0))
                    for o2 in range(4):
                        load_w2(p, o2, ck_w2_load,
                                ck_w2b_use.get((p - 1, 28 + o2), 0))
                W('s', 'sT', ckt)
                W('s', 'sV', ck_h1_ev.get(oc - 1, 0))
                cka = E('s', lambda e, q=pp: e.activation(aG1f[:, :], q[:, :],
                                                          AF.Relu), inc=('sA', 1))
                W('v', 'sA', cka)
                E('v', lambda e, o=oc: e.tensor_copy(aH1hi[:, o, :], aG1f[:, :]))
                ck_h1_ev[oc] = E('v', lambda e, o=oc: e.tensor_sub(
                    aH1lo[:, o, :], aG1f[:, :], aH1hi[:, o, :]), inc=('sV', 1))
            ck_h1_all = ck_h1_ev[31]
            if KDBG and p == 0:
                W('g', 'sV', ck_h1_all)
                h1flat = aH1hi.rearrange("q a b -> q (a b)").bitcast(F32)
                for di in range(4):
                    dma('g', 'dOut', d_dbg[11 + di, :, :],
                        h1flat[:, di * 1024:(di + 1) * 1024])
                x1flat = aX1hi.rearrange("q a b -> q (a b)").bitcast(F32)
                ckdx = dma('g', 'dOut', d_dbg[15, :, 0:1024], x1flat[:, 0:1024])
                WC('t', ckdx)

            ck_h2a_ev = {}
            ck_h2b_ev = {}
            ck_mm2a = {}

            def emit_mm2a(oc):
                sl = oc % 2
                W('t', 'sV', ck_h2a_ev[oc])
                WC('t', ck_w2_load[oc])
                if oc == 0:
                    W('t', 'sV', ck_z2b)
                ckt = None
                for hilo in range(2):
                    src = aH2Ahi if hilo == 0 else aH2Alo
                    for tb in range(2):
                        for chh in range(2):
                            ckt = E('t', lambda e, s2=sl, hl=hilo, t=tb, c=chh, o=oc,
                                    sr=src: e.matmul(
                                        pM[t * 2 + c][:, :],
                                        sr[:, s2, t * 128:(t + 1) * 128],
                                        aW2[:, o % 4, c * 512:(c + 1) * 512],
                                        start=(o == 0 and hl == 0),
                                        stop=(o == 31 and hl == 1)),
                                    inc=('sT', 1) if (hilo == 1 and tb == 1 and
                                                      chh == 1) else None)
                ck_mm2a[oc] = ckt
                ck_w2_use[(p, oc)] = ckt
                if oc + 4 <= 31:
                    load_w2(p, oc + 4, ck_w2_load, ckt)
                elif oc == 31:
                    for o2 in range(4):
                        W('y', 'sT', ck_w2_use[(p, 28 + o2)])
                        ck_w2b_load[o2] = dma('y', W2SEM[o2 % 4], aW2[:, o2 % 4, :],
                                              d_w2[o2, :, :])

            for oc in range(32):
                pp = ppA if oc % 2 == 0 else ppB
                sl = oc % 2
                W('t', 'sV', ck_h2a_ev.get(oc - 2, 0))
                W('t', 'sV', ck_h1_all)
                WC('t', ck_w1_load[oc])
                ckt = None
                w1slot = 0 if W1R1 else oc % 2
                for ic in range(32):
                    E('t', lambda e, q=pp, s2=w1slot, i=ic: e.matmul(
                        q[:, :], aW1[:, s2, i, :], aH1hi[:, i, :],
                        start=(i == 0), stop=False))
                    ckt = E('t', lambda e, q=pp, s2=w1slot, i=ic: e.matmul(
                        q[:, :], aW1[:, s2, i, :], aH1lo[:, i, :],
                        start=False, stop=(i == 31)),
                        inc=('sT', 1) if ic == 31 else None)
                ck_w1_use[(p, oc)] = ckt
                if W1R1 and oc + 1 <= 31:
                    load_w1(p, oc + 1, ck_w1_load, ckt)
                elif (not W1R1) and oc + 2 <= 31:
                    load_w1(p, oc + 2, ck_w1_load, ckt)
                W('s', 'sT', ckt)
                W('s', 'sV', ck_h2b_ev.get(oc - 1, 0))
                cka = E('s', lambda e, q=pp: e.activation(aH2f[:, :], q[:, :],
                                                          AF.Relu), inc=('sA', 1))
                W('v', 'sA', cka)
                W('v', 'sT', ck_mm2a.get(oc - 2, 0))
                E('v', lambda e, s2=sl: e.tensor_copy(aH2Ahi[:, s2, :],
                                                      aH2f[:, 0:256]))
                ck_h2a_ev[oc] = E('v', lambda e, s2=sl: e.tensor_sub(
                    aH2Alo[:, s2, :], aH2f[:, 0:256], aH2Ahi[:, s2, :]),
                    inc=('sV', 1))
                E('v', lambda e, o=oc: e.tensor_copy(aH2Bhi[:, o, :],
                                                     aH2f[:, 256:512]))
                ck_h2b_ev[oc] = E('v', lambda e, o=oc: e.tensor_sub(
                    aH2Blo[:, o, :], aH2f[:, 256:512], aH2Bhi[:, o, :]),
                    inc=('sV', 1))
                if oc >= 1:
                    emit_mm2a(oc - 1)
            emit_mm2a(31)

            ck_z2a = 0
            ck_z2a_stt = 0
            for tb in range(2):
                W('v', 'sT', ck_mm2a[31])
                if p > 0:
                    WC('v', ck_out.get((p - 1, 1)))
                ck_stt = None
                for chh in range(2):
                    ck_stt = E('v', lambda e, t=tb, c=chh: e.scalar_tensor_tensor(
                        aZ2[:, c * 512:(c + 1) * 512], pM[t * 2 + c][:, :],
                        sb_alpha[:, 0:1], aX1tm[:, t, c * 512:(c + 1) * 512],
                        OP.mult, OP.add),
                        inc=('sV', 1) if (chh == 1 and
                                          (tb == 1 or (KDBG and p == 0)))
                        else None)
                if tb == 1:
                    ck_z2a_stt = ck_stt
                if KDBG and p == 0 and tb == 0:
                    W('s', 'sV', ck_stt)
                    E('s', lambda e: e.activation(sb_dump[:, 0:512],
                                                  pM[0][:, :], AF.Copy))
                    ckps = E('s', lambda e: e.activation(sb_dump[:, 512:1024],
                                                         pM[1][:, :], AF.Copy),
                             inc=('sA', 1))
                    W('g', 'sA', ckps)
                    dma('g', 'dOut', d_dbg[7, :, :], aZ2[:, :])
                    dma('g', 'dOut', d_dbg[8, :, 0:512], sb_dump[:, 0:512])
                    E('g', lambda e: e.dma_start(
                        out=d_dbg[8, :, 512:513], in_=sb_alpha[:, 0:1],
                        allow_slow_non_contiguous=True), inc=('dOut', 16))
                    dma('g', 'dOut', d_dbg[9, :, 0:512], sb_dump[:, 512:1024])
                    ckdz = dma('g', 'dOut', d_dbg[10, :, :], aX1tm[:, 0, :])
                    WC('v', ckdz)
                    WC('t', ckdz)
                    WC('s', ckdz)
                ck_z2a = ln_emit(aZ2[:, :], aX1tm[:, tb, :])
            W('g', 'sV', ck_z2a)
            ck_out[(p, 0)] = dma('g', 'dOut',
                                 d_out[p * 512:p * 512 + 256, :].rearrange(
                                     "(a q) c -> q a c", q=128), aX1tm[:, 0:2, :])
            if p + 1 < NPAIR:
                load_x1_half(p + 1, 0, gate_out=ck_out[(p, 0)])

            ck_mm2b = 0
            for oc in range(32):
                W('t', 'sV', ck_z2a_stt)
                WC('t', ck_w2b_load[oc])
                W('t', 'sV', ck_h2b_ev[31])
                ckt = None
                for hilo in range(2):
                    src = aH2Bhi if hilo == 0 else aH2Blo
                    for tb in range(2):
                        for chh in range(2):
                            ckt = E('t', lambda e, hl=hilo, t=tb, c=chh, o=oc,
                                    sr=src: e.matmul(
                                        pM[t * 2 + c][:, :],
                                        sr[:, o, t * 128:(t + 1) * 128],
                                        aW2[:, o % 4, c * 512:(c + 1) * 512],
                                        start=(o == 0 and hl == 0),
                                        stop=(o == 31 and hl == 1)),
                                    inc=('sT', 1) if (hilo == 1 and tb == 1 and
                                                      chh == 1) else None)
                ck_mm2b = ckt
                ck_w2b_use[(p, oc)] = ckt
                if oc + 4 <= 31:
                    W('y', 'sT', ckt)
                    ck_w2b_load[oc + 4] = dma('y', W2SEM[(oc + 4) % 4],
                                              aW2[:, (oc + 4) % 4, :],
                                              d_w2[oc + 4, :, :])
                if oc == 16 and p + 1 < NPAIR:
                    ntr_x1t = {}
                    ntr_tr = {}
                    WC('t', ck_x1_half[(p + 1, 0)])
                    emit_transposes(0, 16, ntr_x1t, ntr_tr)
                    pend_tr[p + 1] = (ntr_x1t, ntr_tr)
            for tb in range(2):
                W('v', 'sT', ck_mm2b)
                WC('v', ck_out[(p, 0)])
                for chh in range(2):
                    E('v', lambda e, t=tb, c=chh: e.scalar_tensor_tensor(
                        aZ2[:, c * 512:(c + 1) * 512], pM[t * 2 + c][:, :],
                        sb_alpha[:, 0:1], aX1tm[:, 2 + t, c * 512:(c + 1) * 512],
                        OP.mult, OP.add))
                ck_z2b = ln_emit(aZ2[:, :], aX1tm[:, 2 + tb, :])
            W('g', 'sV', ck_z2b)
            ck_out[(p, 1)] = dma('g', 'dOut',
                                 d_out[p * 512 + 256:(p + 1) * 512, :].rearrange(
                                     "(a q) c -> q a c", q=128), aX1tm[:, 2:4, :])
            if p + 1 < NPAIR:
                load_x1_half(p + 1, 1, gate_out=ck_out[(p, 1)])

        W('g', 'dOut', sch.cnt['dOut'])
        return _finish(sch, blk, sems, nc)


def _finish(sch, blk, sems, nc):
    @blk.tensor
    def _(eng):
        sch.run('t', eng, sems)

    @blk.vector
    def _(eng):
        sch.run('v', eng, sems)

    @blk.scalar
    def _(eng):
        sch.run('s', eng, sems)

    @blk.sync
    def _(eng):
        sch.run('y', eng, sems)

    @blk.gpsimd
    def _(eng):
        sch.run('g', eng, sems)

    return nc


def _prep_inputs(inputs):
    u = np.asarray(inputs['L0_attn_u'])
    D = np.ascontiguousarray(np.asarray(inputs['L0_attn_D'], dtype=np.float32))
    lam_re = np.asarray(inputs['L0_attn_Lambda_re'], dtype=np.float32)
    lam_im = np.asarray(inputs['L0_attn_Lambda_im'], dtype=np.float32)
    log_dt = np.asarray(inputs['L0_attn_log_dt'], dtype=np.float32)
    W_re = np.asarray(inputs['L0_attn_W_re'], dtype=np.float32)
    W_im = np.asarray(inputs['L0_attn_W_im'], dtype=np.float32)
    k0 = np.asarray(inputs['L0_ffn_k0'])
    k1 = np.asarray(inputs['L0_ffn_k1'])
    k2 = np.asarray(inputs['L0_ffn_k2'])
    s0 = np.float32(np.asarray(inputs['L0_ffn_s0']))
    s1 = np.float32(np.asarray(inputs['L0_ffn_s1']))
    s2 = np.float32(np.asarray(inputs['L0_ffn_s2']))

    logdt_arr = np.empty((128, 512), np.float32)
    ld = log_dt.reshape(512, 2)
    logdt_arr[0::2, :] = ld[:, 0][None, :]
    logdt_arr[1::2, :] = ld[:, 1][None, :]
    lamre_col = np.ascontiguousarray(np.repeat(lam_re, 2)[:, None])
    lamim_col = np.ascontiguousarray(np.repeat(lam_im, 2)[:, None])
    wre_arr = np.empty((128, 512), np.float32)
    wim_arr = np.empty((128, 512), np.float32)
    wr = W_re.reshape(512, 2, 64)
    wi = W_im.reshape(512, 2, 64)
    for par in range(2):
        wre_arr[par::2, :] = wr[:, par, :].T
        wim_arr[par::2, :] = wi[:, par, :].T
    k0t = np.ascontiguousarray(k0[:1024]).view(np.uint8)
    k0b = np.ascontiguousarray(k0[1024:]).view(np.uint8)
    k1u = np.ascontiguousarray(k1).view(np.uint8)
    k2u = np.ascontiguousarray(k2).view(np.uint8)
    svec = np.broadcast_to(np.array([s0, s1, s2], np.float32), (128, 3)).copy()
    D_rep = np.broadcast_to(D, (128, H)).copy()

    consts = _host_consts()
    shared = dict(logdt_arr=logdt_arr, lamre_col=lamre_col, lamim_col=lamim_col,
                  wre_arr=wre_arr, wim_arr=wim_arr, k0t=k0t, k0b=k0b, k1=k1u,
                  k2=k2u, svec=svec, D_rep=D_rep)
    for nm, arr in consts.items():
        shared['c_' + nm] = np.ascontiguousarray(arr)

    in_maps = []
    for c in range(8):
        b, half = c // 2, c % 2
        tok0 = half * 4096
        if half == 0:
            halo = np.zeros((512, H), np.float32)
        else:
            halo = u[b, tok0 - 512:tok0]
        shard = np.concatenate([halo, u[b, tok0:tok0 + 4096]], axis=0)
        m = dict(shared)
        m['u_shard'] = np.ascontiguousarray(shard, dtype=np.float32)
        in_maps.append(m)
    return in_maps


_NC_CACHE = None


def kernel(**inputs):
    global _NC_CACHE
    if _NC_CACHE is None:
        _NC_CACHE = build_nc()
    nc = _NC_CACHE
    in_maps = _prep_inputs(inputs)
    res = run_bass_kernel_spmd(nc, in_maps, core_ids=list(range(8)))
    outs = [np.asarray(res.results[c]['out']) for c in range(8)]
    full = np.stack([np.concatenate([outs[2 * b], outs[2 * b + 1]], axis=0)
                     for b in range(4)], axis=0)
    return full.astype(np.float32)


# version 14 (final)
# speedup vs baseline: 1.2337x; 1.0225x over previous
from contextlib import ExitStack

import numpy as np
import ml_dtypes

import concourse.bass as bass
import concourse.mybir as mybir
from concourse.bass_utils import run_bass_kernel_spmd

F32 = mybir.dt.float32
BF16 = mybir.dt.bfloat16
I32 = mybir.dt.int32
U8 = mybir.dt.uint8
AF = mybir.ActivationFunctionType
OP = mybir.AluOpType
AXX = mybir.AxisListType.X

bf16 = ml_dtypes.bfloat16

H = 1024
TOK = 4096
NBLK = 8
NPAIR = 8
EPS = 1e-5
R1024 = float(np.float32(1.0 / 1024.0))

CHI = float(np.float32(2 * np.pi))
CLO = float(np.float32(2 * np.pi - np.float64(np.float32(2 * np.pi))))
I2HI = float(np.float32(1.0 / (2 * np.pi)))
I2LO = float(np.float32(1.0 / (2 * np.pi) - np.float64(np.float32(1.0 / (2 * np.pi)))))
PHHI = float(np.float32(np.pi / 2))
PHLO = float(np.float32(np.pi / 2 - np.float64(np.float32(np.pi / 2))))
PI_SAFE = float(np.nextafter(np.float32(np.pi), np.float32(0)))

DMA_SEMS = ('dC', 'dKa', 'dKb', 'dOa', 'dOb', 'dU0', 'dU1', 'dSp', 'dX',
            'dW0a', 'dW0b', 'dW1a', 'dW1b', 'dW2a', 'dW2b', 'dW2c', 'dW2d',
            'dOut')
ENG_SEMS = ('sT', 'sV', 'sA')


def _split_const(M64):
    m32 = M64.astype(np.float32)
    corr = (M64 - m32.astype(np.float64)).astype(bf16)
    return m32, corr


def _host_consts():
    t = np.arange(512, dtype=np.float64)
    f = np.arange(512, dtype=np.float64)
    ang = 2 * np.pi * np.outer(t, f) / 1024.0
    C32, Ccorr = _split_const(np.cos(ang))
    S32, Scorr = _split_const(-np.sin(ang))
    t2 = np.arange(512, dtype=np.float64)
    wf = np.where(f == 0, 1.0, 2.0)
    angi = 2 * np.pi * np.outer(f, 512 + t2) / 1024.0
    Ci32, Cicorr = _split_const(wf[:, None] * np.cos(angi) / 1024.0)
    Si32, Sicorr = _split_const(-wf[:, None] * np.sin(angi) / 1024.0)
    altcol = ((-1.0) ** np.arange(128, dtype=np.float64)).astype(np.float32)[:, None]
    S32[:, 0] = ((-1.0) ** t).astype(np.float32)
    Scorr[:, 0] = 0
    Si32[0, :] = (((-1.0) ** t2) / 1024.0).astype(np.float32)
    Sicorr[0, :] = 0
    iota32 = np.broadcast_to(np.arange(32, dtype=np.float32), (128, 32)).copy()
    iota16 = np.broadcast_to(np.arange(16, dtype=np.float32) * 32.0, (128, 16)).copy()
    ident = np.eye(128, dtype=np.float32)
    par = (np.arange(128) & 1)
    Mplus = np.zeros((128, 254), np.float32)
    Mplus[:, 126] = (par == 0)
    Mplus[:, 127] = (par == 1)
    Mminus = -Mplus
    cpool = np.broadcast_to(np.array([-1.0, 1.0, 2.0, EPS], np.float32),
                            (128, 4)).copy()
    return dict(C32=C32, Ccorr=Ccorr, S32=S32, Scorr=Scorr,
                Ci32=Ci32, Cicorr=Cicorr, Si32=Si32, Sicorr=Sicorr,
                altcol=altcol, iota32=iota32, iota16=iota16,
                ident=ident, Mplus=Mplus, Mminus=Mminus, cpool=cpool)


class Sched:
    ENGS = ('t', 'v', 's', 'y', 'g')

    def __init__(self):
        self.ops = {e: [] for e in self.ENGS}
        self.cnt = {s: 0 for s in DMA_SEMS + ENG_SEMS}
        self.lastwait = {}

    def wait(self, eng, sem, val):
        if val is None or val <= 0:
            return
        key = (eng, sem)
        if self.lastwait.get(key, -1) >= val:
            return
        self.lastwait[key] = val
        self.ops[eng].append(('w', sem, val))

    def waitc(self, eng, ck):
        if ck:
            self.wait(eng, ck[0], ck[1])

    def emit(self, eng, fn, inc=None):
        if inc is not None:
            sname, k = inc
            self.cnt[sname] += k
            post = self.cnt[sname]
            self.ops[eng].append(('i', fn, sname, k))
            return post
        self.ops[eng].append(('e', fn))
        return None

    def run(self, eng, proxy, sems):
        for op in self.ops[eng]:
            if op[0] == 'w':
                proxy.wait_ge(sems[op[1]], op[2])
            elif op[0] == 'e':
                op[1](proxy)
            else:
                op[1](proxy).then_inc(sems[op[2]], op[3])


class Arena:
    def __init__(self, handle, nf32):
        self.h = handle
        self.n = nf32
        self.off = 0

    @staticmethod
    def _shape(ap, shape):
        if len(shape) == 3:
            return ap.rearrange("p (a b) -> p a b", b=shape[2])
        if len(shape) == 4:
            return ap.rearrange("p (a b c) -> p a b c", b=shape[2], c=shape[3])
        return ap

    def f32(self, shape):
        n = int(np.prod(shape[1:]))
        ap = self.h[0:shape[0], self.off:self.off + n]
        self.off += n
        assert self.off <= self.n, f"arena overflow {self.off} > {self.n}"
        return self._shape(ap, shape)

    def bf(self, shape):
        n = int(np.prod(shape[1:]))
        slots = (n + 1) // 2
        ap = self.h[0:shape[0], self.off:self.off + slots].bitcast(BF16)
        self.off += slots
        assert self.off <= self.n, f"arena overflow {self.off} > {self.n}"
        return self._shape(ap, shape)


ARENA_N = 44500


def build_nc():
    nc = bass.Bass("TRN2", target_bir_lowering=False,
                   detect_race_conditions=False)

    d_u = nc.dram_tensor("u_shard", [TOK + 512, H], F32, kind="ExternalInput").ap()
    d_logdt = nc.dram_tensor("logdt_arr", [128, 512], F32, kind="ExternalInput").ap()
    d_lamre = nc.dram_tensor("lamre_col", [128, 1], F32, kind="ExternalInput").ap()
    d_lamim = nc.dram_tensor("lamim_col", [128, 1], F32, kind="ExternalInput").ap()
    d_wre = nc.dram_tensor("wre_arr", [128, 512], F32, kind="ExternalInput").ap()
    d_wim = nc.dram_tensor("wim_arr", [128, 512], F32, kind="ExternalInput").ap()
    d_k0t = nc.dram_tensor("k0t", [1024, 4096], U8, kind="ExternalInput").ap()
    d_k0b = nc.dram_tensor("k0b", [1024, 4096], U8, kind="ExternalInput").ap()
    d_k1 = nc.dram_tensor("k1", [4096, 4096], U8, kind="ExternalInput").ap()
    d_k2 = nc.dram_tensor("k2", [4096, 1024], U8, kind="ExternalInput").ap()
    d_svec = nc.dram_tensor("svec", [128, 3], F32, kind="ExternalInput").ap()
    d_Drep = nc.dram_tensor("D_rep", [128, H], F32, kind="ExternalInput").ap()
    CONSTS = _host_consts()
    dc = {}
    for name, arr in CONSTS.items():
        dt = BF16 if arr.dtype == bf16 else F32
        dc[name] = nc.dram_tensor("c_" + name, list(arr.shape), dt,
                                  kind="ExternalInput").ap()
    d_out = nc.dram_tensor("out", [TOK, H], F32, kind="ExternalOutput").ap()
    d_dbg = None
    d_x1 = nc.dram_tensor("x1scr", [TOK, H], F32, kind="Internal").ap()
    d_w0 = nc.dram_tensor("w0scr", [32, 128, 8, 128], BF16, kind="Internal").ap()
    d_w1 = nc.dram_tensor("w1scr", [32, 128, 32, 128], BF16, kind="Internal").ap()
    d_w2 = nc.dram_tensor("w2scr", [32, 128, 1024], BF16, kind="Internal").ap()

    import os
    KCUT = os.environ.get('KCUT', '')
    KDBG = os.environ.get('KDBG') == '1'

    sch = Sched()
    import os as _os
    if _os.environ.get('KDBG') == '1':
        d_dbg = nc.dram_tensor("dbg", [16, 128, 1024], F32,
                               kind="ExternalOutput").ap()

    with ExitStack() as _st:
        def sb(name, shape, dtype):
            return _st.enter_context(nc.sbuf_tensor(name, shape, dtype))

        sb_cpool = sb("cn_cpool", [128, 4], F32)
        nc.const_aps.register(F32, -1.0, sb_cpool[:, 0:1])
        nc.const_aps.register(F32, 1.0, sb_cpool[:, 1:2])
        nc.const_aps.register(F32, 2.0, sb_cpool[:, 2:3])
        nc.const_aps.register(F32, EPS, sb_cpool[:, 3:4])
        sb_altcol = sb("cn_altcol", [128, 1], F32)
        sb_iota32 = sb("cn_iota32", [128, 32], F32)
        sb_iota16 = sb("cn_iota16", [128, 16], F32)
        sb_ident = sb("cn_ident", [128, 128], F32)
        sb_mplus = sb("cn_mplus", [128, 254], F32)
        sb_mminus = sb("cn_mminus", [128, 254], F32)
        sb_logdt = sb("in_logdt", [128, 512], F32)
        sb_lamre = sb("in_lamre", [128, 1], F32)
        sb_lamim = sb("in_lamim", [128, 1], F32)
        sb_wre = sb("in_wre", [128, 512], F32)
        sb_wim = sb("in_wim", [128, 512], F32)
        sb_svec = sb("in_svec", [128, 3], F32)
        sb_drep = sb("in_drep", [128, H], F32)
        sb_ab = sb("t_ab", [128, 1024], F32)
        sb_alpha = sb("t_alpha", [128, 1], F32)
        sb_dp1 = sb("t_dp1", [128, H], F32)
        sb_dump = sb("t_dump", [128, 1024], F32)
        sb_stats = sb("t_stats", [128, 16], F32)
        sb_mi = sb("t_mi", [128, 1024], I32)
        sb_arena = sb("arena", [128, ARENA_N], F32)
        ppA = _st.enter_context(nc.psum_tensor("ppA", [128, 512], F32))
        ppB = _st.enter_context(nc.psum_tensor("ppB", [128, 512], F32))
        ppC = _st.enter_context(nc.psum_tensor("ppC", [128, 512], F32))
        ppD = _st.enter_context(nc.psum_tensor("ppD", [128, 512], F32))
        pM = [_st.enter_context(nc.psum_tensor(f"pM{i}", [128, 512], F32))
              for i in range(4)]
        pM0 = pM[0]
        sems = {}
        for s in DMA_SEMS + ENG_SEMS:
            sems[s] = _st.enter_context(nc.semaphore(s))
        blk = _st.enter_context(nc.Block())

        ar = Arena(sb_arena, ARENA_N)

        aC32 = ar.f32([128, 4, 512])
        aS32 = ar.f32([128, 4, 512])
        aCc = ar.bf([128, 4, 512])
        aSc = ar.bf([128, 4, 512])
        aCi32 = ar.f32([128, 4, 512])
        aSi32 = ar.f32([128, 4, 512])
        aCic = ar.bf([128, 4, 512])
        aSic = ar.bf([128, 4, 512])
        aKfRe = ar.f32([128, 4, 1024])
        aKfIm = ar.f32([128, 4, 1024])
        aKfX = ar.f32([128, 2, 1024])
        p1_base = ar.off

        aU = [ar.f32([128, 4, 1024]) for _ in range(2)]
        aUbf = ar.bf([128, 8, 512])
        aUf = ar.f32([128, 8, 512])
        aYbf = ar.bf([128, 8, 512])
        aZX = ar.f32([128, 4, 512])
        aZasm = ar.f32([128, 1024])
        aScr = ar.f32([128, 4, 512])
        p1_end = ar.off

        ar.off = p1_base
        cvA = ar.f32([128, 1024]).bitcast(U8)
        cvB = ar.f32([128, 1024]).bitcast(U8)
        cvAb = ar.f32([128, 2048]).bitcast(BF16)
        cvBb = ar.f32([128, 2048]).bitcast(BF16)

        ar.off = p1_base
        g = {}
        for nm, w in (('ApRe', 1024), ('ApIm', 1024), ('EA', 1024), ('snA', 1024),
                      ('csA', 1024), ('tA', 1024), ('tB', 1024), ('argI', 1024),
                      ('BRe', 512), ('BIm', 512), ('EB', 512), ('snB', 512),
                      ('csB', 512), ('SS1', 512), ('SS2', 512), ('khl', 512),
                      ('SS1b', 512), ('SS2b', 512)):
            g[nm] = ar.f32([128, w])
        aKlh = ar.f32([128, 4, 1024])
        aKbf = ar.bf([128, 4, 1024])
        assert ar.off <= p1_end, (ar.off, p1_end)
        ar.off = p1_end

        ar.off = 0
        aH1hi = ar.bf([128, 32, 512])
        aH1lo = ar.bf([128, 32, 512])
        aH2Bhi = ar.bf([128, 32, 256])
        aH2Blo = ar.bf([128, 32, 256])
        aH2f = ar.f32([128, 512])
        aH2Ahi = ar.bf([128, 2, 256])
        aH2Alo = ar.bf([128, 2, 256])
        aX1tm = ar.f32([128, 4, 1024])
        aX1hi = ar.bf([128, 8, 512])
        aX1lo = ar.bf([128, 8, 512])
        aW0 = ar.bf([128, 2, 8, 128])
        w1_off = ar.off
        aW1 = ar.bf([128, 2, 32, 128])
        aW1_f32 = sb_arena[0:128, w1_off:w1_off + 4096]
        aW2 = ar.bf([128, 4, 1024])
        aZ2 = ar.f32([128, 1024])
        aOutB = ar.f32([128, 2, 1024])
        assert ar.off <= ARENA_N, ar.off

        E = sch.emit
        W = sch.wait
        WC = sch.waitc

        def dma(q, sem, dst, src):
            post = E(q, lambda e, d=dst, r=src: e.dma_start(out=d, in_=r),
                     inc=(sem, 16))
            return (sem, post)

        dma('y', 'dC', aC32, dc['C32'].rearrange("(a p) f -> p a f", p=128))
        dma('y', 'dC', aS32, dc['S32'].rearrange("(a p) f -> p a f", p=128))
        dma('y', 'dC', aCc, dc['Ccorr'].rearrange("(a p) f -> p a f", p=128))
        dma('y', 'dC', aSc, dc['Scorr'].rearrange("(a p) f -> p a f", p=128))
        dma('y', 'dC', aCi32, dc['Ci32'].rearrange("(a p) f -> p a f", p=128))
        dma('y', 'dC', aSi32, dc['Si32'].rearrange("(a p) f -> p a f", p=128))
        dma('y', 'dC', aCic, dc['Cicorr'].rearrange("(a p) f -> p a f", p=128))
        dma('y', 'dC', aSic, dc['Sicorr'].rearrange("(a p) f -> p a f", p=128))
        for nm, dst in (('cpool', sb_cpool), ('altcol', sb_altcol),
                        ('iota32', sb_iota32), ('iota16', sb_iota16),
                        ('ident', sb_ident), ('Mplus', sb_mplus),
                        ('Mminus', sb_mminus)):
            dma('y', 'dC', dst[:, :], dc[nm])
        for src, dst in ((d_logdt, sb_logdt), (d_lamre, sb_lamre), (d_lamim, sb_lamim),
                         (d_wre, sb_wre), (d_wim, sb_wim), (d_svec, sb_svec),
                         (d_Drep, sb_drep)):
            dma('y', 'dC', dst[:, :], src)
        ck_consts = ('dC', sch.cnt['dC'])

        a_col = sb_ab[:, 0:512]
        b_col = sb_ab[:, 512:1024]
        WC('s', ck_consts)
        E('s', lambda e: e.activation(sb_logdt[:, :], sb_logdt[:, :], AF.Exp))
        E('s', lambda e: e.activation(sb_lamre[:, :], sb_lamre[:, :], AF.Exp))
        ck = E('s', lambda e: e.activation(sb_dp1[:, :], sb_drep[:, :], AF.Identity,
                                           bias=1.0), inc=('sA', 1))
        W('v', 'sA', ck)
        E('v', lambda e: e.tensor_mul(sb_stats[:, 0:1], sb_svec[:, 0:1],
                                      sb_svec[:, 1:2]))
        E('v', lambda e: e.tensor_scalar(a_col, sb_logdt[:, :], sb_lamre[:, 0:1],
                                         -1.0, OP.mult, OP.mult))
        E('v', lambda e: e.tensor_scalar(sb_stats[:, 1:2], sb_svec[:, 2:3], 8.0,
                                         None, OP.mult))
        E('v', lambda e: e.tensor_scalar(b_col, sb_logdt[:, :], sb_lamim[:, 0:1],
                                         None, OP.mult))
        E('v', lambda e: e.tensor_mul(sb_alpha[:, :], sb_stats[:, 0:1],
                                      sb_stats[:, 1:2]), inc=('sV', 1))

        if KCUT == 'p0a':
            W('g', 'sV', sch.cnt['sV'])
            ck_dummy = dma('g', 'dOut', d_out[0:128, :], sb_dump[:, :])
            WC('g', ck_dummy)
            return _finish(sch, blk, sems, nc)

        ck_w1out = {}
        ck_w1cvt = {}
        for ic in range(32):
            srcu, ksem = (cvA, 'dKa') if ic % 2 == 0 else (cvB, 'dKb')
            dstb, osem = (cvAb, 'dOa') if ic % 2 == 0 else (cvBb, 'dOb')
            W('y', 'sA', ck_w1cvt.get(ic - 2, 0))
            ckl = dma('y', ksem, srcu[:, 0:4096], d_k1[ic * 128:(ic + 1) * 128, :])
            WC('s', ckl)
            WC('s', ck_w1out.get(ic - 2))
            cka = E('s', lambda e, d=dstb, r=srcu: e.activation(
                d[:, 0:4096], r[:, 0:4096], AF.Identity, bias=-1.0, scale=2.0),
                inc=('sA', 1))
            ck_w1cvt[ic] = cka
            W('g', 'sA', cka)
            for ocg in range(4):
                ck_w1out[ic] = dma(
                    'g', osem,
                    d_w1[ocg * 8:(ocg + 1) * 8, :, ic, :].rearrange(
                        "o q c -> q o c"),
                    dstb[:, ocg * 1024:(ocg + 1) * 1024].rearrange(
                        "p (o c) -> p o c", c=128))
        ck_w0out = {}
        ck_w0cvt = 0
        for ic in range(8):
            W('y', 'sV', ck_w0cvt)
            if ic == 0:
                W('y', 'sA', ck_w1cvt[31])
            ckl0 = dma('y', 'dKa', cvA[:, 0:4096], d_k0t[ic * 128:(ic + 1) * 128, :])
            ckl1 = dma('y', 'dKb', cvB[:, 0:4096], d_k0b[ic * 128:(ic + 1) * 128, :])
            WC('v', ckl0)
            WC('v', ckl1)
            if ic == 0:
                WC('v', ck_w1out[30])
            WC('v', ck_w0out.get(ic - 1))
            ckv = E('v', lambda e: e.tensor_sub(cvAb[:, 0:4096], cvA[:, 0:4096],
                                                cvB[:, 0:4096]), inc=('sV', 1))
            ck_w0cvt = ckv
            W('g', 'sV', ckv)
            for ocg in range(4):
                ck_w0out[ic] = dma(
                    'g', 'dOa',
                    d_w0[ocg * 8:(ocg + 1) * 8, :, ic, :].rearrange(
                        "o q c -> q o c"),
                    cvAb[:, ocg * 1024:(ocg + 1) * 1024].rearrange(
                        "p (o c) -> p o c", c=128))
        ck_w2out = {}
        ck_w2cvt = 0
        for ic in range(32):
            W('y', 'sA', ck_w2cvt)
            if ic == 0:
                W('y', 'sV', ck_w0cvt)
            ckl = dma('y', 'dKa', cvA[:, 0:1024], d_k2[ic * 128:(ic + 1) * 128, :])
            WC('s', ckl)
            if ic == 0:
                WC('s', ck_w1out[31])
            WC('s', ck_w2out.get(ic - 1))
            cka = E('s', lambda e: e.activation(cvBb[:, 0:1024], cvA[:, 0:1024],
                                                AF.Identity, bias=-1.0, scale=2.0),
                    inc=('sA', 1))
            ck_w2cvt = cka
            W('g', 'sA', cka)
            ck_w2out[ic] = dma('g', 'dOb', d_w2[ic, :, :], cvBb[:, 0:1024])
        ck_oa_total = ('dOa', sch.cnt['dOa'])
        ck_ob_total = ('dOb', sch.cnt['dOb'])

        if KCUT == 'p0b':
            WC('g', ck_oa_total)
            WC('g', ck_ob_total)
            ck_dummy = dma('g', 'dOut', d_out[0:128, :], sb_dump[:, :])
            WC('g', ck_dummy)
            return _finish(sch, blk, sems, nc)

        def r3(ap, b):
            return ap.rearrange("p (a b) -> p a b", b=b)

        def sincos(argI, sn_out, cs_out, width):
            tA = g['tA'][:, 0:width]
            tB = g['tB'][:, 0:width]
            mi = sb_mi[:, 0:width]
            sg = sb_dump[:, 0:width]
            cks = None
            for mode in (0, 1):
                if mode == 0:
                    E('v', lambda e: e.tensor_scalar(tA, argI, I2LO, None, OP.mult))
                else:
                    W('v', 'sA', cks)
                    E('v', lambda e: e.tensor_scalar(tA, argI, I2LO, 0.25,
                                                     OP.mult, OP.add))
                E('v', lambda e: e.scalar_tensor_tensor(
                    tA, argI, I2HI, tA, OP.mult, OP.add))
                E('v', lambda e: e.tensor_copy(mi, tA))
                E('v', lambda e: e.tensor_copy(tB, mi))
                E('v', lambda e: e.tensor_sub(tA, tA, tB))
                E('v', lambda e: e.tensor_single_scalar(sg, tA, 0.5, OP.is_gt))
                E('v', lambda e: e.tensor_single_scalar(tA, tA, -0.5, OP.is_lt))
                E('v', lambda e: e.tensor_add(tB, tB, sg))
                E('v', lambda e: e.tensor_sub(tB, tB, tA))
                E('v', lambda e: e.scalar_tensor_tensor(tA, tB, -CHI, argI,
                                                        OP.mult, OP.add))
                E('v', lambda e: e.scalar_tensor_tensor(tA, tB, -CLO, tA,
                                                        OP.mult, OP.add))
                if mode == 1:
                    E('v', lambda e: e.tensor_scalar(tA, tA, PHHI, PHLO,
                                                     OP.add, OP.add))
                ckv = E('v', lambda e: e.tensor_scalar(tA, tA, PI_SAFE, -PI_SAFE,
                                                       OP.min, OP.max), inc=('sV', 1))
                W('s', 'sV', ckv)
                dst = sn_out if mode == 0 else cs_out
                cks = E('s', lambda e, d=dst: e.activation(d, tA, AF.Sin),
                        inc=('sA', 1))
            return cks

        WC('v', ck_oa_total)
        WC('v', ck_ob_total)
        WC('s', ck_oa_total)
        WC('s', ck_ob_total)

        KHG = int(os.environ.get('KHG', '16'))
        ck_ss_mm = 0
        ck_ssmm_hist = {}
        ck_klh_v = 0
        for hg in range(KHG):
            j0 = hg * 32
            a_sl = a_col[:, j0:j0 + 32].unsqueeze(2).broadcast_to([128, 32, 32])
            b_sl = b_col[:, j0:j0 + 32].unsqueeze(2).broadcast_to([128, 32, 32])
            wre_sl = sb_wre[:, j0:j0 + 32].unsqueeze(2).broadcast_to([128, 32, 32])
            wim_sl = sb_wim[:, j0:j0 + 32].unsqueeze(2).broadcast_to([128, 32, 32])
            i32b = sb_iota32[:, :].unsqueeze(1).broadcast_to([128, 32, 32])
            W('v', 'sA', sch.cnt['sA'])
            ckv = E('v', lambda e, a=a_sl, i=i32b: e.tensor_tensor(
                r3(g['ApRe'], 32), a, i, OP.mult), inc=('sV', 1))
            W('s', 'sV', ckv)
            E('s', lambda e: e.activation(g['EA'], g['ApRe'], AF.Exp))
            E('v', lambda e, b=b_sl, i=i32b: e.tensor_tensor(
                r3(g['argI'], 32), b, i, OP.mult))
            cks = sincos(g['argI'], g['snA'], g['csA'], 1024)
            W('v', 'sA', cks)
            E('v', lambda e, w=wre_sl: e.tensor_tensor(r3(g['tA'], 32),
                                                       r3(g['csA'], 32), w, OP.mult))
            E('v', lambda e, w=wim_sl: e.tensor_tensor(r3(g['tB'], 32),
                                                       r3(g['snA'], 32), w, OP.mult))
            E('v', lambda e: e.tensor_sub(g['tA'], g['tA'], g['tB']))
            E('v', lambda e: e.tensor_mul(g['ApRe'], g['tA'], g['EA']))
            E('v', lambda e, w=wre_sl: e.tensor_tensor(r3(g['tA'], 32),
                                                       r3(g['snA'], 32), w, OP.mult))
            E('v', lambda e, w=wim_sl: e.tensor_tensor(r3(g['tB'], 32),
                                                       r3(g['csA'], 32), w, OP.mult))
            E('v', lambda e: e.tensor_add(g['tA'], g['tA'], g['tB']))
            E('v', lambda e: e.tensor_mul(g['ApIm'], g['tA'], g['EA']))
            a_sb = a_col[:, j0:j0 + 32].unsqueeze(2).broadcast_to([128, 32, 16])
            b_sb = b_col[:, j0:j0 + 32].unsqueeze(2).broadcast_to([128, 32, 16])
            i16b = sb_iota16[:, :].unsqueeze(1).broadcast_to([128, 32, 16])
            ckv = E('v', lambda e, a=a_sb, i=i16b: e.tensor_tensor(
                r3(g['BRe'], 16), a, i, OP.mult), inc=('sV', 1))
            W('s', 'sV', ckv)
            E('s', lambda e: e.activation(g['EB'], g['BRe'], AF.Exp))
            E('v', lambda e, b=b_sb, i=i16b: e.tensor_tensor(
                r3(g['argI'][:, 0:512], 16), b, i, OP.mult))
            cks = sincos(g['argI'][:, 0:512], g['snB'], g['csB'], 512)
            W('v', 'sA', cks)
            E('v', lambda e: e.tensor_mul(g['BRe'], g['EB'], g['csB']))
            ckB = E('v', lambda e: e.tensor_mul(g['BIm'], g['EB'], g['snB']),
                    inc=('sV', 1))
            kp = ppA if (hg // 2) % 2 == 0 else ppB
            ApRe_j = r3(g['ApRe'], 32)
            ApIm_j = r3(g['ApIm'], 32)
            BRe_j = r3(g['BRe'], 16)
            BIm_j = r3(g['BIm'], 16)
            for jl in range(32):
                jfull = (hg % 2) * 32 + jl
                jg = hg * 32 + jl
                s1n = 'SS1' if jg % 2 == 0 else 'SS1b'
                s2n = 'SS2' if jg % 2 == 0 else 'SS2b'
                ss1 = r3(g[s1n], 32)
                ss2 = r3(g[s2n], 32)
                apre = ApRe_j[:, jl, :].unsqueeze(1).broadcast_to([128, 16, 32])
                apim = ApIm_j[:, jl, :].unsqueeze(1).broadcast_to([128, 16, 32])
                bre = BRe_j[:, jl, :].unsqueeze(2).broadcast_to([128, 16, 32])
                bim = BIm_j[:, jl, :].unsqueeze(2).broadcast_to([128, 16, 32])
                W('v', 'sV', ckB)
                W('v', 'sT', ck_ssmm_hist.get(jg - 2, 0))
                E('v', lambda e, a=apre, b=bre, o=ss1: e.tensor_tensor(o, a, b, OP.mult))
                ckss = E('v', lambda e, a=apim, b=bim, o=ss2: e.tensor_tensor(
                    o, a, b, OP.mult), inc=('sV', 1))
                W('t', 'sV', ckss)
                c0 = 126 - 2 * jfull
                E('t', lambda e, p=kp, c=c0, s1=s1n: e.matmul(
                    p[:, :], sb_mplus[:, c:c + 128], g[s1],
                    start=(c == 126), stop=False))
                ck_ss_mm = E('t', lambda e, p=kp, c=c0, s2=s2n,
                             lst=(jfull == 63): e.matmul(
                    p[:, :], sb_mminus[:, c:c + 128], g[s2],
                    start=False, stop=lst), inc=('sT', 1))
                ck_ssmm_hist[jg] = ck_ss_mm
            if hg % 2 == 1:
                grp = hg // 2
                W('v', 'sT', ck_ss_mm)
                ckv = E('v', lambda e, p=kp: e.tensor_copy(g['khl'], p[:, :]),
                        inc=('sV', 1))
                khl2 = r3(g['khl'], 128)
                ck_tev = {}
                for lc in range(4):
                    pt = ppC if lc % 2 == 0 else ppD
                    W('t', 'sV', ckv)
                    W('t', 'sV', ck_tev.get(lc - 2, 0))
                    ckt = E('t', lambda e, p=pt, l=lc: e.transpose(
                        p[:, 0:128], khl2[:, l, :], sb_ident[:, :]), inc=('sT', 1))
                    W('v', 'sT', ckt)
                    E('v', lambda e, p=pt, l=lc, gg=grp: e.tensor_copy(
                        aKlh[:, l, gg * 128:(gg + 1) * 128], p[:, 0:128]))
                    ck_klh_v = E('v', lambda e, p=pt, l=lc, gg=grp: e.tensor_copy(
                        aKbf[:, l, gg * 128:(gg + 1) * 128], p[:, 0:128]),
                        inc=('sV', 1))
                    ck_tev[lc] = ck_klh_v
        ck_k_done = ck_klh_v

        if KCUT == 'p0c':
            W('g', 'sV', sch.cnt['sV'])
            W('g', 'sT', sch.cnt['sT'])
            ck_dummy = dma('g', 'dOut', d_out[0:128, :], sb_dump[:, :])
            WC('g', ck_dummy)
            return _finish(sch, blk, sems, nc)

        ck_kf_ev = {}
        for comp in range(2):
            dst = aKfRe if comp == 0 else aKfIm
            c32v, ccv = (aC32, aCc) if comp == 0 else (aS32, aSc)
            for fc in range(4):
                for hh in range(2):
                    idx = comp * 8 + fc * 2 + hh
                    pp = ppA if idx % 2 == 0 else ppB
                    W('t', 'sV', ck_k_done)
                    W('t', 'sV', ck_kf_ev.get(idx - 2, 0))
                    for lc in range(4):
                        E('t', lambda e, p=pp, l=lc, f=fc, h=hh, cv=c32v: e.matmul(
                            p[:, :], cv[:, l, f * 128:(f + 1) * 128],
                            aKlh[:, l, h * 512:(h + 1) * 512],
                            start=(l == 0), stop=False))
                    ckt = None
                    for lc in range(4):
                        ckt = E('t', lambda e, p=pp, l=lc, f=fc, h=hh, cv=ccv:
                                e.matmul(
                                    p[:, :], cv[:, l, f * 128:(f + 1) * 128],
                                    aKbf[:, l, h * 512:(h + 1) * 512],
                                    start=False, stop=(l == 3)),
                                inc=('sT', 1) if lc == 3 else None)
                    W('v', 'sT', ckt)
                    ck_kf_ev[idx] = E(
                        'v', lambda e, p=pp, f=fc, h=hh, d=dst: e.tensor_copy(
                            d[:, f, h * 512:(h + 1) * 512], p[:, :]), inc=('sV', 1))
        ck_last_kf = max(ck_kf_ev.values())
        W('v', 'sV', ck_last_kf)
        E('v', lambda e: e.tensor_copy(aKfX[:, 0, :], aKfIm[:, 0, :]))
        E('v', lambda e: e.memset(aKfX[0:1, 0, :], 0.0))
        E('v', lambda e: e.tensor_copy(aKfX[:, 1, :], aKfRe[:, 0, :]))
        E('v', lambda e: e.tensor_copy(aKfX[0:1, 1, :], aKfIm[0:1, 0, :]),
          inc=('sV', 1))
        ck_p0_v = sch.cnt['sV']
        ck_p0_t = sch.cnt['sT']
        if KDBG:
            W('g', 'sV', ck_p0_v)
            W('g', 'sT', ck_p0_t)
            WC('g', ck_oa_total)
            WC('g', ck_ob_total)
            ckw1t = dma('g', 'dKb', cvAb.rearrange("p (a b) -> p a b", b=128),
                        d_w1[1, :, :, :])
            WC('g', ckw1t)
            dma('g', 'dOut', d_dbg[3, :, :],
                cvAb.bitcast(F32)[:, 0:1024])
            dma('g', 'dOut', d_dbg[4, :, :],
                cvAb.bitcast(F32)[:, 1024:2048])
            dma('g', 'dOut', d_dbg[0, :, :], aKfRe[:, 0, :])
            dma('g', 'dOut', d_dbg[1, :, :], aKfIm[:, 0, :])
            dma('g', 'dOut', d_dbg[2, :, :], aKfX[:, 0, :])
            dma('g', 'dOut', d_dbg[5, :, 0:1024], sb_ab[:, :])
            ckdx = dma('g', 'dOut', d_dbg[6, :, 0:512], sb_wre[:, :])
            WC('t', ckdx)

        if KCUT == 'p0':
            W('g', 'sV', ck_p0_v)
            W('g', 'sT', ck_p0_t)
            ck_dummy = dma('g', 'dOut', d_out[0:128, :], sb_dump[:, :])
            WC('g', ck_dummy)
            return _finish(sch, blk, sems, nc)

        ck_uh = {}

        def load_uhalf(h, ck_t=None, ck_s=None, ck_v=None):
            if h > NBLK:
                return
            if h < 2:
                W('g', 'sV', ck_p0_v)
                W('g', 'sT', ck_p0_t)
            else:
                W('g', 'sT', sch.cnt['sT'] if ck_t is None else ck_t)
                W('g', 'sA', sch.cnt['sA'] if ck_s is None else ck_s)
                if ck_v is not None:
                    W('g', 'sV', ck_v)
            ck_uh[h] = dma('g', 'dU0' if h % 2 == 0 else 'dU1', aU[h % 2],
                           d_u[h * 512:(h + 1) * 512, :].rearrange(
                               "(a p) c -> p a c", p=128))

        load_uhalf(0)
        load_uhalf(1)

        def ln_emit(tile, outtile):
            st = sb_stats
            ckv = E('v', lambda e, ti=tile: e.tensor_reduce(st[:, 0:1], ti, AXX,
                                                            OP.add), inc=('sV', 1))
            W('s', 'sV', ckv)
            cks = E('s', lambda e, ti=tile: e.activation(sb_dump[:, :], ti, AF.Square,
                                                         accum_out=st[:, 1:2]),
                    inc=('sA', 1))
            W('v', 'sA', cks)
            E('v', lambda e: e.tensor_scalar(st[:, 2:3], st[:, 0:1], R1024,
                                             None, OP.mult))
            E('v', lambda e: e.tensor_scalar(st[:, 3:4], st[:, 1:2], R1024,
                                             None, OP.mult))
            ckv = E('v', lambda e: e.tensor_scalar(st[:, 8:9], st[:, 0:1], -R1024,
                                                   None, OP.mult), inc=('sV', 1))
            W('s', 'sV', ckv)
            cks = E('s', lambda e: e.activation(st[:, 4:5], st[:, 2:3], AF.Square),
                    inc=('sA', 1))
            W('v', 'sA', cks)
            ckv = E('v', lambda e: e.tensor_sub(st[:, 5:6], st[:, 3:4], st[:, 4:5]),
                    inc=('sV', 1))
            W('s', 'sV', ckv)
            cks = E('s', lambda e: e.activation(st[:, 6:7], st[:, 5:6], AF.Sqrt,
                                                bias=EPS), inc=('sA', 1))
            W('v', 'sA', cks)
            ckv = E('v', lambda e: e.reciprocal(st[:, 7:8], st[:, 6:7]),
                    inc=('sV', 1))
            W('s', 'sV', ckv)
            cks = E('s', lambda e: e.activation(st[:, 9:10], st[:, 8:9], AF.Copy,
                                                scale=st[:, 7:8]), inc=('sA', 1))
            W('v', 'sA', cks)
            return E('v', lambda e, ti=tile, o=outtile: e.scalar_tensor_tensor(
                o, ti, st[:, 7:8], st[:, 9:10].broadcast_to([128, 1024]),
                OP.mult, OP.add), inc=('sV', 1))

        ck_fwd_last = 0
        ck_inv_ev = 0
        zbufs = (aZasm, sb_ab)
        spill_ck = {}
        pend_ln = None
        ztile = 0

        def flush_ln():
            nonlocal pend_ln
            ti, buf = pend_ln
            ckln = ln_emit(buf[:, :], buf[:, :])
            W('g', 'sV', ckln)
            spill_ck[ti] = dma('g', 'dSp',
                               d_x1[ti * 128:(ti + 1) * 128, :], buf[:, :])
            pend_ln = None

        for blkid in range(NBLK):
            hA, hB = blkid, blkid + 1
            uA, uB = aU[hA % 2], aU[hB % 2]
            ck_inv_prev = ck_inv_ev
            ck_ubf_last = 0

            for chh in range(2):
                def usl(tc, uA=uA, uB=uB, c=chh):
                    return (uA if tc < 4 else uB)[:, tc % 4, c * 512:(c + 1) * 512]

                WC('s', ck_uh[hA])
                WC('s', ck_uh[hB])
                W('s', 'sT', ck_fwd_last)
                ck_ubf = None
                for tc in range(8):
                    ck_ubf = E('s', lambda e, t=tc, us=usl: e.activation(
                        aUbf[:, t, :], us(t), AF.Copy),
                        inc=('sA', 1) if tc == 7 else None)
                ck_ubf_last = ck_ubf

                ck_yf_v = None
                ck_yf_s = None

                def emit_pw(fc, c=chh):
                    nonlocal ck_yf_v, ck_yf_s
                    sl = slice(c * 512, (c + 1) * 512)
                    kre = aKfRe[:, fc, sl]
                    kim = aKfX[:, 0, sl] if fc == 0 else aKfIm[:, fc, sl]
                    kreI = aKfX[:, 1, sl] if fc == 0 else aKfRe[:, fc, sl]
                    ure = aUf[:, fc, :]
                    uim = aUf[:, 4 + fc, :]
                    E('v', lambda e, k=kreI, u=uim: e.tensor_tensor(
                        aScr[:, 0, :], k, u, OP.mult))
                    E('v', lambda e, k=kim, u=ure: e.tensor_tensor(
                        aScr[:, 1, :], k, u, OP.mult))
                    E('v', lambda e, k=kre, u=ure: e.tensor_tensor(
                        aScr[:, 2, :], k, u, OP.mult))
                    E('v', lambda e, k=kim, u=uim: e.tensor_tensor(
                        aScr[:, 3, :], k, u, OP.mult))
                    E('v', lambda e, o=ure: e.tensor_sub(o, aScr[:, 2, :],
                                                         aScr[:, 3, :]))
                    ck_yf_v = E('v', lambda e, o=uim: e.tensor_add(
                        o, aScr[:, 0, :], aScr[:, 1, :]), inc=('sV', 1))
                    W('s', 'sV', ck_yf_v)
                    E('s', lambda e, o=ure, f=fc: e.activation(
                        aYbf[:, f, :], o, AF.Copy))
                    ck_yf_s = E('s', lambda e, o=uim, f=fc: e.activation(
                        aYbf[:, 4 + f, :], o, AF.Copy), inc=('sA', 1))

                WC('t', ck_uh[hA])
                WC('t', ck_uh[hB])
                W('t', 'sA', ck_ubf)
                W('t', 'sV', ck_inv_ev)
                ck_uf_ev = {}
                for comp in range(2):
                    c32v, ccv = (aC32, aCc) if comp == 0 else (aS32, aSc)
                    for fc in range(4):
                        gi = comp * 4 + fc
                        g1 = ppA if gi % 2 == 0 else ppC
                        g2 = ppB if gi % 2 == 0 else ppD
                        W('t', 'sV', ck_uf_ev.get(gi - 2, 0))
                        ckt = None
                        for tc in range(4):
                            for half in range(2):
                                pp = g1 if half == 0 else g2
                                E('t', lambda e, p=pp, t=tc, hf=half, f=fc, us=usl,
                                  cv=c32v: e.matmul(
                                    p[:, :], cv[:, t, f * 128:(f + 1) * 128],
                                    us(t + 4 * hf), start=(t == 0), stop=False))
                        for tc in range(4):
                            for half in range(2):
                                pp = g1 if half == 0 else g2
                                ckt = E('t', lambda e, p=pp, t=tc, hf=half, f=fc,
                                        cv=ccv: e.matmul(
                                            p[:, :], cv[:, t, f * 128:(f + 1) * 128],
                                            aUbf[:, t + 4 * hf, :],
                                            start=False, stop=(t == 3)),
                                        inc=('sT', 1) if (half == 1 and tc == 3)
                                        else None)
                        W('v', 'sT', ckt)
                        E('v', lambda e, p=g2: e.tensor_copy(aScr[:, 0, :], p[:, :]))
                        ck_uf_ev[gi] = E(
                            'v', lambda e, p=g1, sl=gi: e.scalar_tensor_tensor(
                                aUf[:, sl, :], aScr[:, 0, :], sb_altcol[:, 0:1],
                                p[:, :], OP.mult, OP.add), inc=('sV', 1))
                        if comp == 1:
                            emit_pw(fc)
                ck_fwd_last = sch.cnt['sT']
                if chh == 1:
                    load_uhalf(blkid + 2, ck_t=ck_fwd_last, ck_s=ck_ubf_last,
                               ck_v=ck_inv_prev)

                for tb in range(4):
                    pp = ppA if tb % 2 == 0 else ppB
                    W('t', 'sV', ck_yf_v)
                    W('t', 'sA', ck_yf_s)
                    W('t', 'sV', ck_inv_ev)
                    for comp in range(2):
                        civ = aCi32 if comp == 0 else aSi32
                        for fc in range(4):
                            E('t', lambda e, p=pp, f=fc, t=tb, cv=civ, cp=comp:
                              e.matmul(
                                p[:, :], cv[:, f, t * 128:(t + 1) * 128],
                                aUf[:, cp * 4 + f, :], start=(cp == 0 and f == 0),
                                stop=False))
                    ckt = None
                    for comp in range(2):
                        cicv = aCic if comp == 0 else aSic
                        for fc in range(4):
                            islast = (comp == 1 and fc == 3)
                            ckt = E('t', lambda e, p=pp, f=fc, t=tb, cv=cicv, cp=comp,
                                    lst=islast: e.matmul(
                                        p[:, :], cv[:, f, t * 128:(t + 1) * 128],
                                        aYbf[:, cp * 4 + f, :], start=False, stop=lst),
                                    inc=('sT', 1) if islast else None)
                    W('v', 'sT', ckt)
                    E('v', lambda e, t=tb, c=chh, u=uB: e.tensor_tensor(
                        aScr[:, 0, :], u[:, t, c * 512:(c + 1) * 512],
                        sb_dp1[:, c * 512:(c + 1) * 512], OP.mult))
                    if chh == 0:
                        ck_inv_ev = E('v', lambda e, p=pp, t=tb: e.tensor_add(
                            aZX[:, t, :], aScr[:, 0, :], p[:, :]), inc=('sV', 1))
                    else:
                        buf = zbufs[ztile % 2]
                        WC('v', spill_ck.get(ztile - 2))
                        E('v', lambda e, t=tb, b=buf: e.tensor_copy(
                            b[:, 0:512], aZX[:, t, :]))
                        ck_inv_ev = E('v', lambda e, p=pp, b=buf: e.tensor_add(
                            b[:, 512:1024], aScr[:, 0, :], p[:, :]),
                            inc=('sV', 1))
                        if pend_ln is not None:
                            flush_ln()
                        pend_ln = (ztile, buf)
                        ztile += 1
        flush_ln()

        if KCUT == 'p1':
            W('g', 'dSp', sch.cnt['dSp'])
            ck_dummy = dma('g', 'dOut', d_out[0:128, :], sb_dump[:, :])
            WC('g', ck_dummy)
            return _finish(sch, blk, sems, nc)

        ck_bar = {s: sch.cnt[s] for s in DMA_SEMS + ENG_SEMS}
        for eng in ('t', 'v', 's', 'y', 'g'):
            for sname, val in ck_bar.items():
                W(eng, sname, val)

        W0SEM = ('dW0a', 'dW0b')
        W1SEM = ('dW1a', 'dW1b')
        W2SEM = ('dW2a', 'dW2b', 'dW2c', 'dW2d')
        ck_w0_use = {}
        ck_w1_use = {}
        ck_w2_use = {}
        ck_w2b_use = {}
        ck_z2b = 0
        ck_out = {}
        def load_w0(p, oc, ck_w0_load, gate):
            W('y', 'sT', gate)
            ck_w0_load[oc] = dma('y', W0SEM[oc % 2], aW0[:, oc % 2, :, :],
                                 d_w0[oc, :, :, :])

        import os as _os2
        W1Q = _os2.environ.get('KW1Q', 'y')
        assert W1Q in ('y', 'g', 's')
        W1R1 = _os2.environ.get('KW1R1') == '1'

        def load_w1(p, oc, ck_w1_load, gate):
            W(W1Q, 'sT', gate)
            slot = 0 if W1R1 else oc % 2
            ck_w1_load[oc] = dma(
                W1Q, W1SEM[oc % 2],
                aW1_f32[:, slot * 2048:(slot + 1) * 2048],
                d_w1[oc, :, :, :].rearrange("q a b -> q (a b)").bitcast(F32))

        def load_w2(p, oc, ck_w2_load, gate):
            W('y', 'sT', gate)
            ck_w2_load[oc] = dma('y', W2SEM[oc % 4], aW2[:, oc % 4, :],
                                 d_w2[oc, :, :])

        ck_x1_half = {}
        pend_tr = {}

        def emit_transposes(i0, i1, ck_x1t, ck_tr):
            for i in range(i0, i1):
                tc, fc = divmod(i, 8)
                pt = ppC if i % 2 == 0 else ppD
                W('t', 'sV', ck_tr.get(i - 2, 0))
                ckt = E('t', lambda e, q=pt, t=tc, f=fc: e.transpose(
                    q[:, 0:128], aX1tm[:, t, f * 128:(f + 1) * 128],
                    sb_ident[:, :]), inc=('sT', 1))
                W('s', 'sT', ckt)
                ck_x1t[i] = E('s', lambda e, q=pt, t=tc, f=fc: e.activation(
                    aX1hi[:, f, t * 128:(t + 1) * 128], q[:, 0:128], AF.Copy),
                    inc=('sA', 1))
                W('v', 'sA', ck_x1t[i])
                ck_tr[i] = E('v', lambda e, q=pt, t=tc, f=fc: e.tensor_sub(
                    aX1lo[:, f, t * 128:(t + 1) * 128], q[:, 0:128],
                    aX1hi[:, f, t * 128:(t + 1) * 128]), inc=('sV', 1))

        def load_x1_half(p, half, gate_out=None):
            if gate_out is not None:
                WC('g', gate_out)
            r0 = p * 512 + half * 256
            ck_x1_half[(p, half)] = dma(
                'g', 'dU0' if half == 0 else 'dU1',
                aX1tm[:, half * 2:half * 2 + 2, :],
                d_x1[r0:r0 + 256, :].rearrange("(a q) c -> q a c", q=128))

        for p in range(NPAIR):
            if p == 0:
                load_x1_half(0, 0)
                load_x1_half(0, 1)
            ck_w0_load = {}
            ck_w1_load = {}
            ck_w2_load = {}
            ck_w2b_load = {}
            for oc in range(2):
                load_w0(p, oc, ck_w0_load, ck_w0_use.get((p - 1, 30 + oc), 0))

            if p == 0:
                ck_x1t = {}
                ck_tr = {}
                WC('t', ck_x1_half[(0, 0)])
                emit_transposes(0, 16, ck_x1t, ck_tr)
            else:
                ck_x1t, ck_tr = pend_tr.pop(p)
            WC('t', ck_x1_half[(p, 1)])
            emit_transposes(16, 32, ck_x1t, ck_tr)
            ck_tr_last = ck_tr[31]

            ck_h1_ev = {}
            for oc in range(32):
                pp = ppA if oc % 2 == 0 else ppB
                W('t', 'sV', ck_h1_ev.get(oc - 2, 0))
                WC('t', ck_w0_load[oc])
                W('t', 'sV', ck_tr_last)
                ckt = None
                for ic in range(8):
                    E('t', lambda e, q=pp, o=oc, i=ic: e.matmul(
                        q[:, :], aW0[:, o % 2, i, :], aX1hi[:, i, :],
                        start=(i == 0), stop=False))
                    ckt = E('t', lambda e, q=pp, o=oc, i=ic: e.matmul(
                        q[:, :], aW0[:, o % 2, i, :], aX1lo[:, i, :],
                        start=False, stop=(i == 7)),
                        inc=('sT', 1) if ic == 7 else None)
                ck_w0_use[(p, oc)] = ckt
                if oc + 2 <= 31:
                    load_w0(p, oc + 2, ck_w0_load, ckt)
                elif oc == 30:
                    if W1R1:
                        load_w1(p, 0, ck_w1_load, ck_w1_use.get((p - 1, 31), 0))
                    else:
                        for o2 in range(2):
                            load_w1(p, o2, ck_w1_load,
                                    ck_w1_use.get((p - 1, 30 + o2), 0))
                    for o2 in range(4):
                        load_w2(p, o2, ck_w2_load,
                                ck_w2b_use.get((p - 1, 28 + o2), 0))
                W('s', 'sT', ckt)
                W('s', 'sV', ck_h1_ev.get(oc - 1, 0))
                cka = E('s', lambda e, q=pp: e.activation(aH2f[:, :], q[:, :],
                                                          AF.Relu), inc=('sA', 1))
                W('v', 'sA', cka)
                E('v', lambda e, o=oc: e.tensor_copy(aH1hi[:, o, :], aH2f[:, :]))
                ck_h1_ev[oc] = E('v', lambda e, o=oc: e.tensor_sub(
                    aH1lo[:, o, :], aH2f[:, :], aH1hi[:, o, :]), inc=('sV', 1))
            ck_h1_all = ck_h1_ev[31]
            if KDBG and p == 0:
                W('g', 'sV', ck_h1_all)
                h1flat = aH1hi.rearrange("q a b -> q (a b)").bitcast(F32)
                for di in range(4):
                    dma('g', 'dOut', d_dbg[11 + di, :, :],
                        h1flat[:, di * 1024:(di + 1) * 1024])
                x1flat = aX1hi.rearrange("q a b -> q (a b)").bitcast(F32)
                ckdx = dma('g', 'dOut', d_dbg[15, :, 0:1024], x1flat[:, 0:1024])
                WC('t', ckdx)

            ck_h2a_ev = {}
            ck_h2b_ev = {}
            ck_mm2a = {}

            def emit_mm2a(oc):
                sl = oc % 2
                W('t', 'sV', ck_h2a_ev[oc])
                WC('t', ck_w2_load[oc])
                if oc == 0:
                    W('t', 'sV', ck_z2b)
                ckt = None
                for hilo in range(2):
                    src = aH2Ahi if hilo == 0 else aH2Alo
                    for tb in range(2):
                        for chh in range(2):
                            ckt = E('t', lambda e, s2=sl, hl=hilo, t=tb, c=chh, o=oc,
                                    sr=src: e.matmul(
                                        pM[t * 2 + c][:, :],
                                        sr[:, s2, t * 128:(t + 1) * 128],
                                        aW2[:, o % 4, c * 512:(c + 1) * 512],
                                        start=(o == 0 and hl == 0),
                                        stop=(o == 31 and hl == 1)),
                                    inc=('sT', 1) if (hilo == 1 and tb == 1 and
                                                      chh == 1) else None)
                ck_mm2a[oc] = ckt
                ck_w2_use[(p, oc)] = ckt
                if oc + 4 <= 31:
                    load_w2(p, oc + 4, ck_w2_load, ckt)
                elif oc == 31:
                    for o2 in range(4):
                        W('y', 'sT', ck_w2_use[(p, 28 + o2)])
                        ck_w2b_load[o2] = dma('y', W2SEM[o2 % 4], aW2[:, o2 % 4, :],
                                              d_w2[o2, :, :])

            for oc in range(32):
                pp = ppA if oc % 2 == 0 else ppB
                sl = oc % 2
                W('t', 'sV', ck_h2a_ev.get(oc - 2, 0))
                W('t', 'sV', ck_h1_all)
                WC('t', ck_w1_load[oc])
                ckt = None
                w1slot = 0 if W1R1 else oc % 2
                for ic in range(32):
                    E('t', lambda e, q=pp, s2=w1slot, i=ic: e.matmul(
                        q[:, :], aW1[:, s2, i, :], aH1hi[:, i, :],
                        start=(i == 0), stop=False))
                    ckt = E('t', lambda e, q=pp, s2=w1slot, i=ic: e.matmul(
                        q[:, :], aW1[:, s2, i, :], aH1lo[:, i, :],
                        start=False, stop=(i == 31)),
                        inc=('sT', 1) if ic == 31 else None)
                ck_w1_use[(p, oc)] = ckt
                if W1R1 and oc + 1 <= 31:
                    load_w1(p, oc + 1, ck_w1_load, ckt)
                elif (not W1R1) and oc + 2 <= 31:
                    load_w1(p, oc + 2, ck_w1_load, ckt)
                W('s', 'sT', ckt)
                W('s', 'sV', ck_h2b_ev.get(oc - 1, 0))
                cka = E('s', lambda e, q=pp: e.activation(aH2f[:, :], q[:, :],
                                                          AF.Relu), inc=('sA', 1))
                W('v', 'sA', cka)
                W('v', 'sT', ck_mm2a.get(oc - 2, 0))
                E('v', lambda e, s2=sl: e.tensor_copy(aH2Ahi[:, s2, :],
                                                      aH2f[:, 0:256]))
                ck_h2a_ev[oc] = E('v', lambda e, s2=sl: e.tensor_sub(
                    aH2Alo[:, s2, :], aH2f[:, 0:256], aH2Ahi[:, s2, :]),
                    inc=('sV', 1))
                E('v', lambda e, o=oc: e.tensor_copy(aH2Bhi[:, o, :],
                                                     aH2f[:, 256:512]))
                ck_h2b_ev[oc] = E('v', lambda e, o=oc: e.tensor_sub(
                    aH2Blo[:, o, :], aH2f[:, 256:512], aH2Bhi[:, o, :]),
                    inc=('sV', 1))
                if oc >= 1:
                    emit_mm2a(oc - 1)
            emit_mm2a(31)

            ck_z2a = 0
            ck_z2a_stt = 0
            for tb in range(2):
                W('v', 'sT', ck_mm2a[31])
                if p > 0:
                    WC('v', ck_out.get((p - 1, 1)))
                ck_stt = None
                for chh in range(2):
                    ck_stt = E('v', lambda e, t=tb, c=chh: e.scalar_tensor_tensor(
                        aZ2[:, c * 512:(c + 1) * 512], pM[t * 2 + c][:, :],
                        sb_alpha[:, 0:1], aX1tm[:, t, c * 512:(c + 1) * 512],
                        OP.mult, OP.add),
                        inc=('sV', 1) if (chh == 1 and
                                          (tb == 1 or (KDBG and p == 0)))
                        else None)
                if tb == 1:
                    ck_z2a_stt = ck_stt
                if KDBG and p == 0 and tb == 0:
                    W('s', 'sV', ck_stt)
                    E('s', lambda e: e.activation(sb_dump[:, 0:512],
                                                  pM[0][:, :], AF.Copy))
                    ckps = E('s', lambda e: e.activation(sb_dump[:, 512:1024],
                                                         pM[1][:, :], AF.Copy),
                             inc=('sA', 1))
                    W('g', 'sA', ckps)
                    dma('g', 'dOut', d_dbg[7, :, :], aZ2[:, :])
                    dma('g', 'dOut', d_dbg[8, :, 0:512], sb_dump[:, 0:512])
                    E('g', lambda e: e.dma_start(
                        out=d_dbg[8, :, 512:513], in_=sb_alpha[:, 0:1],
                        allow_slow_non_contiguous=True), inc=('dOut', 16))
                    dma('g', 'dOut', d_dbg[9, :, 0:512], sb_dump[:, 512:1024])
                    ckdz = dma('g', 'dOut', d_dbg[10, :, :], aX1tm[:, 0, :])
                    WC('v', ckdz)
                    WC('t', ckdz)
                    WC('s', ckdz)
                ck_z2a = ln_emit(aZ2[:, :], aOutB[:, tb, :])
            W('g', 'sV', ck_z2a)
            ck_out[(p, 0)] = dma('g', 'dOut',
                                 d_out[p * 512:p * 512 + 256, :].rearrange(
                                     "(a q) c -> q a c", q=128), aOutB[:, 0:2, :])
            if p + 1 < NPAIR:
                load_x1_half(p + 1, 0)

            ck_mm2b = 0
            for oc in range(32):
                W('t', 'sV', ck_z2a_stt)
                WC('t', ck_w2b_load[oc])
                W('t', 'sV', ck_h2b_ev[31])
                ckt = None
                for hilo in range(2):
                    src = aH2Bhi if hilo == 0 else aH2Blo
                    for tb in range(2):
                        for chh in range(2):
                            ckt = E('t', lambda e, hl=hilo, t=tb, c=chh, o=oc,
                                    sr=src: e.matmul(
                                        pM[t * 2 + c][:, :],
                                        sr[:, o, t * 128:(t + 1) * 128],
                                        aW2[:, o % 4, c * 512:(c + 1) * 512],
                                        start=(o == 0 and hl == 0),
                                        stop=(o == 31 and hl == 1)),
                                    inc=('sT', 1) if (hilo == 1 and tb == 1 and
                                                      chh == 1) else None)
                ck_mm2b = ckt
                ck_w2b_use[(p, oc)] = ckt
                if oc + 4 <= 31:
                    W('y', 'sT', ckt)
                    ck_w2b_load[oc + 4] = dma('y', W2SEM[(oc + 4) % 4],
                                              aW2[:, (oc + 4) % 4, :],
                                              d_w2[oc + 4, :, :])
                if oc == 16 and p + 1 < NPAIR:
                    ntr_x1t = {}
                    ntr_tr = {}
                    WC('t', ck_x1_half[(p + 1, 0)])
                    emit_transposes(0, 16, ntr_x1t, ntr_tr)
                    pend_tr[p + 1] = (ntr_x1t, ntr_tr)
            ck_z2b_stt = 0
            for tb in range(2):
                W('v', 'sT', ck_mm2b)
                WC('v', ck_out[(p, 0)])
                for chh in range(2):
                    ck_z2b_stt = E(
                        'v', lambda e, t=tb, c=chh: e.scalar_tensor_tensor(
                            aZ2[:, c * 512:(c + 1) * 512], pM[t * 2 + c][:, :],
                            sb_alpha[:, 0:1],
                            aX1tm[:, 2 + t, c * 512:(c + 1) * 512],
                            OP.mult, OP.add),
                        inc=('sV', 1) if (tb == 1 and chh == 1) else None)
                ck_z2b = ln_emit(aZ2[:, :], aOutB[:, tb, :])
            if p + 1 < NPAIR:
                W('g', 'sV', ck_z2b_stt)
                load_x1_half(p + 1, 1)
            W('g', 'sV', ck_z2b)
            ck_out[(p, 1)] = dma('g', 'dOut',
                                 d_out[p * 512 + 256:(p + 1) * 512, :].rearrange(
                                     "(a q) c -> q a c", q=128), aOutB[:, 0:2, :])

        W('g', 'dOut', sch.cnt['dOut'])
        return _finish(sch, blk, sems, nc)


def _finish(sch, blk, sems, nc):
    @blk.tensor
    def _(eng):
        sch.run('t', eng, sems)

    @blk.vector
    def _(eng):
        sch.run('v', eng, sems)

    @blk.scalar
    def _(eng):
        sch.run('s', eng, sems)

    @blk.sync
    def _(eng):
        sch.run('y', eng, sems)

    @blk.gpsimd
    def _(eng):
        sch.run('g', eng, sems)

    return nc


def _prep_inputs(inputs):
    u = np.asarray(inputs['L0_attn_u'])
    D = np.ascontiguousarray(np.asarray(inputs['L0_attn_D'], dtype=np.float32))
    lam_re = np.asarray(inputs['L0_attn_Lambda_re'], dtype=np.float32)
    lam_im = np.asarray(inputs['L0_attn_Lambda_im'], dtype=np.float32)
    log_dt = np.asarray(inputs['L0_attn_log_dt'], dtype=np.float32)
    W_re = np.asarray(inputs['L0_attn_W_re'], dtype=np.float32)
    W_im = np.asarray(inputs['L0_attn_W_im'], dtype=np.float32)
    k0 = np.asarray(inputs['L0_ffn_k0'])
    k1 = np.asarray(inputs['L0_ffn_k1'])
    k2 = np.asarray(inputs['L0_ffn_k2'])
    s0 = np.float32(np.asarray(inputs['L0_ffn_s0']))
    s1 = np.float32(np.asarray(inputs['L0_ffn_s1']))
    s2 = np.float32(np.asarray(inputs['L0_ffn_s2']))

    logdt_arr = np.empty((128, 512), np.float32)
    ld = log_dt.reshape(512, 2)
    logdt_arr[0::2, :] = ld[:, 0][None, :]
    logdt_arr[1::2, :] = ld[:, 1][None, :]
    lamre_col = np.ascontiguousarray(np.repeat(lam_re, 2)[:, None])
    lamim_col = np.ascontiguousarray(np.repeat(lam_im, 2)[:, None])
    wre_arr = np.empty((128, 512), np.float32)
    wim_arr = np.empty((128, 512), np.float32)
    wr = W_re.reshape(512, 2, 64)
    wi = W_im.reshape(512, 2, 64)
    for par in range(2):
        wre_arr[par::2, :] = wr[:, par, :].T
        wim_arr[par::2, :] = wi[:, par, :].T
    k0t = np.ascontiguousarray(k0[:1024]).view(np.uint8)
    k0b = np.ascontiguousarray(k0[1024:]).view(np.uint8)
    k1u = np.ascontiguousarray(k1).view(np.uint8)
    k2u = np.ascontiguousarray(k2).view(np.uint8)
    svec = np.broadcast_to(np.array([s0, s1, s2], np.float32), (128, 3)).copy()
    D_rep = np.broadcast_to(D, (128, H)).copy()

    consts = _host_consts()
    shared = dict(logdt_arr=logdt_arr, lamre_col=lamre_col, lamim_col=lamim_col,
                  wre_arr=wre_arr, wim_arr=wim_arr, k0t=k0t, k0b=k0b, k1=k1u,
                  k2=k2u, svec=svec, D_rep=D_rep)
    for nm, arr in consts.items():
        shared['c_' + nm] = np.ascontiguousarray(arr)

    in_maps = []
    for c in range(8):
        b, half = c // 2, c % 2
        tok0 = half * 4096
        if half == 0:
            halo = np.zeros((512, H), np.float32)
        else:
            halo = u[b, tok0 - 512:tok0]
        shard = np.concatenate([halo, u[b, tok0:tok0 + 4096]], axis=0)
        m = dict(shared)
        m['u_shard'] = np.ascontiguousarray(shard, dtype=np.float32)
        in_maps.append(m)
    return in_maps


_NC_CACHE = None


def kernel(**inputs):
    global _NC_CACHE
    if _NC_CACHE is None:
        _NC_CACHE = build_nc()
    nc = _NC_CACHE
    in_maps = _prep_inputs(inputs)
    res = run_bass_kernel_spmd(nc, in_maps, core_ids=list(range(8)))
    outs = [np.asarray(res.results[c]['out']) for c in range(8)]
    full = np.stack([np.concatenate([outs[2 * b], outs[2 * b + 1]], axis=0)
                     for b in range(4)], axis=0)
    return full.astype(np.float32)
